# version 07
# baseline (speedup 1.0000x reference)
import numpy as np
import concourse.bass as bass
import concourse.mybir as mybir
from concourse.bass import IndirectOffsetOnAxis
from concourse.bass_utils import run_bass_kernel_spmd
import ml_dtypes
from contextlib import ExitStack

F32 = mybir.dt.float32
BF16 = mybir.dt.bfloat16
AF = mybir.ActivationFunctionType
ALU = mybir.AluOpType
AX = mybir.AxisListType
bfd = ml_dtypes.bfloat16

D = 1024
NKC = 8
SOWN = 4096
SALL = 8192
NTA = SALL // 128
NTB = SOWN // 128
NG = 8
GT = 4
MLP = 4096
NF1 = MLP // 128
DFF = 2048
NFC = DFF // 128
NE = 8
NCH = 4
CH = 1024
NT8 = CH // 128
KEPS = 1e-3
LNEPS = 1e-6
PLNEPS = 1e-5
CAP = 1536
NSC = CAP // 512

ENGINES = ("tensor", "vector", "scalar", "gpsimd", "sync")


class Sched:
    def __init__(self):
        self.ops = {e: [] for e in ENGINES}
        self.sems = {}
        self.owner = {}
        self.last_wait = {e: {} for e in ENGINES}
        self.own_inc = {e: {} for e in ENGINES}
        self.streams = {}

    def sem(self, *names):
        for n in names:
            self.sems.setdefault(n, 0)

    def val(self, sname):
        return self.sems[sname]

    def on(self, engine, fn, waits=(), inc=None, n=1, force=False, dma=False):
        pruned = []
        for (sn, v) in waits:
            if v <= 0:
                continue
            if v <= self.last_wait[engine].get(sn, 0):
                continue
            if (not force and self.owner.get(sn, "multi") == engine
                    and v <= self.own_inc[engine].get(sn, 0)):
                continue
            pruned.append((sn, v))
            self.last_wait[engine][sn] = max(self.last_wait[engine].get(sn, 0), v)
        post = None
        if inc is not None:
            self.sems[inc] = self.sems.get(inc, 0) + n
            post = self.sems[inc]
            if dma:
                self.owner[inc] = None
            elif inc not in self.owner:
                self.owner[inc] = engine
            elif self.owner[inc] != engine:
                self.owner[inc] = "multi"
            if not dma:
                self.own_inc[engine][inc] = self.sems[inc]
        self.ops[engine].append((pruned, fn, inc, n))
        return post

    def dma(self, engine, fn, stream, slot=0, waits=()):
        name = f"{stream}@{slot}"
        self.sem(name)
        self.streams.setdefault(stream, set()).add(name)
        v = self.on(engine, fn, waits=waits, inc=name, n=16, dma=True)
        return (name, v)

    def dma_totals(self, stream):
        return [(n, self.sems[n]) for n in sorted(self.streams.get(stream, ()))]

    def all_dma_totals(self):
        out = []
        for st in sorted(self.streams):
            out.extend(self.dma_totals(st))
        return out

    def used_sems(self):
        used = set()
        for e in ENGINES:
            for waits, fn, inc, n in self.ops[e]:
                for (sn, v) in waits:
                    used.add(sn)
                if inc is not None:
                    used.add(inc)
        return used

    def n_ops(self):
        return {e: len(self.ops[e]) for e in ENGINES}

    def emit(self, nc, semhandles):
        with nc.Block() as block:
            for engname in ENGINES:
                if not self.ops[engname]:
                    continue

                def make_body(en):
                    def body(eng):
                        for waits, fn, inc, n in self.ops[en]:
                            for (sn, v) in waits:
                                eng.wait_ge(semhandles[sn], v)
                            ins = fn(eng)
                            if inc is not None:
                                ins.then_inc(semhandles[inc], n)
                    return body

                getattr(block, engname)(make_body(engname))


SEMS = [
    "cst", "ld_kv", "fold", "rowmm", "rowev", "rowasm",
    "axa", "axt", "ssum", "sssq", "sch", "tr", "trc", "kmm", "vmm",
    "epik", "epiv", "kvmm", "bg_ld", "bg_fold", "bg_st", "wold",
    "tev", "tks", "tkc", "lqmm", "lqev", "lqasm", "b1mm", "b1ev", "b1tr", "b1tc",
    "bxt", "bxa", "qmm", "phr", "phv", "dnmm", "rcp", "bcmm", "bcev",
    "atn", "womm", "xrs", "s2ssq", "s2ch", "ln2", "trb", "trbc",
    "w1ld", "h1", "w2ld", "ymm", "zf", "s3ssq", "s3ch", "outv",
    "x1st", "x1tst", "yst", "x1bfc",
    "fxt", "fw", "fwd", "rtmm", "rtev", "rttr", "rtsm", "ce",
    "gu", "hd", "dd", "acc", "acchk", "x1ld", "zf2", "s4ssq", "s4ch", "w2u", "w1rd",
    "w2t",
]

KVC = 0
GK = 512
GV = 1536
SM = 2560
SP1 = 3072


def build(stop_phase=None, dumps=()):
    nc = bass.Bass("TRN2", target_bir_lowering=False)

    def din(name, shape, dt=F32):
        return nc.dram_tensor(name, shape, dt, kind="ExternalInput").ap()

    def dscr(name, shape, dt):
        kind = "ExternalOutput" if name in dumps else "Internal"
        return nc.dram_tensor(name, shape, dt, kind=kind).ap()

    x_all = din("x_all", [NTA, 128, D])
    xT_all = din("xT_all", [NKC, 128, SALL])
    wq32 = din("wq32", [NKC, 128, D]); wk32 = din("wk32", [NKC, 128, D])
    wv32 = din("wv32", [NKC, 128, D]); wo32 = din("wo32", [NKC, 128, D])
    w132 = din("w132", [NKC, 128, MLP])
    w232 = din("w232", [NF1, 128, D])
    wr32 = din("wr32", [NKC, 128, NE])
    wg32 = din("wg32", [NE, NFC, 128, NKC * 128])
    wu32 = din("wu32", [NE, NFC, 128, NKC * 128])
    wd32 = din("wd32", [NE, NFC, 128, D])
    ln1s = din("ln1s", [128, NKC]); ln1b = din("ln1b", [128, NKC])
    ln2s = din("ln2s", [128, NKC]); ln2b = din("ln2b", [128, NKC])
    bqr = din("bqr", [1, D]); bkr = din("bkr", [1, D]); bvr = din("bvr", [1, D])
    bor32 = din("bor32", [1, D]); b1r32 = din("b1r32", [1, MLP]); b2r32 = din("b2r32", [1, D])
    identF = din("identF", [128, 128])
    identB = din("identB", [128, 128], BF16)
    sel = din("sel", [NE, 16, 128])
    onesrow_f = din("onesrow_f", [1, 128])
    onescol_b = din("onescol_b", [128, 1], BF16)
    zk_row = din("zk_row", [2, 128], BF16)
    ut128 = din("ut128", [128, 128], BF16)
    cio8 = din("cio8", [1, 8])

    y = nc.dram_tensor("y", [NTB, 128, D], F32, kind="ExternalOutput").ap()

    w1scr = dscr("w1scr", [NF1, 128, NKC * 128], BF16)
    w2scr = dscr("w2scr", [NF1, 128, D], BF16)
    x1scr = dscr("x1scr", [NTB, 128, D], F32)
    x1Tscr = dscr("x1Tscr", [NKC, 128, SOWN], BF16)
    x1g = dscr("x1g", [NE * CAP, D], BF16)
    yg = dscr("yg", [NE * CAP, D], BF16)

    s = Sched()
    s.sem(*SEMS)

    es = ExitStack()
    ctx = {}
    with es:
        def sb(name, shape, dt=F32):
            h = es.enter_context(nc.sbuf_tensor(name, shape, dt))
            return h

        identF_sb = sb("identF_sb", [128, 128])
        identB_sb = sb("identB_sb", [128, 128], BF16)
        sel_sb = sb("sel_sb", [16, NE * 128])
        zk_sb = sb("zk_sb", [2, 128], BF16)
        onesrow_sb = sb("onesrow_sb", [1, 128])
        onescol_sb = sb("onescol_sb", [128, 1], BF16)
        eps6 = sb("eps6", [128, 1]); eps5 = sb("eps5", [128, 1])
        statB = sb("statB", [2, SOWN], BF16)
        kvsb = sb("kvsb", [128, 512], BF16)
        KS = sb("KS", [128, NKC * 16], BF16)
        Lq = sb("Lq", [2, D], BF16)
        bo_row = sb("bo_row", [1, D]); b2_row = sb("b2_row", [1, D])
        b1p = sb("b1p", [128, NF1])
        ce_sb = sb("ce_sb", [128, NTB * NE])
        wr_bf = sb("wr_bf", [128, NKC * NE], BF16)
        smc = sb("smc", [128, 24])
        ut_sb = sb("ut_sb", [128, 128], BF16)
        cio_sb = sb("cio_sb", [1, 8])

        ps = es.enter_context(nc.psum_tensor("ps", [128, 3584], F32))

        scope_ab = ExitStack()

        def sab(name, shape, dt=F32):
            return scope_ab.enter_context(nc.sbuf_tensor(name, shape, dt))

        wq_bf = sab("wq_bf", [128, NKC * D], BF16)
        wo_bf = sab("wo_bf", [128, NKC * D], BF16)

        scope_a = ExitStack()

        def sba(name, shape, dt=F32):
            return scope_a.enter_context(nc.sbuf_tensor(name, shape, dt))

        wk_bf = sba("wk_bf", [128, NKC * D], BF16)
        wv_bf = sba("wv_bf", [128, NKC * D], BF16)
        stg0 = sba("stg0", [128, MLP])
        stg = [stg0, stg0]
        w1f = [sba(f"w1f{i}", [128, MLP], BF16) for i in range(2)]
        ln1s_sb = sba("ln1s_sb", [128, NKC]); ln1b_sb = sba("ln1b_sb", [128, NKC])
        ln2s_sb = sba("ln2s_sb", [128, NKC]); ln2b_sb = sba("ln2b_sb", [128, NKC])
        rat1 = sba("rat1", [128, NKC], BF16)
        rat2 = sba("rat2", [128, NKC], BF16)
        ratt = sba("ratt", [128, NKC])
        stat_sb = sba("stat_sb", [128, NTA * 4])
        ksr0 = sba("ksr0", [1, 512]); ksr1 = sba("ksr1", [1, 512])
        poolA = sba("poolA", [128, MLP])
        rowf = poolA[0:1, :]
        b1_sb = poolA[32:33, :]
        bk_sb = poolA[64:65, 0:D]
        bv_sb = poolA[64:65, D:2 * D]
        bq_sb = poolA[64:65, 2 * D:3 * D]
        poolB = sba("poolB", [128, MLP], BF16)
        Rk = poolB[0:2, 0:D]
        Rv = poolB[0:2, D:2 * D]
        rowb = poolB[32:33, :]
        rowc = poolB[64:65, :]
        junkA = sba("junkA", [128, D])
        xa = [sba(f"xa{i}", [128, D]) for i in range(3)]
        xtb = [sba(f"xtb{i}", [128, NKC * 128], BF16) for i in range(3)]
        phik = [sba(f"phik{i}", [128, D], BF16) for i in range(2)]
        vsb = [sba(f"vsb{i}", [128, D], BF16) for i in range(2)]
        tk = [sba(f"tk{i}", [128, D]) for i in range(2)]
        trin = [sba(f"trin{i}", [128, 4]) for i in range(2)]
        srowA = [sba(f"srowA{i}", [2, 128], BF16) for i in range(2)]

        for t_, src_ in [(identF_sb, identF), (identB_sb, identB), (zk_sb, zk_row),
                         (onesrow_sb, onesrow_f), (onescol_sb, onescol_b),
                         (ln1s_sb, ln1s), (ln1b_sb, ln1b),
                         (ln2s_sb, ln2s), (ln2b_sb, ln2b),
                         (bo_row, bor32), (b2_row, b2r32),
                         (bk_sb, bkr), (bv_sb, bvr), (bq_sb, bqr), (b1_sb, b1r32),
                         (ut_sb, ut128), (cio_sb, cio8)]:
            s.dma("sync", lambda e, t=t_, sr=src_: e.dma_start(out=t[:], in_=sr),
                  "cst", 0)
        s.dma("sync", lambda e: e.dma_start(
            out=sel_sb[:].rearrange("p (c f) -> p c f", c=NE),
            in_=sel.rearrange("c p f -> p c f")), "cst", 0)
        s.dma("gpsimd", lambda e: e.dma_start(
            out=wr_bf[:].rearrange("p (c f) -> p c f", c=NKC),
            in_=wr32.rearrange("c p f -> p c f")), "gwr", 0)
        CST = ("cst@0", s.val("cst@0"))

        def ratio_chain(dst_bf, lns_, lnb_, c0, c1):
            s.on("vector", lambda v: v.tensor_scalar(smc[:, c0:c0 + NKC], lns_[:],
                                                     0.0, None, ALU.is_equal),
                 waits=[CST], inc="fold")
            fv = s.val("fold")
            s.on("vector", lambda v: v.tensor_add(smc[:, c1:c1 + NKC], lns_[:],
                                                  smc[:, c0:c0 + NKC]),
                 waits=[("fold", fv)], force=True, inc="fold")
            fv = s.val("fold")
            s.on("vector", lambda v: v.reciprocal(ratt[:], smc[:, c1:c1 + NKC]),
                 waits=[("fold", fv)], force=True, inc="fold")
            fv = s.val("fold")
            s.on("vector", lambda v: v.tensor_mul(dst_bf[:], lnb_[:], ratt[:]),
                 waits=[("fold", fv)], force=True, inc="fold")

        ratio_chain(rat1, ln1s_sb, ln1b_sb, 0, 8)
        ratio_chain(rat2, ln2s_sb, ln2b_sb, 0, 8)
        s.on("vector", lambda v: v.memset(eps6[:], LNEPS))
        s.on("vector", lambda v: v.memset(eps5[:], PLNEPS))
        s.on("vector", lambda v: v.memset(KS[:], 0.0), inc="fold")
        RATF = s.val("fold")

        for wsrc, wdst in [(wk32, wk_bf), (wv32, wv_bf)]:
            for kc in range(NKC):
                pbuf = stg[kc % 2]
                ldv = s.dma("sync", lambda e, wsrc=wsrc, kc=kc, pbuf=pbuf:
                            e.dma_start(out=pbuf[:, 0:D], in_=wsrc[kc]),
                            "ldkv", kc % 2, waits=[("fold", s.val("fold"))])
                s.on("vector", lambda v, wdst=wdst, kc=kc, pbuf=pbuf:
                     v.tensor_scalar(wdst[:, kc * D:(kc + 1) * D], pbuf[:, 0:D],
                                     ln1s_sb[:, kc:kc + 1], None, ALU.mult),
                     waits=[ldv, CST], inc="fold")
        FOLD_KV = s.val("fold")

        def build_R(wdst, brow_sb, Rdst, ratio_sb, neg0=False, extra_w=()):
            asm0 = s.dma_totals("rasm")
            for kc in range(NKC):
                for h in range(2):
                    s.on("tensor", lambda t, kc=kc, h=h, wdst=wdst: t.matmul(
                        ps[0:1, SM + h * 512:SM + (h + 1) * 512],
                        onescol_sb[:],
                        wdst[:, kc * D + h * 512:kc * D + (h + 1) * 512],
                        start=(kc == 0), stop=(kc == NKC - 1), skip_group_check=True),
                        waits=([("fold", FOLD_KV)] + asm0 + list(extra_w))
                        if (kc == 0 and h == 0) else (), force=True)
            for kc in range(NKC):
                for h in range(2):
                    s.on("tensor", lambda t, kc=kc, h=h, wdst=wdst, rs=ratio_sb: t.matmul(
                        ps[32:33, SM + h * 512:SM + (h + 1) * 512],
                        rs[:, kc:kc + 1],
                        wdst[:, kc * D + h * 512:kc * D + (h + 1) * 512],
                        start=(kc == 0), stop=(kc == NKC - 1), skip_group_check=True,
                        tile_position=(0, 32)),
                        inc="rowmm" if (kc == NKC - 1 and h == 1) else None)
            mmv = s.val("rowmm")
            if neg0:
                s.on("vector", lambda v: v.tensor_scalar(
                    rowb[:, 0:D], ps[0:1, SM:SM + D], -1.0, None, ALU.mult),
                    waits=[("rowmm", mmv)] + asm0, force=True)
            else:
                s.on("vector", lambda v: v.tensor_copy(rowb[:, 0:D], ps[0:1, SM:SM + D]),
                     waits=[("rowmm", mmv)] + asm0, force=True)
            s.on("vector", lambda v, brow_sb=brow_sb: v.tensor_add(
                rowc[:, 0:D], ps[32:33, SM:SM + D], brow_sb[:]), inc="rowev")
            ev = s.val("rowev")
            s.dma("gpsimd", lambda e, Rdst=Rdst: e.dma_start(out=Rdst[0:1, :],
                                                             in_=rowb[:, 0:D]),
                  "rasm", 0, waits=[("rowev", ev)])
            s.dma("gpsimd", lambda e, Rdst=Rdst: e.dma_start(out=Rdst[1:2, :],
                                                             in_=rowc[:, 0:D]),
                  "rasm", 1)
            return s.dma_totals("rasm")

        build_R(wk_bf, bk_sb, Rk, rat1)
        RASM = build_R(wv_bf, bv_sb, Rv, rat1)

        s.dma("gpsimd", lambda e: e.dma_start(
            out=wo_bf[:].rearrange("p (c f) -> p c f", c=NKC),
            in_=wo32.rearrange("c p f -> p c f")), "gwo", 0)

        bg_jobs = []
        last_bgst = {}

        def bg_fold_job(idx, wsrc, kc, width, kind):
            def emitjob():
                pbuf = stg[idx % 2]
                ldv = s.dma("gpsimd",
                            lambda e: e.dma_start(out=pbuf[:, 0:width], in_=wsrc[kc]),
                            "bgld", idx % 2,
                            waits=[("bg_fold", s.val("bg_fold")), ("fold", FOLD_KV)])
                if kind == "wq":
                    s.on("vector", lambda v: v.tensor_scalar(
                        wq_bf[:, kc * D:(kc + 1) * D], pbuf[:, 0:D],
                        ln1s_sb[:, kc:kc + 1], None, ALU.mult),
                        waits=[ldv], inc="bg_fold")
                else:
                    wbuf = w1f[idx % 2]
                    s.on("vector", lambda v: v.tensor_scalar(
                        wbuf[:, 0:MLP], pbuf[:, 0:MLP],
                        ln2s_sb[:, kc:kc + 1], None, ALU.mult),
                        waits=[ldv, last_bgst.get(idx % 2, ("fold", 0))],
                        inc="bg_fold")
                    fv = s.val("bg_fold")
                    last_bgst[idx % 2] = s.dma("sync", lambda e: e.dma_start(
                        out=w1scr[:, :, kc * 128:(kc + 1) * 128].rearrange(
                            "f p c -> p f c"),
                        in_=wbuf[:, 0:MLP].rearrange("p (f c) -> p f c", c=128)),
                        "bgst", idx % 2, waits=[("bg_fold", fv)])
            return emitjob

        jidx = 0
        for kc in range(NKC):
            bg_jobs.append(bg_fold_job(jidx, wq32, kc, D, "wq")); jidx += 1
        for kc in range(NKC):
            bg_jobs.append(bg_fold_job(jidx, w132, kc, MLP, "w1")); jidx += 1

        s.sem("bgst@0", "bgst@1")
        W1ST_FINAL = [("bgst@0", 64), ("bgst@1", 64)]

        def b1_job(f1):
            def emit():
                ldv = s.dma("sync", lambda e, f1=f1: e.dma_start(
                    out=w1f[f1 % 2][:, 0:NKC * 128], in_=w1scr[f1]),
                    "w1rd", f1 % 2,
                    waits=[("b1mm", max(0, f1 - 1))] + W1ST_FINAL)
                for kc in range(NKC):
                    s.on("tensor", lambda t, f1=f1, kc=kc: t.matmul(
                        ps[0:1, SP1 + (f1 % 4) * 128:SP1 + (f1 % 4) * 128 + 128],
                        rat2[:, kc:kc + 1],
                        w1f[f1 % 2][:, kc * 128:(kc + 1) * 128],
                        start=(kc == 0), stop=(kc == NKC - 1),
                        skip_group_check=True),
                        waits=([ldv, ("b1ev", f1)]) if kc == 0 else (),
                        force=True,
                        inc="b1mm" if kc == NKC - 1 else None)
                mmv = s.val("b1mm")
                s.on("vector", lambda v, f1=f1: v.tensor_add(
                    rowf[:, f1 * 128:(f1 + 1) * 128],
                    ps[0:1, SP1 + (f1 % 4) * 128:SP1 + (f1 % 4) * 128 + 128],
                    b1_sb[:, f1 * 128:(f1 + 1) * 128]),
                    waits=[("b1mm", mmv)], force=True, inc="b1ev")
            return emit
        b1_jobs = [b1_job(f1) for f1 in range(NF1)]

        free_gk = ("fold", 0)
        free_gv = ("fold", 0)
        free_sm_dve = ("fold", 0)
        last_phik = {}; last_vsb = {}; last_xa = {}; last_xtb = {}
        last_trin = {}; last_srowA_use = {}; last_tk = {}

        for t in range(NTA):
            own = t < NTB
            axa_v = s.dma("sync", lambda e, t=t: e.dma_start(
                out=xa[t % 3][:], in_=x_all[t]), "axa", t % 3,
                waits=list(last_xa.get(t % 3, ())))
            axt_v = s.dma("gpsimd", lambda e, t=t: e.dma_start(
                out=xtb[t % 3][:].rearrange("p (c f) -> p c f", c=NKC),
                in_=xT_all[:, :, t * 128:(t + 1) * 128].rearrange("c p f -> p c f")),
                "axt", t % 3, waits=[last_xtb.get(t % 3, ("fold", 0))])
            if t % 3 == 2 and bg_jobs:
                bg_jobs.pop(0)()
            if t % 2 == 0 and t // 2 < NF1:
                s.dma("gpsimd", lambda e, f1c=t // 2: e.dma_start(
                    out=w2scr[f1c], in_=w232[f1c]), "w2c", 0)
            if t >= 48:
                for _ in range(2):
                    if b1_jobs:
                        b1_jobs.pop(0)()

            s.on("vector", lambda v, t=t: v.tensor_scalar(
                junkA[:], xa[t % 3][:], 1.0, None, ALU.mult, ALU.add,
                accum_out=stat_sb[:, 4 * t + 3:4 * t + 4]),
                waits=[axa_v], inc="ssum")
            ssum_v = s.val("ssum")
            s.on("scalar", lambda sc, t=t: sc.activation(
                tk[t % 2][:], xa[t % 3][:], AF.Square,
                accum_out=smc[:, 12 + t % 2:13 + t % 2]),
                waits=[axa_v, last_tk.get(t % 2, ("fold", 0))],
                inc="sssq")
            sssq_v = s.val("sssq")
            s.on("vector", lambda v, t=t: v.tensor_scalar(
                stat_sb[:, 4 * t:4 * t + 1], stat_sb[:, 4 * t + 3:4 * t + 4],
                1.0 / D, None, ALU.mult),
                waits=[("ssum", ssum_v)], force=True, inc="sch")
            s.on("vector", lambda v, t=t: v.tensor_mul(
                smc[:, 14 + t % 2:15 + t % 2], stat_sb[:, 4 * t:4 * t + 1],
                stat_sb[:, 4 * t:4 * t + 1]),
                waits=[("sch", s.val("sch"))], force=True, inc="sch")
            s.on("vector", lambda v, t=t: v.scalar_tensor_tensor(
                stat_sb[:, 4 * t + 3:4 * t + 4], smc[:, 12 + t % 2:13 + t % 2],
                1.0 / D, smc[:, 14 + t % 2:15 + t % 2], ALU.mult, ALU.subtract),
                waits=[("sssq", sssq_v), ("sch", s.val("sch"))], force=True,
                inc="sch")
            schv = s.val("sch")
            s.on("scalar", lambda sc, t=t: sc.activation(
                smc[:, 16 + t % 2:17 + t % 2], stat_sb[:, 4 * t + 3:4 * t + 4],
                AF.Ln, bias=eps6[:]),
                waits=[("sch", schv)], inc="sch")
            s.on("scalar", lambda sc, t=t: sc.activation(
                stat_sb[:, 4 * t + 1:4 * t + 2], smc[:, 16 + t % 2:17 + t % 2],
                AF.Exp, scale=-0.5),
                waits=[("sch", s.val("sch"))], force=True, inc="sch")
            s.on("scalar", lambda sc, t=t: sc.activation(
                stat_sb[:, 4 * t + 2:4 * t + 3], smc[:, 16 + t % 2:17 + t % 2],
                AF.Exp, scale=0.5), inc="sch")
            schv2 = s.val("sch")
            s.on("vector", lambda v, t=t: v.tensor_scalar(
                trin[t % 2][:, 0:1], stat_sb[:, 4 * t:4 * t + 1], -1.0, None,
                ALU.mult),
                waits=[("sch", schv2), last_trin.get(t % 2, ("fold", 0))],
                force=True)
            s.on("vector", lambda v, t=t: v.tensor_copy(
                trin[t % 2][:, 1:2], stat_sb[:, 4 * t + 2:4 * t + 3]))
            if own:
                s.on("vector", lambda v, t=t: v.tensor_copy(
                    trin[t % 2][:, 2:3], stat_sb[:, 4 * t:4 * t + 1]))
                s.on("vector", lambda v, t=t: v.tensor_copy(
                    trin[t % 2][:, 3:4], stat_sb[:, 4 * t + 2:4 * t + 3]),
                    inc="sch")
            else:
                s.on("vector", lambda v: v.memset(smc[:, 20:21], 0.0),
                     inc="sch")
            trin_ready = s.val("sch")

            s.on("tensor", lambda tt, t=t: tt.transpose(
                ps[0:2, SM:SM + 128], trin[t % 2][:, 0:2],
                identF_sb[:]),
                waits=[("sch", trin_ready), free_sm_dve] + RASM,
                force=True, inc="tr")
            if own:
                s.on("tensor", lambda tt, t=t: tt.transpose(
                    ps[0:2, SM + 128:SM + 256], trin[t % 2][:, 2:4],
                    identF_sb[:]), inc="tr")
            tr_v = s.val("tr")

            kw = [axt_v, ("fold", FOLD_KV), free_gk]
            for kc in range(NKC):
                for h in range(2):
                    s.on("tensor", lambda tt, t=t, kc=kc, h=h: tt.matmul(
                        ps[:, GK + h * 512:GK + (h + 1) * 512],
                        xtb[t % 3][:, kc * 128:(kc + 1) * 128],
                        wk_bf[:, kc * D + h * 512:kc * D + (h + 1) * 512],
                        start=(kc == 0), stop=False, skip_group_check=True),
                        waits=kw if (kc == 0 and h == 0) else (), force=True)
            vw = [free_gv]
            for kc in range(NKC):
                for h in range(2):
                    s.on("tensor", lambda tt, t=t, kc=kc, h=h: tt.matmul(
                        ps[:, GV + h * 512:GV + (h + 1) * 512],
                        xtb[t % 3][:, kc * 128:(kc + 1) * 128],
                        wv_bf[:, kc * D + h * 512:kc * D + (h + 1) * 512],
                        start=(kc == 0), stop=False, skip_group_check=True),
                        waits=vw if (kc == 0 and h == 0) else (), force=True)

            s.on("vector", lambda v, t=t: v.tensor_copy(
                srowA[t % 2][:], ps[0:2, SM:SM + 128]),
                waits=[("tr", tr_v), last_srowA_use.get(t % 2, ("fold", 0))],
                force=True)
            if own:
                s.on("vector", lambda v, t=t: v.tensor_copy(
                    statB[:, t * 128:(t + 1) * 128],
                    ps[0:2, SM + 128:SM + 256]), inc="trc")
            else:
                s.on("vector", lambda v: v.memset(smc[:, 20:21], 0.0),
                     inc="trc")
            trc_v = s.val("trc")
            free_sm_dve = ("trc", trc_v)

            for dst, R, incn in ((GK, Rk, "kmm"), (GV, Rv, "vmm")):
                for h in range(2):
                    s.on("tensor", lambda tt, t=t, dst=dst, R=R, h=h: tt.matmul(
                        ps[:, dst + h * 512:dst + (h + 1) * 512],
                        srowA[t % 2][:], R[:, h * 512:(h + 1) * 512],
                        start=False, stop=True, skip_group_check=True),
                        waits=[("trc", trc_v)] if (h == 0 and dst == GK) else (),
                        force=True, inc=incn if h == 1 else None)
            kmm_v = s.val("kmm"); vmm_v = s.val("vmm")
            last_srowA_use[t % 2] = ("vmm", vmm_v)
            last_trin[t % 2] = ("tr", tr_v)

            s.on("scalar", lambda sc, t=t: sc.activation(
                tk[t % 2][:], ps[:, GK:GK + D], AF.Relu,
                scale=stat_sb[:, 4 * t + 1:4 * t + 2]),
                waits=[("kmm", kmm_v)], inc="epik")
            s.on("vector", lambda v, t=t: v.tensor_scalar(
                phik[t % 2][:], tk[t % 2][:], KEPS, None, ALU.add),
                waits=[("epik", s.val("epik")), last_phik.get(t % 2, ("fold", 0))],
                force=True, inc="epik")
            epik_v = s.val("epik")
            free_gk = ("epik", epik_v)
            last_tk[t % 2] = ("epik", epik_v)
            s.on("vector", lambda v, t=t: v.tensor_scalar(
                vsb[t % 2][:], ps[:, GV:GV + D],
                stat_sb[:, 4 * t + 1:4 * t + 2], None, ALU.mult),
                waits=[("vmm", vmm_v), last_vsb.get(t % 2, ("fold", 0))],
                force=True, inc="epiv")
            epiv_v = s.val("epiv")
            free_gv = ("epiv", epiv_v)
            last_xa[t % 3] = [("sssq", sssq_v), ("ssum", ssum_v)]
            last_xtb[t % 3] = ("vmm", vmm_v)

            for p in range(8):
                s.on("tensor", lambda tt, t=t, p=p: tt.matmul(
                    ps[0:64, KVC + p * 64:KVC + (p + 1) * 64],
                    phik[t % 2][:, p * 128:p * 128 + 64],
                    vsb[t % 2][:, (2 * p) * 64:(2 * p + 1) * 64],
                    start=(t == 0 and p == 0), stop=(t == NTA - 1),
                    skip_group_check=True),
                    waits=[("epik", epik_v), ("epiv", epiv_v)] if p == 0 else (),
                    force=True)
                s.on("tensor", lambda tt, t=t, p=p: tt.matmul(
                    ps[64:128, KVC + p * 64:KVC + (p + 1) * 64],
                    phik[t % 2][:, p * 128 + 64:(p + 1) * 128],
                    vsb[t % 2][:, (2 * p + 1) * 64:(2 * p + 2) * 64],
                    start=(t == 0 and p == 0), stop=(t == NTA - 1),
                    skip_group_check=True, tile_position=(0, 64)))
            s.on("tensor", lambda tt, t=t: tt.matmul(
                ps[32:33, SM:SM + 512], onescol_sb[:], phik[t % 2][:, 0:512],
                start=(t == 0), stop=(t == NTA - 1), skip_group_check=True,
                tile_position=(0, 32)))
            s.on("tensor", lambda tt, t=t: tt.matmul(
                ps[64:65, SM:SM + 512], onescol_sb[:], phik[t % 2][:, 512:1024],
                start=(t == 0), stop=(t == NTA - 1), skip_group_check=True,
                tile_position=(0, 64)), inc="kvmm")
            kv_done = s.val("kvmm")
            last_phik[t % 2] = ("kvmm", kv_done)
            last_vsb[t % 2] = ("kvmm", kv_done)

        while bg_jobs:
            bg_jobs.pop(0)()
        while b1_jobs:
            b1_jobs.pop(0)()

        KV_ALL = s.val("kvmm")
        s.on("vector", lambda v: v.tensor_copy(kvsb[:], ps[:, KVC:KVC + 512]),
             waits=[("kvmm", KV_ALL)], force=True)
        s.on("vector", lambda v: v.tensor_copy(ksr0[:], ps[32:33, SM:SM + 512]))
        s.on("vector", lambda v: v.tensor_copy(ksr1[:], ps[64:65, SM:SM + 512]),
             inc="tev")
        TEV = s.val("tev")
        for c in range(NKC):
            src = ksr0 if c < 4 else ksr1
            off = (c % 4) * 128
            pcol = GK + c * 64
            s.on("tensor", lambda tt, c=c, src=src, off=off, pcol=pcol: tt.transpose(
                ps[0:128, pcol:pcol + 1], src[:, off:off + 128],
                identF_sb[0:1, 0:1]),
                waits=[("tev", TEV)] if c == 0 else (), force=True, inc="tks")
        TKS = s.val("tks")
        for c in range(NKC):
            pcol = GK + c * 64
            s.on("vector", lambda v, c=c, pcol=pcol: v.tensor_copy(
                KS[0:64, c * 16 + 2 * c:c * 16 + 2 * c + 1], ps[0:64, pcol:pcol + 1]),
                waits=[("tks", TKS)], force=True)
            s.on("vector", lambda v, c=c, pcol=pcol: v.tensor_copy(
                KS[64:128, c * 16 + 2 * c + 1:c * 16 + 2 * c + 2],
                ps[64:128, pcol:pcol + 1]),
                inc="tkc")
        TKC = s.val("tkc")

        BGF = s.val("bg_fold")
        RASM2 = build_R(wq_bf, bq_sb, Lq, rat1, neg0=True,
                        extra_w=[("bg_fold", BGF), ("tkc", TKC)])

        W1ST = s.dma_totals("bgst")
        B1EV = s.val("b1ev")
        for f1 in range(NF1):
            s.on("tensor", lambda tt, f1=f1: tt.transpose(
                ps[0:128, GK + f1 * 32:GK + f1 * 32 + 1],
                rowf[:, f1 * 128:(f1 + 1) * 128], identF_sb[0:1, 0:1]),
                waits=([("b1ev", B1EV), ("tkc", TKC)] if f1 == 0 else ()),
                force=True, inc="b1tr" if f1 == NF1 - 1 else None)
        B1TR = s.val("b1tr")
        s.on("vector", lambda v: v.tensor_copy(
            b1p[:, 0:NF1], ps[0:128, GK:GK + NF1 * 32:32]),
            waits=[("b1tr", B1TR)], force=True, inc="b1tc")
        B1TC = s.val("b1tc")
        W2CASTS = s.dma_totals("w2c")

        BARB = ([("kvmm", KV_ALL), ("sssq", s.val("sssq")),
                 ("epik", s.val("epik")), ("epiv", s.val("epiv")),
                 ("b1mm", s.val("b1mm")), ("b1ev", B1EV), ("b1tc", B1TC),
                 ("tkc", TKC), ("tr", s.val("tr")), ("vmm", s.val("vmm"))]
                + RASM2 + W1ST + s.dma_totals("gwo") + s.dma_totals("gwr"))

        ctx.update(locals())
        if stop_phase == "A":
            dbgs = [("dbg_kv", kvsb, [128, 512], BF16),
                    ("dbg_KS", KS, [128, NKC * 16], BF16),
                    ("dbg_statB", statB, [2, SOWN], BF16),
                    ("dbg_stat", stat_sb, [128, NTA * 4], F32),
                    ("dbg_Rk", Rk, [2, D], BF16),
                    ("dbg_Rv", Rv, [2, D], BF16),
                    ("dbg_Lq", Lq, [2, D], BF16),
                    ("dbg_b1p", b1p, [128, NF1], F32),
                    ("dbg_wq", wq_bf, [128, NKC * D], BF16),
                    ("dbg_phik", phik[1], [128, D], BF16),
                    ("dbg_vsb", vsb[1], [128, D], BF16)]
            for name, sbt, shape, dt in dbgs:
                dout = nc.dram_tensor(name, shape, dt, kind="ExternalOutput").ap()
                s.dma("sync", lambda e, dout=dout, sbt=sbt: e.dma_start(
                    out=dout, in_=sbt[:]), "dbgst", 0, waits=BARB)
            s.on("sync", lambda e: e.nop(),
                 waits=BARB + s.all_dma_totals(), force=True)
            semh = {name: es.enter_context(nc.semaphore(name.replace("@", "_")))
                    for name in s.used_sems()}
            s.emit(nc, semh)
            scope_a.close(); scope_ab.close()
            return nc, s

        scope_a.close()
        ctx["scope_ab"] = scope_ab
        build_phase_b(nc, s, ctx)
        if stop_phase == "B":
            s.on("sync", lambda e: e.nop(),
                 waits=ctx["BARF"] + s.all_dma_totals(), force=True)
            semh = {name: es.enter_context(nc.semaphore(name.replace("@", "_")))
                    for name in s.used_sems()}
            s.emit(nc, semh)
            ctx["scope_b"].close()
            scope_ab.close()
            return nc, s
        build_ffn(nc, s, ctx)
        semh = {name: es.enter_context(nc.semaphore(name.replace("@", "_")))
                    for name in s.used_sems()}
        s.emit(nc, semh)
        ctx["scope_f"].close()
        scope_ab2 = None
    return nc, s


def build_phase_b(nc, s, ctx):
    ps = ctx["ps"]; es = ctx["es"]
    identF_sb = ctx["identF_sb"]; identB_sb = ctx["identB_sb"]
    sel_sb = ctx["sel_sb"]; zk_sb = ctx["zk_sb"]
    onesrow_sb = ctx["onesrow_sb"]; statB = ctx["statB"]; kvsb = ctx["kvsb"]
    KS = ctx["KS"]; Lq = ctx["Lq"]; bo_row = ctx["bo_row"]; b2_row = ctx["b2_row"]
    b1p = ctx["b1p"]; eps6 = ctx["eps6"]; eps5 = ctx["eps5"]
    wq_bf = ctx["wq_bf"]; wo_bf = ctx["wo_bf"]
    xT_all = ctx["xT_all"]; x_all = ctx["x_all"]
    w1scr = ctx["w1scr"]; w2scr = ctx["w2scr"]; W2CASTS = ctx["W2CASTS"]
    x1scr = ctx["x1scr"]; x1Tscr = ctx["x1Tscr"]
    BARB = ctx["BARB"]; smc = ctx["smc"]

    scope_b = ExitStack()
    ctx["scope_b"] = scope_b

    def sbb(name, shape, dt=F32):
        return scope_b.enter_context(nc.sbuf_tensor(name, shape, dt))

    xtbg = [sbb("xtbg0", [128, NKC * 512], BF16)] * 2
    xaB = [sbb(f"xaB{i}", [128, D]) for i in range(4)]
    tq = [sbb(f"tq{i}", [128, 512], BF16) for i in range(2)]
    phtil = sbb("phtil", [128, NKC * 512], BF16)
    attnT = sbb("attnT", [128, NKC * 512], BF16)
    epsbc = sbb("epsbc", [128, 512])
    recd = sbb("recd", [16, 512])
    bcsb = [sbb(f"bcsb{i}", [128, 512]) for i in range(2)]
    xresB = [sbb(f"xresB{i}", [128, D]) for i in range(4)]
    ln2xb = [sbb(f"ln2xb{i}", [128, D], BF16) for i in range(2)]
    ln2xT = sbb("ln2xT", [128, NKC * 512], BF16)
    h1T = sbb("h1T", [128, NF1 * 512], BF16)
    w1r = [sbb(f"w1r{i}", [128, NKC * 128], BF16) for i in range(3)]
    w2r = [sbb(f"w2r{i}", [128, 512], BF16) for i in range(3)]
    x1f = [sbb(f"x1f{i}", [128, D]) for i in range(2)]
    x1bfB = [sbb("x1bfB0", [128, D], BF16)] * 2
    x1tb = [sbb(f"x1tb{i}", [128, NKC * 128], BF16) for i in range(2)]
    zbuf = [sbb(f"zbuf{i}", [128, D]) for i in range(2)]
    junkB = sbb("junkB", [128, D])
    statC = sbb("statC", [128, 16])
    psb = scope_b.enter_context(nc.psum_tensor("psb", [128, 1024], BF16))
    ctx["psb"] = psb

    singles = [0, SP1]
    pairs = [512, 1536]
    free_single = {0: BARB[0], SP1: ("b1mm", s.val("b1mm"))}
    free_pair = {512: ("b1tc", s.val("b1tc")), 1536: ("b1tc", s.val("b1tc"))}
    free_den = ("tev", s.val("tev"))
    free_psb = ("fold", 0)

    last_xtbg = {}; last_xaB = {}; last_tq = {}; last_w1r = {}; last_w2r = {}
    last_ln2xb = {}; last_x1f = {}; last_x1bf = {}; last_zbuf = {}
    last_x1tb = {}; last_bcsb = {}
    last_attnT = ("fold", 0)
    prev_atn = ("fold", 0)
    prev_h1 = 0
    barrier = list(BARB)

    for g in range(NG):
        tw0 = 512 * g
        sing_i = [0]; pair_i = [0]

        def nsingle(extra=()):
            col = singles[sing_i[0] % 2]
            sing_i[0] += 1
            return col, [free_single[col]] + list(extra)

        def npair(extra=()):
            col = pairs[pair_i[0] % 2]
            pair_i[0] += 1
            return col, [free_pair[col]] + list(extra)

        bxt_v = s.dma("gpsimd", lambda e, g=g, tw0=tw0: e.dma_start(
            out=xtbg[g % 2][:].rearrange("p (c f) -> p c f", c=NKC),
            in_=xT_all[:, :, tw0:tw0 + 512].rearrange("c p f -> p c f")),
            "bxt", g % 2, waits=barrier + [last_xtbg.get(g % 2, ("fold", 0))])
        bxa_w = {}
        for w in range(GT):
            t = 4 * g + w
            bxa_w[w] = s.dma("sync", lambda e, t=t, w=w: e.dma_start(
                out=xaB[w][:], in_=x_all[t]),
                "bxa", w, waits=barrier + [last_xaB.get(w, ("fold", 0))])

        col_e, we = nsingle()
        s.on("tensor", lambda tt, tw0=tw0, col_e=col_e: tt.matmul(
            ps[:, col_e:col_e + 512], zk_sb[:], statB[:, tw0:tw0 + 512],
            start=True, stop=True, skip_group_check=True),
            waits=barrier + we, inc="qmm")
        s.on("vector", lambda v, col_e=col_e: v.tensor_copy(
            epsbc[:], ps[:, col_e:col_e + 512]),
            waits=barrier + [("qmm", s.val("qmm"))], force=True, inc="phv")
        epsbc_v = s.val("phv")
        free_single[col_e] = ("phv", epsbc_v)

        for fc in range(NKC):
            col, qw = nsingle([bxt_v])
            for kc in range(NKC):
                s.on("tensor", lambda tt, g=g, fc=fc, kc=kc, col=col: tt.matmul(
                    ps[:, col:col + 512],
                    wq_bf[:, kc * D + fc * 128:kc * D + (fc + 1) * 128],
                    xtbg[g % 2][:, kc * 512:(kc + 1) * 512],
                    start=(kc == 0), stop=False, skip_group_check=True),
                    waits=qw if kc == 0 else (), force=True)
            s.on("tensor", lambda tt, tw0=tw0, fc=fc, col=col: tt.matmul(
                ps[:, col:col + 512], Lq[:, fc * 128:(fc + 1) * 128],
                statB[:, tw0:tw0 + 512],
                start=False, stop=True, skip_group_check=True), inc="qmm")
            qv = s.val("qmm")
            s.on("scalar", lambda sc, fc=fc, col=col: sc.activation(
                tq[fc % 2][:], ps[:, col:col + 512], AF.Relu),
                waits=[("qmm", qv), last_tq.get(fc % 2, ("fold", 0))], force=True,
                inc="phr")
            prv = s.val("phr")
            s.on("vector", lambda v, fc=fc: v.tensor_add(
                phtil[:, fc * 512:(fc + 1) * 512], tq[fc % 2][:], epsbc[:]),
                waits=[("phr", prv), ("phv", epsbc_v), prev_atn], force=True,
                inc="phv")
            pv = s.val("phv")
            free_single[col] = ("phv", pv)
            last_tq[fc % 2] = ("phv", pv)
        PHV = s.val("phv")
        last_xtbg[g % 2] = ("qmm", s.val("qmm"))

        for c in range(NKC):
            s.on("tensor", lambda tt, c=c: tt.matmul(
                ps[0:16, SM:SM + 512], KS[:, c * 16:(c + 1) * 16],
                phtil[:, c * 512:(c + 1) * 512],
                start=(c == 0), stop=(c == NKC - 1), skip_group_check=True),
                waits=[("phv", PHV), free_den] if c == 0 else (), force=True,
                inc="dnmm" if c == NKC - 1 else None)
        dnv = s.val("dnmm")
        s.on("vector", lambda v: v.reciprocal(recd[:], ps[0:16, SM:SM + 512]),
             waits=[("dnmm", dnv)], force=True, inc="rcp")
        rcpv = s.val("rcp")
        free_den = ("rcp", rcpv)

        for c in range(NKC):
            col, nw = npair([("phv", PHV)])
            s.on("tensor", lambda tt, c=c, col=col: tt.matmul(
                ps[0:64, col:col + 512], kvsb[0:64, c * 64:(c + 1) * 64],
                phtil[0:64, c * 512:(c + 1) * 512],
                start=True, stop=True, skip_group_check=True),
                waits=nw, force=True)
            s.on("tensor", lambda tt, c=c, col=col: tt.matmul(
                ps[64:128, col:col + 512], kvsb[64:128, c * 64:(c + 1) * 64],
                phtil[64:128, c * 512:(c + 1) * 512],
                start=True, stop=True, skip_group_check=True,
                tile_position=(64, 64)))
            s.on("tensor", lambda tt, c=c, col=col: tt.matmul(
                ps[:, col + 512:col + 1024], sel_sb[:, c * 128:(c + 1) * 128],
                recd[:], start=True, stop=True, skip_group_check=True),
                waits=[("rcp", rcpv)], force=True, inc="bcmm")
            bcv = s.val("bcmm")
            s.on("vector", lambda v, c=c, col=col: v.tensor_copy(
                bcsb[c % 2][:], ps[:, col + 512:col + 1024]),
                waits=[("bcmm", bcv), last_bcsb.get(c % 2, ("fold", 0))],
                force=True, inc="bcev")
            s.on("vector", lambda v, c=c, col=col: v.tensor_mul(
                attnT[:, c * 512:(c + 1) * 512], ps[:, col:col + 512],
                bcsb[c % 2][:]),
                waits=[("bcev", s.val("bcev"))] + ([last_attnT] if c == 0 else []),
                force=True, inc="atn")
            av = s.val("atn")
            free_pair[col] = ("atn", av)
            last_bcsb[c % 2] = ("atn", av)
        ATN = s.val("atn")
        prev_atn = ("atn", ATN)

        ln2_done = {}

        def emit_trb(w):
            nonlocal free_psb
            lnv = ln2_done[w]
            for j in range(NKC):
                s.on("tensor", lambda tt, w=w, j=j: tt.transpose(
                    psb[:, j * 128:(j + 1) * 128],
                    ln2xb[w % 2][:, j * 128:(j + 1) * 128], identB_sb[:]),
                    waits=[("ln2", lnv), free_psb] if j == 0 else (), force=True,
                    inc="trb" if j == NKC - 1 else None)
            trbv = s.val("trb")
            for j in range(NKC):
                s.on("vector", lambda v, w=w, j=j: v.tensor_copy(
                    ln2xT[:, j * 512 + w * 128:j * 512 + (w + 1) * 128],
                    psb[:, j * 128:(j + 1) * 128]),
                    waits=([("trb", trbv), ("h1", prev_h1)]) if j == 0 else (),
                    force=True, inc="trbc" if j == NKC - 1 else None)
            trbcv = s.val("trbc")
            free_psb = ("trbc", trbcv)
            last_ln2xb[w % 2] = ("trbc", trbcv)

        for w in range(GT):
            t = 4 * g + w
            col, ww = npair([("atn", ATN)])
            for c in range(NKC):
                for h in range(2):
                    s.on("tensor", lambda tt, c=c, h=h, w=w, col=col: tt.matmul(
                        ps[:, col + h * 512:col + (h + 1) * 512],
                        attnT[:, c * 512 + w * 128:c * 512 + (w + 1) * 128],
                        wo_bf[:, c * D + h * 512:c * D + (h + 1) * 512],
                        start=(c == 0), stop=False, skip_group_check=True),
                        waits=ww if (c == 0 and h == 0) else (), force=True)
            for h in range(2):
                s.on("tensor", lambda tt, h=h, col=col: tt.matmul(
                    ps[:, col + h * 512:col + (h + 1) * 512], onesrow_sb[:],
                    bo_row[:, h * 512:(h + 1) * 512],
                    start=False, stop=True, skip_group_check=True),
                    inc="womm" if h == 1 else None)
            wov = s.val("womm")
            s.on("vector", lambda v, w=w, col=col: v.scalar_tensor_tensor(
                xresB[w][:], ps[:, col:col + D], 1.0, xaB[w][:],
                ALU.mult, ALU.add, accum_out=statC[:, 0 + w % 2:1 + w % 2]),
                waits=[("womm", wov), bxa_w[w]], force=True, inc="xrs")
            xrv = s.val("xrs")
            free_pair[col] = ("xrs", xrv)
            last_xaB[w] = ("xrs", xrv)
            s.on("scalar", lambda sc, w=w: sc.activation(
                junkB[:], xresB[w][:], AF.Square,
                accum_out=statC[:, 2 + w % 2:3 + w % 2]),
                waits=[("xrs", xrv)], inc="s2ssq")
            sqv = s.val("s2ssq")
            s.on("vector", lambda v, w=w: v.tensor_scalar(
                statC[:, 4 + w % 2:5 + w % 2], statC[:, 0 + w % 2:1 + w % 2],
                1.0 / D, None, ALU.mult),
                waits=[("xrs", xrv)], force=True, inc="s2ch")
            s.on("vector", lambda v, w=w: v.tensor_mul(
                smc[:, 18 + w % 2:19 + w % 2], statC[:, 4 + w % 2:5 + w % 2],
                statC[:, 4 + w % 2:5 + w % 2]),
                waits=[("s2ch", s.val("s2ch"))], force=True, inc="s2ch")
            s.on("vector", lambda v, w=w: v.scalar_tensor_tensor(
                statC[:, 6 + w % 2:7 + w % 2], statC[:, 2 + w % 2:3 + w % 2],
                1.0 / D, smc[:, 18 + w % 2:19 + w % 2], ALU.mult, ALU.subtract),
                waits=[("s2ssq", sqv), ("s2ch", s.val("s2ch"))], force=True,
                inc="s2ch")
            scv = s.val("s2ch")
            s.on("scalar", lambda sc, w=w: sc.activation(
                smc[:, 20 + w % 2:21 + w % 2], statC[:, 6 + w % 2:7 + w % 2],
                AF.Ln, bias=eps6[:]),
                waits=[("s2ch", scv)], inc="s2ch")
            s.on("scalar", lambda sc, w=w: sc.activation(
                statC[:, 6 + w % 2:7 + w % 2], smc[:, 20 + w % 2:21 + w % 2],
                AF.Exp, scale=-0.5),
                waits=[("s2ch", s.val("s2ch"))], force=True, inc="s2ch")
            scv = s.val("s2ch")
            s.on("vector", lambda v, w=w: v.tensor_scalar(
                ln2xb[w % 2][:], xresB[w][:], statC[:, 4 + w % 2:5 + w % 2],
                statC[:, 6 + w % 2:7 + w % 2], ALU.subtract, ALU.mult),
                waits=[("s2ch", scv), last_ln2xb.get(w % 2, ("fold", 0))],
                force=True, inc="ln2")
            ln2_done[w] = s.val("ln2")
            if w >= 1:
                emit_trb(w - 1)
        emit_trb(GT - 1)
        TRBC = s.val("trbc")
        LN2G = s.val("ln2")

        for f1 in range(NF1):
            wldv = s.dma("sync", lambda e, f1=f1: e.dma_start(
                out=w1r[f1 % 3][:], in_=w1scr[f1]),
                "w1ld", f1 % 3, waits=[last_w1r.get(f1 % 3, ("fold", 0))])
            col, w1w = nsingle([wldv, ("trbc", TRBC)])
            for kc in range(NKC):
                s.on("tensor", lambda tt, f1=f1, kc=kc, col=col: tt.matmul(
                    ps[:, col:col + 512],
                    w1r[f1 % 3][:, kc * 128:(kc + 1) * 128],
                    ln2xT[:, kc * 512:(kc + 1) * 512],
                    start=(kc == 0), stop=(kc == NKC - 1), skip_group_check=True),
                    waits=w1w if kc == 0 else (), force=True,
                    inc="ymm" if kc == NKC - 1 else None)
            mv = s.val("ymm")
            s.on("scalar", lambda sc, f1=f1, col=col: sc.activation(
                h1T[:, f1 * 512:(f1 + 1) * 512], ps[:, col:col + 512],
                AF.Gelu_apprx_tanh, bias=b1p[:, f1:f1 + 1]),
                waits=[("ymm", mv)], force=True, inc="h1")
            hv = s.val("h1")
            free_single[col] = ("h1", hv)
            last_w1r[f1 % 3] = ("h1", hv)
        H1V = s.val("h1")
        prev_h1 = H1V

        XRS4 = s.val("xrs")
        fh_evac = {}
        for h in range(2):
            w2u0 = s.val("w2u")
            for w in range(GT):
                col = 512 + 512 * w
                if h == 0:
                    bw = ([("xrs", XRS4), free_pair[512], free_pair[1536]]
                          if w == 0 else ())
                else:
                    bw = [fh_evac[(0, w)]]
                s.on("tensor", lambda tt, h=h, col=col: tt.matmul(
                    ps[:, col:col + 512], onesrow_sb[:],
                    b2_row[:, h * 512:(h + 1) * 512],
                    start=True, stop=False, skip_group_check=True),
                    waits=bw, force=True)
            for f1 in range(NF1):
                slot = f1 % 3
                eng = "sync" if f1 % 2 == 0 else "gpsimd"
                w2v = s.dma(eng, lambda e, f1=f1, slot=slot, h=h: e.dma_start(
                    out=w2r[slot][:], in_=w2scr[f1, :, h * 512:(h + 1) * 512]),
                    "w2f", slot,
                    waits=[last_w2r.get(slot, ("fold", 0))]
                    + (W2CASTS if (g == 0 and h == 0 and f1 < 3) else []))
                for w in range(GT):
                    col = 512 + 512 * w
                    s.on("tensor", lambda tt, f1=f1, w=w, col=col, slot=slot:
                         tt.matmul(
                             ps[:, col:col + 512],
                             h1T[:, f1 * 512 + w * 128:f1 * 512 + (w + 1) * 128],
                             w2r[slot][:, 0:512],
                             start=False, stop=(f1 == NF1 - 1),
                             skip_group_check=True),
                         waits=[w2v, ("h1", H1V)] if w == 0 else (), force=True,
                         inc=("w2u" if f1 == NF1 - 1 else
                              ("w2t" if w == GT - 1 else None)))
                if f1 == NF1 - 1:
                    last_w2r[slot] = ("w2u", s.val("w2u"))
                else:
                    last_w2r[slot] = ("w2t", s.val("w2t"))
            for w in range(GT):
                col = 512 + 512 * w
                s.on("vector", lambda v, w=w, col=col, h=h: v.tensor_add(
                    xresB[w][:, h * 512:(h + 1) * 512], ps[:, col:col + 512],
                    xresB[w][:, h * 512:(h + 1) * 512]),
                    waits=[("w2u", w2u0 + w + 1), ("ln2", LN2G)], force=True,
                    inc="zf")
                fh_evac[(h, w)] = ("zf", s.val("zf"))
        free_pair[512] = fh_evac[(1, 1)]
        free_pair[1536] = fh_evac[(1, 3)]

        if True:
            for w in range(GT):
                t = 4 * g + w
                s.on("vector", lambda v, w=w: v.scalar_tensor_tensor(
                    zbuf[w % 2][:], xresB[w][:], 1.0, xaB[w][:],
                    ALU.mult, ALU.add, accum_out=statC[:, 8 + w % 2:9 + w % 2]),
                    waits=[fh_evac[(1, w)], last_zbuf.get(w % 2, ("fold", 0))],
                    force=True, inc="zf")
                zv = s.val("zf")
                last_xaB[w] = ("zf", zv)
                s.on("scalar", lambda sc, w=w: sc.activation(
                    junkB[:], zbuf[w % 2][:], AF.Square,
                    accum_out=statC[:, 10 + w % 2:11 + w % 2]),
                    waits=[("zf", zv)], inc="s3ssq")
                sqv = s.val("s3ssq")
                s.on("vector", lambda v, w=w: v.tensor_scalar(
                    statC[:, 12 + w % 2:13 + w % 2], statC[:, 8 + w % 2:9 + w % 2],
                    1.0 / D, None, ALU.mult),
                    waits=[("zf", zv)], force=True, inc="s3ch")
                s.on("vector", lambda v, w=w: v.tensor_mul(
                    smc[:, 6 + w % 2:7 + w % 2], statC[:, 12 + w % 2:13 + w % 2],
                    statC[:, 12 + w % 2:13 + w % 2]),
                    waits=[("s3ch", s.val("s3ch"))], force=True, inc="s3ch")
                s.on("vector", lambda v, w=w: v.scalar_tensor_tensor(
                    statC[:, 14 + w % 2:15 + w % 2],
                    statC[:, 10 + w % 2:11 + w % 2],
                    1.0 / D, smc[:, 6 + w % 2:7 + w % 2], ALU.mult, ALU.subtract),
                    waits=[("s3ssq", sqv), ("s3ch", s.val("s3ch"))], force=True,
                    inc="s3ch")
                scv = s.val("s3ch")
                s.on("scalar", lambda sc, w=w: sc.activation(
                    smc[:, 8 + w % 2:9 + w % 2], statC[:, 14 + w % 2:15 + w % 2],
                    AF.Ln, bias=eps5[:]),
                    waits=[("s3ch", scv)], inc="s3ch")
                s.on("scalar", lambda sc, w=w: sc.activation(
                    statC[:, 14 + w % 2:15 + w % 2], smc[:, 8 + w % 2:9 + w % 2],
                    AF.Exp, scale=-0.5),
                    waits=[("s3ch", s.val("s3ch"))], force=True, inc="s3ch")
                scv = s.val("s3ch")
                s.on("vector", lambda v, w=w: v.tensor_scalar(
                    x1f[w % 2][:], zbuf[w % 2][:], statC[:, 12 + w % 2:13 + w % 2],
                    statC[:, 14 + w % 2:15 + w % 2], ALU.subtract, ALU.mult),
                    waits=[("s3ch", scv), last_x1f.get(w % 2, ("fold", 0))],
                    force=True, inc="zf2")
                z2v = s.val("zf2")
                last_zbuf[w % 2] = ("zf2", z2v)
                last_x1f[w % 2] = s.dma("sync", lambda e, t=t, w=w: e.dma_start(
                    out=x1scr[t], in_=x1f[w % 2][:]),
                    "x1st", w % 2, waits=[("zf2", z2v)])
                s.on("vector", lambda v, w=w: v.tensor_copy(
                    x1bfB[w % 2][:], x1f[w % 2][:]),
                    waits=[("zf2", z2v), last_x1bf.get(w % 2, ("fold", 0))],
                    force=True, inc="x1bfc")
                x1bv = s.val("x1bfc")
                for j in range(NKC):
                    s.on("tensor", lambda tt, w=w, j=j: tt.transpose(
                        psb[:, j * 128:(j + 1) * 128],
                        x1bfB[w % 2][:, j * 128:(j + 1) * 128], identB_sb[:]),
                        waits=[("x1bfc", x1bv), free_psb] if j == 0 else (),
                        force=True, inc="trb" if j == NKC - 1 else None)
                trbv = s.val("trb")
                last_x1bf[w % 2] = ("trb", trbv)
                for j in range(NKC):
                    s.on("vector", lambda v, w=w, j=j: v.tensor_copy(
                        x1tb[w % 2][:, j * 128:(j + 1) * 128],
                        psb[:, j * 128:(j + 1) * 128]),
                        waits=([("trb", trbv), last_x1tb.get(w % 2, ("fold", 0))])
                        if j == 0 else (),
                        force=True, inc="trbc" if j == NKC - 1 else None)
                trbcv = s.val("trbc")
                free_psb = ("trbc", trbcv)
                last_x1tb[w % 2] = s.dma("sync", lambda e, t=t, w=w: e.dma_start(
                    out=x1Tscr[:, :, t * 128:(t + 1) * 128].rearrange("c p f -> p c f"),
                    in_=x1tb[w % 2][:].rearrange("p (c f) -> p c f", c=NKC)),
                    "x1tst", w % 2, waits=[("trbc", trbcv)])
        last_attnT = ("womm", s.val("womm"))

    ctx["BARF"] = ([("zf2", s.val("zf2")), ("trbc", s.val("trbc")),
                    ("w2u", s.val("w2u")), ("w2t", s.val("w2t")),
                    ("h1", s.val("h1")), ("zf", s.val("zf")),
                    ("atn", s.val("atn")), ("qmm", s.val("qmm")),
                    ("s3ssq", s.val("s3ssq")), ("ymm", s.val("ymm"))]
                   + s.dma_totals("x1tst") + s.dma_totals("x1st")
                   + s.dma_totals("w1ld") + s.dma_totals("w2f")
                   + s.dma_totals("bxa") + s.dma_totals("bxt"))


def build_ffn(nc, s, ctx):
    ctx["scope_b"].close()
    ctx["scope_ab"].close()
    ps = ctx["ps"]; es = ctx["es"]
    identF_sb = ctx["identF_sb"]; identB_sb = ctx["identB_sb"]
    onesrow_sb = ctx["onesrow_sb"]; onescol_sb = ctx["onescol_sb"]
    eps5 = ctx["eps5"]; smc = ctx["smc"]
    wr_bf = ctx["wr_bf"]
    ut_sb = ctx["ut_sb"]; cio_sb = ctx["cio_sb"]
    x1scr = ctx["x1scr"]; x1Tscr = ctx["x1Tscr"]
    x1g = ctx["x1g"]; yg = ctx["yg"]
    wg32 = ctx["wg32"]; wu32 = ctx["wu32"]; wd32 = ctx["wd32"]
    y = ctx["y"]
    BARF = ctx["BARF"]

    scope_f = ExitStack()
    ctx["scope_f"] = scope_f

    def sbf(name, shape, dt=F32):
        return scope_f.enter_context(nc.sbuf_tensor(name, shape, dt))

    x1Tc = [sbf(f"x1Tc{i}", [128, NKC * CH], BF16) for i in range(2)]
    wgr = [sbf(f"wgr{i}", [128, NKC * 128], BF16) for i in range(4)]
    wur = [sbf(f"wur{i}", [128, NKC * 128], BF16) for i in range(4)]
    wdb = sbf("wdb", [128, NFC * D], BF16)
    gxt = [sbf(f"gxt{i}", [128, 4 * D], BF16) for i in range(2)]
    xgT = sbf("xgT", [128, NKC * 512], BF16)
    hTf = sbf("hTf", [128, NFC * 512], BF16)
    tref = [sbf(f"tref{i}", [128, 512]) for i in range(2)]
    ygb = [sbf(f"ygb{i}", [128, D], BF16) for i in range(2)]
    gbuf1 = [sbf(f"gbuf1{i}", [128, D], BF16) for i in range(3)]
    gbuf2 = [sbf(f"gbuf2{i}", [128, D], BF16) for i in range(3)]
    x1rd = [sbf(f"x1rd{i}", [128, D]) for i in range(2)]
    xbf_t = [sbf(f"xbf_t{i}", [128, D], BF16) for i in range(2)]
    z2b = [sbf(f"z2b{i}", [128, D]) for i in range(3)]
    outb = [sbf(f"outb{i}", [128, D]) for i in range(2)]
    prt = sbf("prt", [8, CH])
    rtl = [sbf(f"rtl{i}", [128, 8]) for i in range(2)]
    rtm = [sbf(f"rtm{i}", [128, 8]) for i in range(2)]
    rtm2 = [sbf(f"rtm2{i}", [128, 8]) for i in range(2)]
    rtm3 = [sbf(f"rtm3{i}", [128, 8]) for i in range(2)]
    rtmA = [sbf(f"rtmA{i}", [128, 8]) for i in range(2)]
    m12b = [sbf(f"m12b{i}", [128, 8], BF16) for i in range(2)]
    prefsb = [sbf(f"prefsb{i}", [128, 8]) for i in range(2)]
    s0sb = [sbf(f"s0sb{i}", [128, 8]) for i in range(2)]
    tmp8 = [sbf(f"tmp8{i}", [128, 8]) for i in range(2)]
    ciobc = sbf("ciobc", [128, 8])
    baseR = [sbf(f"baseR{i}", [1, 8]) for i in range(2)]
    idx1i = sbf("idx1i", [128, NTB], mybir.dt.int32)
    idx2i = sbf("idx2i", [128, NTB], mybir.dt.int32)
    idxi = [sbf(f"idxi{i}", [128, 1], mybir.dt.int32) for i in range(4)]
    ce1s = sbf("ce1s", [128, NTB])
    ce2s = sbf("ce2s", [128, NTB])
    psf = scope_f.enter_context(nc.psum_tensor("psf", [128, 1024], BF16))

    barrier = list(BARF)
    PREF, TOT, BC = 2560, 2576, 2592

    last_rt = {}; last_xbf = {}; last_x1rd = {}; last_z2b = {}; last_outb = {}
    last_gb = {}; last_wslot = {}; last_gxt = {}; last_ygb = {}; last_sci = {}
    free_rtbank = ("fold", 0)
    free_psf = ("trbc", s.val("trbc"))
    free_b5 = ("fold", 0)
    gf = [0]

    s.on("tensor", lambda t: t.matmul(ps[:, BC:BC + 8], onesrow_sb[:], cio_sb[:],
                                      start=True, stop=True, skip_group_check=True),
         waits=barrier, force=True, inc="dnmm")
    s.on("vector", lambda v: v.tensor_copy(ciobc[:], ps[:, BC:BC + 8]),
         waits=barrier + [("dnmm", s.val("dnmm"))], force=True, inc="ln2")
    s.on("vector", lambda v: v.memset(baseR[0][:], 0.0), inc="s2ch")
    free_b5 = ("ln2", s.val("ln2"))

    last_x1Tc = {}
    for ch in range(NCH):
        fxt_v = s.dma("sync", lambda e, ch=ch: e.dma_start(
            out=x1Tc[ch % 2][:].rearrange("p (c f) -> p c f", c=NKC),
            in_=x1Tscr[:, :, ch * CH:(ch + 1) * CH].rearrange("c p f -> p c f")),
            "fxt", ch % 2, waits=barrier + [last_x1Tc.get(ch % 2, ("fold", 0))])
        for kc in range(NKC):
            for h in range(2):
                s.on("tensor", lambda tt, ch=ch, kc=kc, h=h: tt.matmul(
                    ps[0:8, h * 512:(h + 1) * 512],
                    wr_bf[:, kc * NE:(kc + 1) * NE],
                    x1Tc[ch % 2][:, kc * CH + h * 512:kc * CH + (h + 1) * 512],
                    start=(kc == 0), stop=(kc == NKC - 1), skip_group_check=True),
                    waits=(barrier + [fxt_v, ("atn", s.val("atn")),
                                      ("ln2", s.val("ln2")),
                                      ("bcev", s.val("bcev"))]
                           + s.dma_totals("gwr"))
                    if (kc == 0 and h == 0) else (), force=True,
                    inc="dnmm" if (kc == NKC - 1 and h == 1) else None)
        rtv = s.val("dnmm")
        last_x1Tc[ch % 2] = ("dnmm", rtv)
        s.on("vector", lambda v: v.tensor_copy(prt[:], ps[0:8, 0:CH]),
             waits=[("dnmm", rtv)], force=True, inc="bcev")
        rte_v = s.val("bcev")
        for t8 in range(NT8):
            tt_g = ch * NT8 + t8
            sl = t8 % 4
            pr = t8 % 2
            s.on("tensor", lambda tx, t8=t8, sl=sl: tx.transpose(
                ps[0:128, 2048 + sl * 128:2048 + sl * 128 + 8],
                prt[:, t8 * 128:(t8 + 1) * 128], identF_sb[0:8, 0:8]),
                waits=[("bcev", rte_v), free_rtbank], force=True, inc="trb")
            trv = s.val("trb")
            s.on("vector", lambda v, sl=sl, pr=pr: v.tensor_copy(
                rtl[pr][:], ps[0:128, 2048 + sl * 128:2048 + sl * 128 + 8]),
                waits=[("trb", trv), last_rt.get(pr, ("fold", 0))], force=True,
                inc="ln2")
            free_rtbank = ("ln2", s.val("ln2"))

            def sm_op(fn, waits=(), inc="ln2", eng="vector"):
                s.on(eng, fn, waits=[("ln2", s.val("ln2"))] + list(waits),
                     force=True, inc=inc)

            c0 = 10 + pr
            sm_op(lambda v, pr=pr, c0=c0: v.reduce_max(
                smc[:, c0:c0 + 1], rtl[pr][:], axis=AX.X))
            sm_op(lambda v, pr=pr, c0=c0: v.tensor_scalar(
                rtm[pr][:], rtl[pr][:], smc[:, c0:c0 + 1], None, ALU.subtract))
            sm_op(lambda v, pr=pr: v.tensor_scalar(
                rtmA[pr][:], rtm[pr][:], 0.0, None, ALU.is_equal))
            sm_op(lambda v, pr=pr: v.scalar_tensor_tensor(
                rtm3[pr][:], rtmA[pr][:], -30000.0, rtl[pr][:],
                ALU.mult, ALU.add))
            sm_op(lambda v, pr=pr, c0=c0: v.reduce_max(
                smc[:, c0 + 4:c0 + 5], rtm3[pr][:], axis=AX.X))
            sm_op(lambda v, pr=pr, c0=c0: v.tensor_scalar(
                rtm2[pr][:], rtl[pr][:], smc[:, c0 + 4:c0 + 5], None, ALU.is_ge))
            sm_op(lambda v, pr=pr: v.tensor_copy(m12b[pr][:], rtm2[pr][:]))
            sm_op(lambda v, pr=pr, c0=c0: v.tensor_sub(
                smc[:, c0:c0 + 1], smc[:, c0:c0 + 1], smc[:, c0 + 4:c0 + 5]))
            sm_op(lambda sc_, pr=pr, c0=c0, tt_g=tt_g: sc_.activation(
                ce1s[:, tt_g:tt_g + 1], smc[:, c0:c0 + 1], AF.Sigmoid),
                eng="scalar")
            sm_op(lambda v, pr=pr, tt_g=tt_g: v.tensor_scalar(
                ce2s[:, tt_g:tt_g + 1], ce1s[:, tt_g:tt_g + 1], -1.0, 1.0,
                ALU.mult, ALU.add))
            M12V = s.val("ln2")
            s.on("tensor", lambda tx, pr=pr: tx.matmul(
                ps[:, PREF:PREF + 8], ut_sb[:], m12b[pr][:],
                start=True, stop=True, skip_group_check=True),
                waits=[("ln2", M12V), free_b5], force=True)
            s.on("tensor", lambda tx, pr=pr: tx.matmul(
                ps[0:1, TOT:TOT + 8], onescol_sb[:], m12b[pr][:],
                start=True, stop=True, skip_group_check=True), force=True)
            cur = tt_g % 2
            s.on("tensor", lambda tx, cur=cur: tx.matmul(
                ps[:, BC:BC + 8], onesrow_sb[:], baseR[cur][:],
                start=True, stop=True, skip_group_check=True),
                waits=[("s2ch", s.val("s2ch"))], force=True, inc="dnmm")
            pfv = s.val("dnmm")
            sm_op(lambda v, pr=pr: v.tensor_copy(prefsb[pr][:], ps[:, PREF:PREF + 8]),
                  waits=[("dnmm", pfv)])
            sm_op(lambda v, cur=cur: v.tensor_add(
                baseR[1 - cur][:], baseR[cur][:], ps[0:1, TOT:TOT + 8]),
                inc="s2ch")
            sm_op(lambda v, pr=pr: v.tensor_add(s0sb[pr][:], prefsb[pr][:],
                                                ps[:, BC:BC + 8]))
            FB5 = s.val("ln2")
            free_b5 = ("ln2", FB5)
            sm_op(lambda v, pr=pr: v.tensor_scalar(
                s0sb[pr][:], s0sb[pr][:], float(CAP - 1), None, ALU.min))
            sm_op(lambda v, pr=pr: v.tensor_add(s0sb[pr][:], s0sb[pr][:], ciobc[:]))
            sm_op(lambda v, pr=pr: v.tensor_mul(tmp8[pr][:], s0sb[pr][:], rtmA[pr][:]))
            sm_op(lambda v, pr=pr, tt_g=tt_g: v.reduce_sum(
                smc[:, 14 + pr:15 + pr], tmp8[pr][:], axis=AX.X))
            sm_op(lambda v, pr=pr: v.tensor_scalar(
                smc[:, 14 + pr:15 + pr], smc[:, 14 + pr:15 + pr],
                float(NE * CAP - 1), None, ALU.min))
            sm_op(lambda v, pr=pr, tt_g=tt_g: v.tensor_copy(
                idx1i[:, tt_g:tt_g + 1], smc[:, 14 + pr:15 + pr]))
            sm_op(lambda v, pr=pr: v.tensor_copy(
                idxi[pr][:], smc[:, 14 + pr:15 + pr]),
                waits=[last_sci.get(pr, ("fold", 0))])
            sm_op(lambda v, pr=pr: v.tensor_sub(rtm2[pr][:], rtm2[pr][:], rtmA[pr][:]))
            sm_op(lambda v, pr=pr: v.tensor_mul(tmp8[pr][:], s0sb[pr][:], rtm2[pr][:]))
            sm_op(lambda v, pr=pr, tt_g=tt_g: v.reduce_sum(
                smc[:, 14 + pr:15 + pr], tmp8[pr][:], axis=AX.X))
            sm_op(lambda v, pr=pr: v.tensor_scalar(
                smc[:, 14 + pr:15 + pr], smc[:, 14 + pr:15 + pr],
                float(NE * CAP - 1), None, ALU.min))
            sm_op(lambda v, pr=pr, tt_g=tt_g: v.tensor_copy(
                idx2i[:, tt_g:tt_g + 1], smc[:, 14 + pr:15 + pr]))
            sm_op(lambda v, pr=pr: v.tensor_copy(
                idxi[2 + pr][:], smc[:, 14 + pr:15 + pr]),
                waits=[last_sci.get(2 + pr, ("fold", 0))], inc="xrs")
            last_rt[pr] = ("xrs", s.val("xrs"))
            RTDONE = s.val("xrs")
            x1v = s.dma("sync", lambda e, tt_g=tt_g, t8=t8: e.dma_start(
                out=x1rd[t8 % 2][:], in_=x1scr[tt_g]),
                "x1ld", t8 % 2, waits=[last_x1rd.get(t8 % 2, ("fold", 0))])
            s.on("vector", lambda v, t8=t8: v.tensor_copy(
                xbf_t[t8 % 2][:], x1rd[t8 % 2][:]),
                waits=[x1v, last_xbf.get(t8 % 2, ("fold", 0))], force=True,
                inc="zf")
            xbv = s.val("zf")
            last_x1rd[t8 % 2] = ("zf", xbv)
            sc1 = s.dma("gpsimd", lambda g_, t8=t8: g_.indirect_dma_start(
                x1g, IndirectOffsetOnAxis(idxi[t8 % 2][:], 0),
                xbf_t[t8 % 2][:], None),
                "axt", t8 % 2, waits=[("zf", xbv), ("xrs", RTDONE)])
            sc2 = s.dma("gpsimd", lambda g_, t8=t8: g_.indirect_dma_start(
                x1g, IndirectOffsetOnAxis(idxi[2 + t8 % 2][:], 0),
                xbf_t[t8 % 2][:], None),
                "bxt", t8 % 2)
            last_xbf[t8 % 2] = sc2
            last_sci[t8 % 2] = sc1
            last_sci[2 + t8 % 2] = sc2

    SCT = s.dma_totals("axt") + s.dma_totals("bxt")

    for e8 in range(NE):
        fwd_v = s.dma("gpsimd", lambda g_, e8=e8: g_.dma_start(
            out=wdb[:].rearrange("p (c d) -> p c d", c=NFC),
            in_=wd32[e8].rearrange("c p d -> p c d")),
            "fwd", e8 % 2, waits=barrier + [("womm", s.val("womm"))])
        for sc_i in range(NSC):
            row0 = e8 * CAP + sc_i * 512
            gslot = (e8 * NSC + sc_i) % 2
            WPREV = s.val("womm")
            gx_v = s.dma("sync", lambda e, row0=row0, gslot=gslot: e.dma_start(
                out=gxt[gslot][:].rearrange("p (t d) -> p t d", t=4),
                in_=x1g[row0:row0 + 512].rearrange("(t p) d -> p t d", p=128)),
                "axa", gslot, waits=SCT + [last_gxt.get(gslot, ("fold", 0))])
            for t4 in range(4):
                for j in range(NKC):
                    s.on("tensor", lambda tx, t4=t4, j=j, gslot=gslot: tx.transpose(
                        psf[:, j * 128:(j + 1) * 128],
                        gxt[gslot][:, t4 * D + j * 128:t4 * D + (j + 1) * 128],
                        identB_sb[:]),
                        waits=[gx_v, free_psf] if j == 0 else (), force=True,
                        inc="trb" if j == NKC - 1 else None)
                trv = s.val("trb")
                for j in range(NKC):
                    s.on("vector", lambda v, t4=t4, j=j: v.tensor_copy(
                        xgT[:, j * 512 + t4 * 128:j * 512 + (t4 + 1) * 128],
                        psf[:, j * 128:(j + 1) * 128]),
                        waits=([("trb", trv), ("qmm", s.val("qmm"))])
                        if j == 0 else (),
                        force=True, inc="trbc" if j == NKC - 1 else None)
                free_psf = ("trbc", s.val("trbc"))
            TRX = s.val("trbc")
            for f in range(NFC):
                slot = gf[0] % 4
                s.dma("gpsimd", lambda g_, e8=e8, f=f, slot=slot: g_.dma_start(
                    out=wgr[slot][:], in_=wg32[e8, f]),
                    "fw", slot, waits=[last_wslot.get(slot, ("fold", 0))])
                fw_v = s.dma("gpsimd", lambda g_, e8=e8, f=f, slot=slot: g_.dma_start(
                    out=wur[slot][:], in_=wu32[e8, f]),
                    "fw", slot)
                gf[0] += 1
                gcol = 0 if f % 2 == 0 else 1024
                ucol = 512 if f % 2 == 0 else 1536
                tw = [fw_v, ("trbc", TRX), ("phv", max(0, s.val("phv") - 1)),
                      ("phr", max(0, s.val("phr") - 1))]
                if f == 0:
                    tw.append(("atn", s.val("atn")))
                for kc in range(NKC):
                    s.on("tensor", lambda tx, kc=kc, slot=slot, gcol=gcol: tx.matmul(
                        ps[:, gcol:gcol + 512],
                        wgr[slot][:, kc * 128:(kc + 1) * 128],
                        xgT[:, kc * 512:(kc + 1) * 512],
                        start=(kc == 0), stop=(kc == NKC - 1),
                        skip_group_check=True),
                        waits=tw if kc == 0 else (), force=True)
                for kc in range(NKC):
                    s.on("tensor", lambda tx, kc=kc, slot=slot, ucol=ucol: tx.matmul(
                        ps[:, ucol:ucol + 512],
                        wur[slot][:, kc * 128:(kc + 1) * 128],
                        xgT[:, kc * 512:(kc + 1) * 512],
                        start=(kc == 0), stop=(kc == NKC - 1),
                        skip_group_check=True),
                        force=True, inc="qmm" if kc == NKC - 1 else None)
                guv = s.val("qmm")
                last_wslot[slot] = ("qmm", guv)
                s.on("scalar", lambda sc_, f=f, gcol=gcol: sc_.activation(
                    tref[f % 2][:], ps[:, gcol:gcol + 512], AF.Relu),
                    waits=[("qmm", guv), ("phv", max(0, s.val("phv") - 1))],
                    force=True, inc="phr")
                prv = s.val("phr")
                mw = [("phr", prv)]
                if f == 0:
                    mw.append(("womm", WPREV))
                s.on("vector", lambda v, f=f, ucol=ucol: v.tensor_mul(
                    hTf[:, f * 512:(f + 1) * 512], tref[f % 2][:],
                    ps[:, ucol:ucol + 512]),
                    waits=mw, force=True, inc="phv")
            HDV = s.val("phv")
            last_gxt[gslot] = ("qmm", s.val("qmm"))
            for t4 in range(4):
                dcol = 2048 if t4 % 2 == 0 else 0
                dw = [fwd_v, ("phv", HDV), ("atn", max(0, s.val("atn") - 1))]
                for kc in range(NFC):
                    for h in range(2):
                        s.on("tensor", lambda tx, kc=kc, h=h, t4=t4, dcol=dcol: tx.matmul(
                            ps[:, dcol + h * 512:dcol + (h + 1) * 512],
                            hTf[:, kc * 512 + t4 * 128:kc * 512 + (t4 + 1) * 128],
                            wdb[:, kc * D + h * 512:kc * D + (h + 1) * 512],
                            start=(kc == 0), stop=(kc == NFC - 1),
                            skip_group_check=True),
                            waits=dw if (kc == 0 and h == 0) else (), force=True,
                            inc="womm" if (kc == NFC - 1 and h == 1) else None)
                ddv = s.val("womm")
                s.on("vector", lambda v, t4=t4, dcol=dcol: v.tensor_copy(
                    ygb[t4 % 2][:], ps[:, dcol:dcol + D]),
                    waits=[("womm", ddv), last_ygb.get(t4 % 2, ("fold", 0))],
                    force=True, inc="atn")
                av = s.val("atn")
                last_ygb[t4 % 2] = s.dma("sync", lambda e, row0=row0, t4=t4: e.dma_start(
                    out=yg[row0 + t4 * 128:row0 + (t4 + 1) * 128],
                    in_=ygb[t4 % 2][:]),
                    "bxa", t4 % 2, waits=[("atn", av)])

    YGST = s.dma_totals("bxa")

    RTALL = ("xrs", s.val("xrs"))
    last_zfl = {}; last_stat = {}
    for tt_g in range(NTB):
        p3 = tt_g % 3
        c0 = 12 + 4 * p3
        g1v = s.dma("gpsimd", lambda g_, p3=p3, tt_g=tt_g: g_.indirect_dma_start(
            gbuf1[p3][:], None, yg, IndirectOffsetOnAxis(idx1i[:, tt_g:tt_g + 1], 0)),
            "w2ld", p3, waits=YGST + [RTALL, last_gb.get((0, p3), ("fold", 0))])
        g2v = s.dma("gpsimd", lambda g_, p3=p3, tt_g=tt_g: g_.indirect_dma_start(
            gbuf2[p3][:], None, yg, IndirectOffsetOnAxis(idx2i[:, tt_g:tt_g + 1], 0)),
            "bgld", p3, waits=[last_gb.get((1, p3), ("fold", 0))])
        x1v = s.dma("sync", lambda e, tt_g=tt_g, pr=tt_g % 2: e.dma_start(
            out=x1rd[pr][:], in_=x1scr[tt_g]),
            "x1ld", tt_g % 2, waits=[last_x1rd.get(tt_g % 2, ("fold", 0))])
        s.on("vector", lambda v, p3=p3, tt_g=tt_g, pr=tt_g % 2: v.scalar_tensor_tensor(
            z2b[p3][:], gbuf1[p3][:], ce1s[:, tt_g:tt_g + 1], x1rd[pr][:],
            ALU.mult, ALU.add),
            waits=[g1v, x1v, last_z2b.get(p3, ("fold", 0))], force=True, inc="zf2")
        s.on("vector", lambda v, p3=p3, tt_g=tt_g, c0=c0, pr=tt_g % 2:
             v.scalar_tensor_tensor(
                 x1rd[pr][:], gbuf2[p3][:], ce2s[:, tt_g:tt_g + 1], z2b[p3][:],
                 ALU.mult, ALU.add, accum_out=smc[:, c0:c0 + 1]),
             waits=[g2v, ("zf2", s.val("zf2")),
                    last_stat.get(p3, ("fold", 0))],
             force=True, inc="zf2")
        zv = s.val("zf2")
        last_gb[(0, p3)] = ("zf2", zv)
        s.on("scalar", lambda sc_, p3=p3, c0=c0, pr=tt_g % 2: sc_.activation(
            z2b[p3][:], x1rd[pr][:], AF.Square,
            accum_out=smc[:, c0 + 1:c0 + 2]),
            waits=[("zf2", zv), last_stat.get(p3, ("fold", 0))], inc="s2ssq")
        sqv = s.val("s2ssq")
        s.on("vector", lambda v, c0=c0: v.tensor_scalar(
            smc[:, c0 + 2:c0 + 3], smc[:, c0:c0 + 1], 1.0 / D, None,
            ALU.mult),
            waits=[("zf2", zv)], force=True, inc="s2ch")
        s.on("vector", lambda v, c0=c0: v.tensor_mul(
            smc[:, c0 + 3:c0 + 4], smc[:, c0 + 2:c0 + 3],
            smc[:, c0 + 2:c0 + 3]),
            waits=[("s2ch", s.val("s2ch"))], force=True, inc="s2ch")
        s.on("vector", lambda v, c0=c0: v.scalar_tensor_tensor(
            smc[:, c0:c0 + 1], smc[:, c0 + 1:c0 + 2], 1.0 / D,
            smc[:, c0 + 3:c0 + 4], ALU.mult, ALU.subtract),
            waits=[("s2ssq", sqv), ("s2ch", s.val("s2ch"))], force=True,
            inc="s2ch")
        scv = s.val("s2ch")
        s.on("scalar", lambda sc_, c0=c0: sc_.activation(
            smc[:, c0 + 3:c0 + 4], smc[:, c0:c0 + 1], AF.Ln,
            bias=eps5[:]),
            waits=[("s2ch", scv)], inc="s2ch")
        s.on("scalar", lambda sc_, c0=c0: sc_.activation(
            smc[:, c0 + 1:c0 + 2], smc[:, c0 + 3:c0 + 4], AF.Exp, scale=-0.5),
            waits=[("s2ch", s.val("s2ch"))], force=True, inc="s2ch")
        scv = s.val("s2ch")
        s.on("vector", lambda v, c0=c0, po=tt_g % 2: v.tensor_scalar(
            outb[po][:], x1rd[po][:], smc[:, c0 + 2:c0 + 3],
            smc[:, c0 + 1:c0 + 2], ALU.subtract, ALU.mult),
            waits=[("s2ch", scv), last_outb.get(tt_g % 2, ("fold", 0))],
            force=True, inc="zf2")
        ov = s.val("zf2")
        last_z2b[p3] = ("zf2", ov)
        last_x1rd[tt_g % 2] = ("zf2", ov)
        last_gb[(1, p3)] = ("zf2", ov)
        last_stat[p3] = ("zf2", ov)
        last_outb[tt_g % 2] = s.dma("sync", lambda e, tt_g=tt_g, po=tt_g % 2:
                                    e.dma_start(out=y[tt_g], in_=outb[po][:]),
                                    "x1st", tt_g % 2, waits=[("zf2", ov)])

    s.on("sync", lambda e: e.nop(), waits=s.all_dma_totals(), force=True)


def _host_weights(inputs):
    f32 = np.float32
    iw = {}
    iw["wq32"] = np.ascontiguousarray(
        inputs["L0_attn_Wq"].reshape(D, D).reshape(NKC, 128, D), dtype=f32)
    iw["wk32"] = np.ascontiguousarray(
        inputs["L0_attn_Wk"].reshape(D, D).reshape(NKC, 128, D), dtype=f32)
    iw["wv32"] = np.ascontiguousarray(
        inputs["L0_attn_Wv"].reshape(D, D).reshape(NKC, 128, D), dtype=f32)
    iw["wo32"] = np.ascontiguousarray(
        inputs["L0_attn_Wo"].reshape(D, D).reshape(NKC, 128, D), dtype=f32)
    iw["w132"] = np.ascontiguousarray(
        inputs["L0_attn_W1"].reshape(D, MLP).reshape(NKC, 128, MLP), dtype=f32)
    iw["w232"] = np.ascontiguousarray(
        inputs["L0_attn_W2"].reshape(MLP, D).reshape(NF1, 128, D), dtype=f32)
    iw["wr32"] = np.ascontiguousarray(
        inputs["L0_ffn_Wr"].reshape(D, NE).reshape(NKC, 128, NE), dtype=f32)
    wg = np.asarray(inputs["L0_ffn_Wg"], dtype=f32)
    wu = np.asarray(inputs["L0_ffn_Wu"], dtype=f32)
    wd = np.asarray(inputs["L0_ffn_Wd"], dtype=f32)
    iw["wg32"] = np.ascontiguousarray(
        wg.reshape(NE, NKC, 128, NFC, 128).transpose(0, 3, 2, 1, 4).reshape(
            NE, NFC, 128, NKC * 128))
    iw["wu32"] = np.ascontiguousarray(
        wu.reshape(NE, NKC, 128, NFC, 128).transpose(0, 3, 2, 1, 4).reshape(
            NE, NFC, 128, NKC * 128))
    iw["wd32"] = np.ascontiguousarray(wd.reshape(NE, NFC, 128, D))
    iw["ln1s"] = np.ascontiguousarray(
        np.asarray(inputs["L0_attn_ln1_scale"], f32).reshape(NKC, 128).T)
    iw["ln1b"] = np.ascontiguousarray(
        np.asarray(inputs["L0_attn_ln1_bias"], f32).reshape(NKC, 128).T)
    iw["ln2s"] = np.ascontiguousarray(
        np.asarray(inputs["L0_attn_ln2_scale"], f32).reshape(NKC, 128).T)
    iw["ln2b"] = np.ascontiguousarray(
        np.asarray(inputs["L0_attn_ln2_bias"], f32).reshape(NKC, 128).T)
    iw["bqr"] = np.asarray(inputs["L0_attn_bq"], f32).reshape(1, D)
    iw["bkr"] = np.asarray(inputs["L0_attn_bk"], f32).reshape(1, D)
    iw["bvr"] = np.asarray(inputs["L0_attn_bv"], f32).reshape(1, D)
    iw["bor32"] = np.asarray(inputs["L0_attn_bo"], f32).reshape(1, D)
    iw["b1r32"] = np.asarray(inputs["L0_attn_b1"], f32).reshape(1, MLP)
    iw["b2r32"] = np.asarray(inputs["L0_attn_b2"], f32).reshape(1, D)
    iw["identF"] = np.eye(128, dtype=f32)
    iw["identB"] = np.eye(128, dtype=f32).astype(bfd)
    iw["onesrow_f"] = np.ones((1, 128), f32)
    iw["onescol_b"] = np.ones((128, 1), f32).astype(bfd)
    zk = np.zeros((2, 128), f32); zk[1, :] = KEPS
    iw["zk_row"] = zk.astype(bfd)
    selm = np.zeros((NE, 16, 128), f32)
    for c in range(NE):
        selm[c, 2 * c, 0:64] = 1.0
        selm[c, 2 * c + 1, 64:128] = 1.0
    iw["sel"] = selm
    iw["ut128"] = np.triu(np.ones((128, 128), f32), 1).astype(bfd)
    iw["cio8"] = (CAP * np.arange(8, dtype=f32)).reshape(1, 8)
    return iw


def _core_x(inputs, c):
    b, j = c // 2, c % 2
    xb = np.asarray(inputs["L0_attn_inputs"][b], dtype=np.float32)
    own = xb[j * SOWN:(j + 1) * SOWN]
    oth = xb[(1 - j) * SOWN:(2 - j) * SOWN]
    xcat = np.concatenate([own, oth], 0)
    x_all = np.ascontiguousarray(xcat.reshape(NTA, 128, D))
    xT_all = np.ascontiguousarray(xcat.T).reshape(NKC, 128, SALL)
    return {"x_all": x_all, "xT_all": xT_all}


def kernel(**inputs):
    iw = _host_weights(inputs)
    nc, _ = build()
    in_maps = []
    for c in range(8):
        m = dict(iw)
        m.update(_core_x(inputs, c))
        in_maps.append(m)
    res = run_bass_kernel_spmd(nc, in_maps, core_ids=list(range(8)))
    out = np.empty((4, SALL, D), dtype=np.float32)
    for c in range(8):
        b, j = c // 2, c % 2
        out[b, j * SOWN:(j + 1) * SOWN] = res.results[c]["y"].reshape(SOWN, D)
    return out


# version 11
# speedup vs baseline: 1.1734x; 1.1734x over previous
import numpy as np
import concourse.bass as bass
import concourse.mybir as mybir
from concourse.bass import IndirectOffsetOnAxis
from concourse.bass_utils import run_bass_kernel_spmd
import ml_dtypes
from contextlib import ExitStack

F32 = mybir.dt.float32
BF16 = mybir.dt.bfloat16
AF = mybir.ActivationFunctionType
ALU = mybir.AluOpType
AX = mybir.AxisListType
bfd = ml_dtypes.bfloat16

D = 1024
NKC = 8
SOWN = 4096
SALL = 8192
NTA = SALL // 128
NTB = SOWN // 128
NG = 8
GT = 4
MLP = 4096
NF1 = MLP // 128
DFF = 2048
NFC = DFF // 128
NE = 8
NCH = 4
CH = 1024
NT8 = CH // 128
KEPS = 1e-3
LNEPS = 1e-6
PLNEPS = 1e-5
CAP = 1536
NSC = CAP // 512

ENGINES = ("tensor", "vector", "scalar", "gpsimd", "sync")


class Sched:
    def __init__(self):
        self.ops = {e: [] for e in ENGINES}
        self.sems = {}
        self.owner = {}
        self.last_wait = {e: {} for e in ENGINES}
        self.own_inc = {e: {} for e in ENGINES}
        self.streams = {}

    def sem(self, *names):
        for n in names:
            self.sems.setdefault(n, 0)

    def val(self, sname):
        return self.sems[sname]

    def on(self, engine, fn, waits=(), inc=None, n=1, force=False, dma=False):
        pruned = []
        for (sn, v) in waits:
            if v <= 0:
                continue
            if v <= self.last_wait[engine].get(sn, 0):
                continue
            if (not force and self.owner.get(sn, "multi") == engine
                    and v <= self.own_inc[engine].get(sn, 0)):
                continue
            pruned.append((sn, v))
            self.last_wait[engine][sn] = max(self.last_wait[engine].get(sn, 0), v)
        post = None
        if inc is not None:
            self.sems[inc] = self.sems.get(inc, 0) + n
            post = self.sems[inc]
            if dma:
                self.owner[inc] = None
            elif inc not in self.owner:
                self.owner[inc] = engine
            elif self.owner[inc] != engine:
                self.owner[inc] = "multi"
            if not dma:
                self.own_inc[engine][inc] = self.sems[inc]
        self.ops[engine].append((pruned, fn, inc, n))
        return post

    def dma(self, engine, fn, stream, slot=0, waits=()):
        name = f"{stream}@{slot}"
        self.sem(name)
        self.streams.setdefault(stream, set()).add(name)
        v = self.on(engine, fn, waits=waits, inc=name, n=16, dma=True)
        return (name, v)

    def dma_totals(self, stream):
        return [(n, self.sems[n]) for n in sorted(self.streams.get(stream, ()))]

    def all_dma_totals(self):
        out = []
        for st in sorted(self.streams):
            out.extend(self.dma_totals(st))
        return out

    def used_sems(self):
        used = set()
        for e in ENGINES:
            for waits, fn, inc, n in self.ops[e]:
                for (sn, v) in waits:
                    used.add(sn)
                if inc is not None:
                    used.add(inc)
        return used

    def n_ops(self):
        return {e: len(self.ops[e]) for e in ENGINES}

    def emit(self, nc, semhandles):
        with nc.Block() as block:
            for engname in ENGINES:
                if not self.ops[engname]:
                    continue

                def make_body(en):
                    def body(eng):
                        for waits, fn, inc, n in self.ops[en]:
                            for (sn, v) in waits:
                                eng.wait_ge(semhandles[sn], v)
                            ins = fn(eng)
                            if inc is not None:
                                ins.then_inc(semhandles[inc], n)
                    return body

                getattr(block, engname)(make_body(engname))


SEMS = [
    "cst", "ld_kv", "fold", "rowmm", "rowev", "rowasm",
    "axa", "axt", "ssum", "sssq", "sch", "tr", "trc", "kmm", "vmm",
    "epik", "epiv", "kvmm", "bg_ld", "bg_fold", "bg_st", "wold",
    "tev", "tks", "tkc", "lqmm", "lqev", "lqasm", "b1mm", "b1ev", "b1tr", "b1tc",
    "bxt", "bxa", "qmm", "phr", "phv", "dnmm", "rcp", "bcmm", "bcev",
    "atn", "womm", "xrs", "s2ssq", "s2ch", "ln2", "trb", "trbc",
    "w1ld", "h1", "w2ld", "ymm", "zf", "s3ssq", "s3ch", "outv",
    "x1st", "x1tst", "yst", "x1bfc",
    "fxt", "fw", "fwd", "rtmm", "rtev", "rttr", "rtsm", "ce",
    "gu", "hd", "dd", "acc", "acchk", "x1ld", "zf2", "s4ssq", "s4ch", "w2u", "w1rd",
    "w2t",
]

KVC = 0
GK = 512
GV = 1536
SM = 2560
SP1 = 3072


def build(stop_phase=None, dumps=()):
    nc = bass.Bass("TRN2", target_bir_lowering=False)

    def din(name, shape, dt=F32):
        return nc.dram_tensor(name, shape, dt, kind="ExternalInput").ap()

    def dscr(name, shape, dt):
        kind = "ExternalOutput" if name in dumps else "Internal"
        return nc.dram_tensor(name, shape, dt, kind=kind).ap()

    x_all = din("x_all", [NTA, 128, D])
    xT_all = din("xT_all", [NKC, 128, SALL])
    wq32 = din("wq32", [NKC, 128, D]); wk32 = din("wk32", [NKC, 128, D])
    wv32 = din("wv32", [NKC, 128, D]); wo32 = din("wo32", [NKC, 128, D])
    w132 = din("w132", [NKC, 128, MLP])
    w232 = din("w232", [NF1, 128, D])
    wr32 = din("wr32", [NKC, 128, NE])
    wg32 = din("wg32", [NE, NFC, 128, NKC * 128])
    wu32 = din("wu32", [NE, NFC, 128, NKC * 128])
    wd32 = din("wd32", [NE, NFC, 128, D])
    ln1s = din("ln1s", [128, NKC]); ln1b = din("ln1b", [128, NKC])
    ln2s = din("ln2s", [128, NKC]); ln2b = din("ln2b", [128, NKC])
    bqr = din("bqr", [1, D]); bkr = din("bkr", [1, D]); bvr = din("bvr", [1, D])
    bor32 = din("bor32", [1, D]); b1r32 = din("b1r32", [1, MLP]); b2r32 = din("b2r32", [1, D])
    identF = din("identF", [128, 128])
    identB = din("identB", [128, 128], BF16)
    sel = din("sel", [NE, 16, 128])
    onesrow_f = din("onesrow_f", [1, 128])
    onescol_b = din("onescol_b", [128, 1], BF16)
    zk_row = din("zk_row", [2, 128], BF16)
    ut128 = din("ut128", [128, 128], BF16)
    cio8 = din("cio8", [1, 8])

    y = nc.dram_tensor("y", [NTB, 128, D], F32, kind="ExternalOutput").ap()

    w1scr = dscr("w1scr", [NF1, 128, NKC * 128], BF16)
    w2scr = dscr("w2scr", [NF1, 128, D], BF16)
    x1scr = dscr("x1scr", [NTB, 128, D], F32)
    x1Tscr = dscr("x1Tscr", [NKC, 128, SOWN], BF16)
    x1g = dscr("x1g", [NE * CAP, D], BF16)
    yg = dscr("yg", [NE * CAP, D], BF16)

    s = Sched()
    s.sem(*SEMS)

    es = ExitStack()
    ctx = {}
    with es:
        def sb(name, shape, dt=F32):
            h = es.enter_context(nc.sbuf_tensor(name, shape, dt))
            return h

        identF_sb = sb("identF_sb", [128, 128])
        identB_sb = sb("identB_sb", [128, 128], BF16)
        sel_sb = sb("sel_sb", [16, NE * 128])
        zk_sb = sb("zk_sb", [2, 128], BF16)
        onesrow_sb = sb("onesrow_sb", [1, 128])
        onescol_sb = sb("onescol_sb", [128, 1], BF16)
        eps6 = sb("eps6", [128, 1]); eps5 = sb("eps5", [128, 1])
        statB = sb("statB", [2, SOWN], BF16)
        kvsb = sb("kvsb", [128, 512], BF16)
        KS = sb("KS", [128, NKC * 16], BF16)
        Lq = sb("Lq", [2, D], BF16)
        bo_row = sb("bo_row", [1, D]); b2_row = sb("b2_row", [1, D])
        b1p = sb("b1p", [128, NF1])
        ce_sb = sb("ce_sb", [128, NTB * NE])
        wr_bf = sb("wr_bf", [128, NKC * NE], BF16)
        smc = sb("smc", [128, 24])
        ut_sb = sb("ut_sb", [128, 128], BF16)
        cio_sb = sb("cio_sb", [1, 8])

        ps = es.enter_context(nc.psum_tensor("ps", [128, 3584], F32))

        scope_ab = ExitStack()

        def sab(name, shape, dt=F32):
            return scope_ab.enter_context(nc.sbuf_tensor(name, shape, dt))

        wq_bf = sab("wq_bf", [128, NKC * D], BF16)
        wo_bf = sab("wo_bf", [128, NKC * D], BF16)

        scope_a = ExitStack()

        def sba(name, shape, dt=F32):
            return scope_a.enter_context(nc.sbuf_tensor(name, shape, dt))

        wk_bf = sba("wk_bf", [128, NKC * D], BF16)
        wv_bf = sba("wv_bf", [128, NKC * D], BF16)
        stg0 = sba("stg0", [128, MLP])
        stg = [stg0, stg0]
        w1f = [sba(f"w1f{i}", [128, MLP], BF16) for i in range(2)]
        ln1s_sb = sba("ln1s_sb", [128, NKC]); ln1b_sb = sba("ln1b_sb", [128, NKC])
        ln2s_sb = sba("ln2s_sb", [128, NKC]); ln2b_sb = sba("ln2b_sb", [128, NKC])
        rat1 = sba("rat1", [128, NKC], BF16)
        rat2 = sba("rat2", [128, NKC], BF16)
        ratt = sba("ratt", [128, NKC])
        stat_sb = sba("stat_sb", [128, NTA * 4])
        ksr0 = sba("ksr0", [1, 512]); ksr1 = sba("ksr1", [1, 512])
        poolA = sba("poolA", [128, MLP])
        rowf = poolA[0:1, :]
        b1_sb = poolA[32:33, :]
        bk_sb = poolA[64:65, 0:D]
        bv_sb = poolA[64:65, D:2 * D]
        bq_sb = poolA[64:65, 2 * D:3 * D]
        poolB = sba("poolB", [128, MLP], BF16)
        Rk = poolB[0:2, 0:D]
        Rv = poolB[0:2, D:2 * D]
        rowb = poolB[32:33, :]
        rowc = poolB[64:65, :]
        junkA = sba("junkA", [128, D])
        xa = [sba(f"xa{i}", [128, D]) for i in range(3)]
        xtb = [sba(f"xtb{i}", [128, NKC * 128], BF16) for i in range(3)]
        phik = [sba(f"phik{i}", [128, D], BF16) for i in range(2)]
        vsb = [sba(f"vsb{i}", [128, D], BF16) for i in range(2)]
        tk = [sba(f"tk{i}", [128, D]) for i in range(2)]
        trin = [sba(f"trin{i}", [128, 4]) for i in range(2)]
        srowA = [sba(f"srowA{i}", [2, 128], BF16) for i in range(2)]

        for t_, src_ in [(identF_sb, identF), (identB_sb, identB), (zk_sb, zk_row),
                         (onesrow_sb, onesrow_f), (onescol_sb, onescol_b),
                         (ln1s_sb, ln1s), (ln1b_sb, ln1b),
                         (ln2s_sb, ln2s), (ln2b_sb, ln2b),
                         (bo_row, bor32), (b2_row, b2r32),
                         (bk_sb, bkr), (bv_sb, bvr), (bq_sb, bqr), (b1_sb, b1r32),
                         (ut_sb, ut128), (cio_sb, cio8)]:
            s.dma("sync", lambda e, t=t_, sr=src_: e.dma_start(out=t[:], in_=sr),
                  "cst", 0)
        s.dma("sync", lambda e: e.dma_start(
            out=sel_sb[:].rearrange("p (c f) -> p c f", c=NE),
            in_=sel.rearrange("c p f -> p c f")), "cst", 0)
        s.dma("gpsimd", lambda e: e.dma_start(
            out=wr_bf[:].rearrange("p (c f) -> p c f", c=NKC),
            in_=wr32.rearrange("c p f -> p c f")), "gwr", 0)
        CST = ("cst@0", s.val("cst@0"))

        def ratio_chain(dst_bf, lns_, lnb_, c0, c1):
            s.on("vector", lambda v: v.tensor_scalar(smc[:, c0:c0 + NKC], lns_[:],
                                                     0.0, None, ALU.is_equal),
                 waits=[CST], inc="fold")
            fv = s.val("fold")
            s.on("vector", lambda v: v.tensor_add(smc[:, c1:c1 + NKC], lns_[:],
                                                  smc[:, c0:c0 + NKC]),
                 waits=[("fold", fv)], force=True, inc="fold")
            fv = s.val("fold")
            s.on("vector", lambda v: v.reciprocal(ratt[:], smc[:, c1:c1 + NKC]),
                 waits=[("fold", fv)], force=True, inc="fold")
            fv = s.val("fold")
            s.on("vector", lambda v: v.tensor_mul(dst_bf[:], lnb_[:], ratt[:]),
                 waits=[("fold", fv)], force=True, inc="fold")

        ratio_chain(rat1, ln1s_sb, ln1b_sb, 0, 8)
        ratio_chain(rat2, ln2s_sb, ln2b_sb, 0, 8)
        s.on("vector", lambda v: v.memset(eps6[:], LNEPS))
        s.on("vector", lambda v: v.memset(eps5[:], PLNEPS))
        s.on("vector", lambda v: v.memset(KS[:], 0.0), inc="fold")
        RATF = s.val("fold")

        FOLD_K = None
        for wsrc, wdst, eng, bufs, sl0 in [(wk32, wk_bf, "sync", stg, 0),
                                           (wv32, wv_bf, "gpsimd", w1f, 2)]:
            base = s.val("fold")
            for kc in range(NKC):
                pbuf = bufs[kc % 2]
                ldv = s.dma(eng, lambda e, wsrc=wsrc, kc=kc, pbuf=pbuf:
                            e.dma_start(out=pbuf[:, 0:D], in_=wsrc[kc]),
                            "ldkv", sl0 + kc % 2,
                            waits=[("fold", 0 if kc < 2 else base + kc - 1)])
                s.on("vector", lambda v, wdst=wdst, kc=kc, pbuf=pbuf:
                     v.tensor_scalar(wdst[:, kc * D:(kc + 1) * D], pbuf[:, 0:D],
                                     ln1s_sb[:, kc:kc + 1], None, ALU.mult),
                     waits=[ldv, CST], inc="fold")
            if FOLD_K is None:
                FOLD_K = s.val("fold")
        FOLD_KV = s.val("fold")

        def build_R(wdst, brow_sb, Rdst, ratio_sb, neg0=False, extra_w=(),
                    fold_v=None):
            if fold_v is None:
                fold_v = FOLD_KV
            asm0 = s.dma_totals("rasm")
            for kc in range(NKC):
                for h in range(2):
                    s.on("tensor", lambda t, kc=kc, h=h, wdst=wdst: t.matmul(
                        ps[0:1, SM + h * 512:SM + (h + 1) * 512],
                        onescol_sb[:],
                        wdst[:, kc * D + h * 512:kc * D + (h + 1) * 512],
                        start=(kc == 0), stop=(kc == NKC - 1), skip_group_check=True),
                        waits=([("fold", fold_v)] + asm0 + list(extra_w))
                        if (kc == 0 and h == 0) else (), force=True)
            for kc in range(NKC):
                for h in range(2):
                    s.on("tensor", lambda t, kc=kc, h=h, wdst=wdst, rs=ratio_sb: t.matmul(
                        ps[32:33, SM + h * 512:SM + (h + 1) * 512],
                        rs[:, kc:kc + 1],
                        wdst[:, kc * D + h * 512:kc * D + (h + 1) * 512],
                        start=(kc == 0), stop=(kc == NKC - 1), skip_group_check=True,
                        tile_position=(0, 32)),
                        inc="rowmm" if (kc == NKC - 1 and h == 1) else None)
            mmv = s.val("rowmm")
            if neg0:
                s.on("vector", lambda v: v.tensor_scalar(
                    rowb[:, 0:D], ps[0:1, SM:SM + D], -1.0, None, ALU.mult),
                    waits=[("rowmm", mmv)] + asm0, force=True)
            else:
                s.on("vector", lambda v: v.tensor_copy(rowb[:, 0:D], ps[0:1, SM:SM + D]),
                     waits=[("rowmm", mmv)] + asm0, force=True)
            s.on("vector", lambda v, brow_sb=brow_sb: v.tensor_add(
                rowc[:, 0:D], ps[32:33, SM:SM + D], brow_sb[:]), inc="rowev")
            ev = s.val("rowev")
            s.dma("gpsimd", lambda e, Rdst=Rdst: e.dma_start(out=Rdst[0:1, :],
                                                             in_=rowb[:, 0:D]),
                  "rasm", 0, waits=[("rowev", ev)])
            s.dma("gpsimd", lambda e, Rdst=Rdst: e.dma_start(out=Rdst[1:2, :],
                                                             in_=rowc[:, 0:D]),
                  "rasm", 1)
            return s.dma_totals("rasm")

        build_R(wk_bf, bk_sb, Rk, rat1, fold_v=FOLD_K)
        RASM = build_R(wv_bf, bv_sb, Rv, rat1)

        s.dma("gpsimd", lambda e: e.dma_start(
            out=wo_bf[:].rearrange("p (c f) -> p c f", c=NKC),
            in_=wo32.rearrange("c p f -> p c f")), "gwo", 0)

        bg_jobs = []
        last_bgst = {}

        def bg_fold_job(idx, wsrc, kc, width, kind):
            def emitjob():
                pbuf = stg[idx % 2]
                ldv = s.dma("gpsimd",
                            lambda e: e.dma_start(out=pbuf[:, 0:width], in_=wsrc[kc]),
                            "bgld", idx % 2,
                            waits=[("bg_fold", s.val("bg_fold")), ("fold", FOLD_KV)])
                if kind == "wq":
                    s.on("vector", lambda v: v.tensor_scalar(
                        wq_bf[:, kc * D:(kc + 1) * D], pbuf[:, 0:D],
                        ln1s_sb[:, kc:kc + 1], None, ALU.mult),
                        waits=[ldv], inc="bg_fold")
                else:
                    wbuf = w1f[idx % 2]
                    s.on("vector", lambda v: v.tensor_scalar(
                        wbuf[:, 0:MLP], pbuf[:, 0:MLP],
                        ln2s_sb[:, kc:kc + 1], None, ALU.mult),
                        waits=[ldv, last_bgst.get(idx % 2, ("fold", 0))],
                        inc="bg_fold")
                    fv = s.val("bg_fold")
                    last_bgst[idx % 2] = s.dma("sync", lambda e: e.dma_start(
                        out=w1scr[:, :, kc * 128:(kc + 1) * 128].rearrange(
                            "f p c -> p f c"),
                        in_=wbuf[:, 0:MLP].rearrange("p (f c) -> p f c", c=128)),
                        "bgst", idx % 2, waits=[("bg_fold", fv)])
            return emitjob

        jidx = 0
        for kc in range(NKC):
            bg_jobs.append(bg_fold_job(jidx, wq32, kc, D, "wq")); jidx += 1
        for kc in range(NKC):
            bg_jobs.append(bg_fold_job(jidx, w132, kc, MLP, "w1")); jidx += 1

        s.sem("bgst@0", "bgst@1")
        W1ST_FINAL = [("bgst@0", 64), ("bgst@1", 64)]

        def b1_job(f1):
            def emit():
                ldv = s.dma("sync", lambda e, f1=f1: e.dma_start(
                    out=w1f[f1 % 2][:, 0:NKC * 128], in_=w1scr[f1]),
                    "w1rd", f1 % 2,
                    waits=[("b1mm", max(0, f1 - 1))] + W1ST_FINAL)
                for kc in range(NKC):
                    s.on("tensor", lambda t, f1=f1, kc=kc: t.matmul(
                        ps[0:1, SP1 + (f1 % 4) * 128:SP1 + (f1 % 4) * 128 + 128],
                        rat2[:, kc:kc + 1],
                        w1f[f1 % 2][:, kc * 128:(kc + 1) * 128],
                        start=(kc == 0), stop=(kc == NKC - 1),
                        skip_group_check=True),
                        waits=([ldv, ("b1ev", f1)]) if kc == 0 else (),
                        force=True,
                        inc="b1mm" if kc == NKC - 1 else None)
                mmv = s.val("b1mm")
                s.on("vector", lambda v, f1=f1: v.tensor_add(
                    rowf[:, f1 * 128:(f1 + 1) * 128],
                    ps[0:1, SP1 + (f1 % 4) * 128:SP1 + (f1 % 4) * 128 + 128],
                    b1_sb[:, f1 * 128:(f1 + 1) * 128]),
                    waits=[("b1mm", mmv)], force=True, inc="b1ev")
            return emit
        b1_jobs = [b1_job(f1) for f1 in range(NF1)]

        free_gk = ("fold", 0)
        free_gv = ("fold", 0)
        free_sm_dve = ("fold", 0)
        last_phik = {}; last_vsb = {}; last_xa = {}; last_xtb = {}
        last_trin = {}; last_srowA_use = {}; last_tk = {}

        for t in range(NTA):
            own = t < NTB
            axa_v = s.dma("sync", lambda e, t=t: e.dma_start(
                out=xa[t % 3][:], in_=x_all[t]), "axa", t % 3,
                waits=list(last_xa.get(t % 3, ())))
            axt_v = s.dma("gpsimd", lambda e, t=t: e.dma_start(
                out=xtb[t % 3][:].rearrange("p (c f) -> p c f", c=NKC),
                in_=xT_all[:, :, t * 128:(t + 1) * 128].rearrange("c p f -> p c f")),
                "axt", t % 3, waits=[last_xtb.get(t % 3, ("fold", 0))])
            if t % 3 == 2 and bg_jobs:
                bg_jobs.pop(0)()
            if t % 2 == 0 and t // 2 < NF1:
                s.dma("gpsimd", lambda e, f1c=t // 2: e.dma_start(
                    out=w2scr[f1c], in_=w232[f1c]), "w2c", 0)
            if t >= 48:
                for _ in range(2):
                    if b1_jobs:
                        b1_jobs.pop(0)()

            s.on("vector", lambda v, t=t: v.tensor_scalar(
                junkA[:], xa[t % 3][:], 1.0, None, ALU.mult, ALU.add,
                accum_out=stat_sb[:, 4 * t + 3:4 * t + 4]),
                waits=[axa_v], inc="ssum")
            ssum_v = s.val("ssum")
            s.on("scalar", lambda sc, t=t: sc.activation(
                tk[t % 2][:], xa[t % 3][:], AF.Square,
                accum_out=smc[:, 12 + t % 2:13 + t % 2]),
                waits=[axa_v, last_tk.get(t % 2, ("fold", 0))],
                inc="sssq")
            sssq_v = s.val("sssq")
            s.on("vector", lambda v, t=t: v.tensor_scalar(
                stat_sb[:, 4 * t:4 * t + 1], stat_sb[:, 4 * t + 3:4 * t + 4],
                1.0 / D, None, ALU.mult),
                waits=[("ssum", ssum_v)], force=True, inc="sch")
            s.on("vector", lambda v, t=t: v.tensor_mul(
                smc[:, 14 + t % 2:15 + t % 2], stat_sb[:, 4 * t:4 * t + 1],
                stat_sb[:, 4 * t:4 * t + 1]),
                waits=[("sch", s.val("sch"))], force=True, inc="sch")
            s.on("vector", lambda v, t=t: v.scalar_tensor_tensor(
                stat_sb[:, 4 * t + 3:4 * t + 4], smc[:, 12 + t % 2:13 + t % 2],
                1.0 / D, smc[:, 14 + t % 2:15 + t % 2], ALU.mult, ALU.subtract),
                waits=[("sssq", sssq_v), ("sch", s.val("sch"))], force=True,
                inc="sch")
            schv = s.val("sch")
            s.on("scalar", lambda sc, t=t: sc.activation(
                smc[:, 16 + t % 2:17 + t % 2], stat_sb[:, 4 * t + 3:4 * t + 4],
                AF.Ln, bias=eps6[:]),
                waits=[("sch", schv)], inc="sch")
            s.on("scalar", lambda sc, t=t: sc.activation(
                stat_sb[:, 4 * t + 1:4 * t + 2], smc[:, 16 + t % 2:17 + t % 2],
                AF.Exp, scale=-0.5),
                waits=[("sch", s.val("sch"))], force=True, inc="sch")
            s.on("scalar", lambda sc, t=t: sc.activation(
                stat_sb[:, 4 * t + 2:4 * t + 3], smc[:, 16 + t % 2:17 + t % 2],
                AF.Exp, scale=0.5), inc="sch")
            schv2 = s.val("sch")
            s.on("vector", lambda v, t=t: v.tensor_scalar(
                trin[t % 2][:, 0:1], stat_sb[:, 4 * t:4 * t + 1], -1.0, None,
                ALU.mult),
                waits=[("sch", schv2), last_trin.get(t % 2, ("fold", 0))],
                force=True)
            s.on("vector", lambda v, t=t: v.tensor_copy(
                trin[t % 2][:, 1:2], stat_sb[:, 4 * t + 2:4 * t + 3]))
            if own:
                s.on("vector", lambda v, t=t: v.tensor_copy(
                    trin[t % 2][:, 2:3], stat_sb[:, 4 * t:4 * t + 1]))
                s.on("vector", lambda v, t=t: v.tensor_copy(
                    trin[t % 2][:, 3:4], stat_sb[:, 4 * t + 2:4 * t + 3]),
                    inc="sch")
            else:
                s.on("vector", lambda v: v.memset(smc[:, 20:21], 0.0),
                     inc="sch")
            trin_ready = s.val("sch")

            s.on("tensor", lambda tt, t=t: tt.transpose(
                ps[0:2, SM:SM + 128], trin[t % 2][:, 0:2],
                identF_sb[:]),
                waits=[("sch", trin_ready), free_sm_dve] + RASM,
                force=True, inc="tr")
            if own:
                s.on("tensor", lambda tt, t=t: tt.transpose(
                    ps[0:2, SM + 128:SM + 256], trin[t % 2][:, 2:4],
                    identF_sb[:]), inc="tr")
            tr_v = s.val("tr")

            kw = [axt_v, ("fold", FOLD_K), free_gk]
            for kc in range(NKC):
                for h in range(2):
                    s.on("tensor", lambda tt, t=t, kc=kc, h=h: tt.matmul(
                        ps[:, GK + h * 512:GK + (h + 1) * 512],
                        xtb[t % 3][:, kc * 128:(kc + 1) * 128],
                        wk_bf[:, kc * D + h * 512:kc * D + (h + 1) * 512],
                        start=(kc == 0), stop=False, skip_group_check=True),
                        waits=kw if (kc == 0 and h == 0) else (), force=True)
            vw = [free_gv, ("fold", FOLD_KV)]
            for kc in range(NKC):
                for h in range(2):
                    s.on("tensor", lambda tt, t=t, kc=kc, h=h: tt.matmul(
                        ps[:, GV + h * 512:GV + (h + 1) * 512],
                        xtb[t % 3][:, kc * 128:(kc + 1) * 128],
                        wv_bf[:, kc * D + h * 512:kc * D + (h + 1) * 512],
                        start=(kc == 0), stop=False, skip_group_check=True),
                        waits=vw if (kc == 0 and h == 0) else (), force=True)

            s.on("vector", lambda v, t=t: v.tensor_copy(
                srowA[t % 2][:], ps[0:2, SM:SM + 128]),
                waits=[("tr", tr_v), last_srowA_use.get(t % 2, ("fold", 0))],
                force=True)
            if own:
                s.on("vector", lambda v, t=t: v.tensor_copy(
                    statB[:, t * 128:(t + 1) * 128],
                    ps[0:2, SM + 128:SM + 256]), inc="trc")
            else:
                s.on("vector", lambda v: v.memset(smc[:, 20:21], 0.0),
                     inc="trc")
            trc_v = s.val("trc")
            free_sm_dve = ("trc", trc_v)

            for dst, R, incn in ((GK, Rk, "kmm"), (GV, Rv, "vmm")):
                for h in range(2):
                    s.on("tensor", lambda tt, t=t, dst=dst, R=R, h=h: tt.matmul(
                        ps[:, dst + h * 512:dst + (h + 1) * 512],
                        srowA[t % 2][:], R[:, h * 512:(h + 1) * 512],
                        start=False, stop=True, skip_group_check=True),
                        waits=[("trc", trc_v)] if (h == 0 and dst == GK) else (),
                        force=True, inc=incn if h == 1 else None)
            kmm_v = s.val("kmm"); vmm_v = s.val("vmm")
            last_srowA_use[t % 2] = ("vmm", vmm_v)
            last_trin[t % 2] = ("tr", tr_v)

            s.on("scalar", lambda sc, t=t: sc.activation(
                tk[t % 2][:], ps[:, GK:GK + D], AF.Relu,
                scale=stat_sb[:, 4 * t + 1:4 * t + 2]),
                waits=[("kmm", kmm_v)], inc="epik")
            s.on("vector", lambda v, t=t: v.tensor_scalar(
                phik[t % 2][:], tk[t % 2][:], KEPS, None, ALU.add),
                waits=[("epik", s.val("epik")), last_phik.get(t % 2, ("fold", 0))],
                force=True, inc="epik")
            epik_v = s.val("epik")
            free_gk = ("epik", epik_v)
            last_tk[t % 2] = ("epik", epik_v)
            s.on("vector", lambda v, t=t: v.tensor_scalar(
                vsb[t % 2][:], ps[:, GV:GV + D],
                stat_sb[:, 4 * t + 1:4 * t + 2], None, ALU.mult),
                waits=[("vmm", vmm_v), last_vsb.get(t % 2, ("fold", 0))],
                force=True, inc="epiv")
            epiv_v = s.val("epiv")
            free_gv = ("epiv", epiv_v)
            last_xa[t % 3] = [("sssq", sssq_v), ("ssum", ssum_v)]
            last_xtb[t % 3] = ("vmm", vmm_v)

            for p in range(8):
                s.on("tensor", lambda tt, t=t, p=p: tt.matmul(
                    ps[0:64, KVC + p * 64:KVC + (p + 1) * 64],
                    phik[t % 2][:, p * 128:p * 128 + 64],
                    vsb[t % 2][:, (2 * p) * 64:(2 * p + 1) * 64],
                    start=(t == 0 and p == 0), stop=(t == NTA - 1),
                    skip_group_check=True),
                    waits=[("epik", epik_v), ("epiv", epiv_v)] if p == 0 else (),
                    force=True)
                s.on("tensor", lambda tt, t=t, p=p: tt.matmul(
                    ps[64:128, KVC + p * 64:KVC + (p + 1) * 64],
                    phik[t % 2][:, p * 128 + 64:(p + 1) * 128],
                    vsb[t % 2][:, (2 * p + 1) * 64:(2 * p + 2) * 64],
                    start=(t == 0 and p == 0), stop=(t == NTA - 1),
                    skip_group_check=True, tile_position=(0, 64)))
            s.on("tensor", lambda tt, t=t: tt.matmul(
                ps[32:33, SM:SM + 512], onescol_sb[:], phik[t % 2][:, 0:512],
                start=(t == 0), stop=(t == NTA - 1), skip_group_check=True,
                tile_position=(0, 32)))
            s.on("tensor", lambda tt, t=t: tt.matmul(
                ps[64:65, SM:SM + 512], onescol_sb[:], phik[t % 2][:, 512:1024],
                start=(t == 0), stop=(t == NTA - 1), skip_group_check=True,
                tile_position=(0, 64)), inc="kvmm")
            kv_done = s.val("kvmm")
            last_phik[t % 2] = ("kvmm", kv_done)
            last_vsb[t % 2] = ("kvmm", kv_done)

        while bg_jobs:
            bg_jobs.pop(0)()
        while b1_jobs:
            b1_jobs.pop(0)()

        KV_ALL = s.val("kvmm")
        s.on("vector", lambda v: v.tensor_copy(kvsb[:], ps[:, KVC:KVC + 512]),
             waits=[("kvmm", KV_ALL)], force=True)
        s.on("vector", lambda v: v.tensor_copy(ksr0[:], ps[32:33, SM:SM + 512]))
        s.on("vector", lambda v: v.tensor_copy(ksr1[:], ps[64:65, SM:SM + 512]),
             inc="tev")
        TEV = s.val("tev")
        for c in range(NKC):
            src = ksr0 if c < 4 else ksr1
            off = (c % 4) * 128
            pcol = GK + c * 64
            s.on("tensor", lambda tt, c=c, src=src, off=off, pcol=pcol: tt.transpose(
                ps[0:128, pcol:pcol + 1], src[:, off:off + 128],
                identF_sb[0:1, 0:1]),
                waits=[("tev", TEV)] if c == 0 else (), force=True, inc="tks")
        TKS = s.val("tks")
        for c in range(NKC):
            pcol = GK + c * 64
            s.on("vector", lambda v, c=c, pcol=pcol: v.tensor_copy(
                KS[0:64, c * 16 + 2 * c:c * 16 + 2 * c + 1], ps[0:64, pcol:pcol + 1]),
                waits=[("tks", TKS)], force=True)
            s.on("vector", lambda v, c=c, pcol=pcol: v.tensor_copy(
                KS[64:128, c * 16 + 2 * c + 1:c * 16 + 2 * c + 2],
                ps[64:128, pcol:pcol + 1]),
                inc="tkc")
        TKC = s.val("tkc")

        BGF = s.val("bg_fold")
        RASM2 = build_R(wq_bf, bq_sb, Lq, rat1, neg0=True,
                        extra_w=[("bg_fold", BGF), ("tkc", TKC)])

        W1ST = s.dma_totals("bgst")
        B1EV = s.val("b1ev")
        for f1 in range(NF1):
            s.on("tensor", lambda tt, f1=f1: tt.transpose(
                ps[0:128, GK + f1 * 32:GK + f1 * 32 + 1],
                rowf[:, f1 * 128:(f1 + 1) * 128], identF_sb[0:1, 0:1]),
                waits=([("b1ev", B1EV), ("tkc", TKC)] if f1 == 0 else ()),
                force=True, inc="b1tr" if f1 == NF1 - 1 else None)
        B1TR = s.val("b1tr")
        s.on("vector", lambda v: v.tensor_copy(
            b1p[:, 0:NF1], ps[0:128, GK:GK + NF1 * 32:32]),
            waits=[("b1tr", B1TR)], force=True, inc="b1tc")
        B1TC = s.val("b1tc")
        W2CASTS = s.dma_totals("w2c")

        BARB = ([("kvmm", KV_ALL), ("sssq", s.val("sssq")),
                 ("epik", s.val("epik")), ("epiv", s.val("epiv")),
                 ("b1mm", s.val("b1mm")), ("b1ev", B1EV), ("b1tc", B1TC),
                 ("tkc", TKC), ("tr", s.val("tr")), ("vmm", s.val("vmm"))]
                + RASM2 + W1ST + s.dma_totals("gwo") + s.dma_totals("gwr"))

        ctx.update(locals())
        if stop_phase == "A":
            dbgs = [("dbg_kv", kvsb, [128, 512], BF16),
                    ("dbg_KS", KS, [128, NKC * 16], BF16),
                    ("dbg_statB", statB, [2, SOWN], BF16),
                    ("dbg_stat", stat_sb, [128, NTA * 4], F32),
                    ("dbg_Rk", Rk, [2, D], BF16),
                    ("dbg_Rv", Rv, [2, D], BF16),
                    ("dbg_Lq", Lq, [2, D], BF16),
                    ("dbg_b1p", b1p, [128, NF1], F32),
                    ("dbg_wq", wq_bf, [128, NKC * D], BF16),
                    ("dbg_phik", phik[1], [128, D], BF16),
                    ("dbg_vsb", vsb[1], [128, D], BF16)]
            for name, sbt, shape, dt in dbgs:
                dout = nc.dram_tensor(name, shape, dt, kind="ExternalOutput").ap()
                s.dma("sync", lambda e, dout=dout, sbt=sbt: e.dma_start(
                    out=dout, in_=sbt[:]), "dbgst", 0, waits=BARB)
            s.on("sync", lambda e: e.nop(),
                 waits=BARB + s.all_dma_totals(), force=True)
            semh = {name: es.enter_context(nc.semaphore(name.replace("@", "_")))
                    for name in s.used_sems()}
            s.emit(nc, semh)
            scope_a.close(); scope_ab.close()
            return nc, s

        scope_a.close()
        ctx["scope_ab"] = scope_ab
        build_phase_b(nc, s, ctx)
        if stop_phase == "B":
            s.on("sync", lambda e: e.nop(),
                 waits=ctx["BARF"] + s.all_dma_totals(), force=True)
            semh = {name: es.enter_context(nc.semaphore(name.replace("@", "_")))
                    for name in s.used_sems()}
            s.emit(nc, semh)
            ctx["scope_b"].close()
            scope_ab.close()
            return nc, s
        build_ffn(nc, s, ctx)
        semh = {name: es.enter_context(nc.semaphore(name.replace("@", "_")))
                    for name in s.used_sems()}
        s.emit(nc, semh)
        ctx["scope_f"].close()
        scope_ab2 = None
    return nc, s


def build_phase_b(nc, s, ctx):
    ps = ctx["ps"]; es = ctx["es"]
    identF_sb = ctx["identF_sb"]; identB_sb = ctx["identB_sb"]
    sel_sb = ctx["sel_sb"]; zk_sb = ctx["zk_sb"]
    onesrow_sb = ctx["onesrow_sb"]; statB = ctx["statB"]; kvsb = ctx["kvsb"]
    KS = ctx["KS"]; Lq = ctx["Lq"]; bo_row = ctx["bo_row"]; b2_row = ctx["b2_row"]
    b1p = ctx["b1p"]; eps6 = ctx["eps6"]; eps5 = ctx["eps5"]
    wq_bf = ctx["wq_bf"]; wo_bf = ctx["wo_bf"]
    xT_all = ctx["xT_all"]; x_all = ctx["x_all"]
    w1scr = ctx["w1scr"]; w2scr = ctx["w2scr"]; W2CASTS = ctx["W2CASTS"]
    x1scr = ctx["x1scr"]; x1Tscr = ctx["x1Tscr"]
    BARB = ctx["BARB"]; smc = ctx["smc"]

    scope_b = ExitStack()
    ctx["scope_b"] = scope_b

    def sbb(name, shape, dt=F32):
        return scope_b.enter_context(nc.sbuf_tensor(name, shape, dt))

    xtbg = [sbb("xtbg0", [128, NKC * 512], BF16)] * 2
    xaB = [sbb(f"xaB{i}", [128, D]) for i in range(4)]
    tq = [sbb(f"tq{i}", [128, 512], BF16) for i in range(2)]
    phtil = sbb("phtil", [128, NKC * 512], BF16)
    attnT = sbb("attnT", [128, NKC * 512], BF16)
    epsbc = sbb("epsbc", [128, 512], BF16)
    recd = sbb("recd", [16, 512])
    bcsb = [sbb(f"bcsb{i}", [128, 512], BF16) for i in range(2)]
    xresB = [sbb(f"xresB{i}", [128, D]) for i in range(4)]
    ln2xb = [sbb(f"ln2xb{i}", [128, D], BF16) for i in range(2)]
    ln2xT = sbb("ln2xT", [128, NKC * 512], BF16)
    h1T = sbb("h1T", [128, NF1 * 512], BF16)
    w1r = [sbb(f"w1r{i}", [128, NKC * 128], BF16) for i in range(3)]
    w2r = [sbb(f"w2r{i}", [128, 512], BF16) for i in range(4)]
    x1f = [sbb(f"x1f{i}", [128, D]) for i in range(2)]
    x1bfB = [sbb(f"x1bfB{i}", [128, D], BF16) for i in range(4)]
    x1tb = [sbb(f"x1tb{i}", [128, NKC * 128], BF16) for i in range(2)]
    zbuf = [sbb(f"zbuf{i}", [128, D]) for i in range(2)]
    junkB = sbb("junkB", [128, D])
    statC = sbb("statC", [128, 16])
    psb = scope_b.enter_context(nc.psum_tensor("psb", [128, 1024], BF16))
    ctx["psb"] = psb

    singles = [0, SP1]
    pairs = [512, 1536]
    free_single = {0: BARB[0], SP1: ("b1mm", s.val("b1mm"))}
    free_pair = {512: ("b1tc", s.val("b1tc")), 1536: ("b1tc", s.val("b1tc"))}
    free_den = ("tev", s.val("tev"))
    free_psb = ("fold", 0)

    last_xtbg = {}; last_xaB = {}; last_tq = {}; last_w1r = {}; last_w2r = {}
    last_ln2xb = {}; last_x1f = {}; last_x1bf = {}; last_zbuf = {}
    last_x1tb = {}; last_bcsb = {}
    last_attnT = ("fold", 0)
    prev_atn = ("fold", 0)
    pend_x1 = []

    def make_x1trans(w, t, x1bv):
        def emit():
            nonlocal free_psb
            for j in range(NKC):
                s.on("tensor", lambda tt, w=w, j=j: tt.transpose(
                    psb[:, j * 128:(j + 1) * 128],
                    x1bfB[w][:, j * 128:(j + 1) * 128], identB_sb[:]),
                    waits=[("x1bfc", x1bv), free_psb] if j == 0 else (),
                    force=True, inc="trb" if j == NKC - 1 else None)
            trbv = s.val("trb")
            last_x1bf[w] = ("trb", trbv)
            for j in range(NKC):
                s.on("vector", lambda v, w=w, j=j: v.tensor_copy(
                    x1tb[w % 2][:, j * 128:(j + 1) * 128],
                    psb[:, j * 128:(j + 1) * 128]),
                    waits=([("trb", trbv), last_x1tb.get(w % 2, ("fold", 0))])
                    if j == 0 else (),
                    force=True, inc="trbc" if j == NKC - 1 else None)
            trbcv = s.val("trbc")
            free_psb = ("trbc", trbcv)
            last_x1tb[w % 2] = s.dma("sync", lambda e, t=t, w=w: e.dma_start(
                out=x1Tscr[:, :, t * 128:(t + 1) * 128].rearrange("c p f -> p c f"),
                in_=x1tb[w % 2][:].rearrange("p (c f) -> p c f", c=NKC)),
                "x1tst", w % 2, waits=[("trbc", trbcv)])
        return emit
    prev_h1 = 0
    barrier = list(BARB)

    for g in range(NG):
        tw0 = 512 * g
        sing_i = [0]; pair_i = [0]

        def nsingle(extra=()):
            col = singles[sing_i[0] % 2]
            sing_i[0] += 1
            return col, [free_single[col]] + list(extra)

        def npair(extra=()):
            col = pairs[pair_i[0] % 2]
            pair_i[0] += 1
            return col, [free_pair[col]] + list(extra)

        bxt_v = s.dma("gpsimd", lambda e, g=g, tw0=tw0: e.dma_start(
            out=xtbg[g % 2][:].rearrange("p (c f) -> p c f", c=NKC),
            in_=xT_all[:, :, tw0:tw0 + 512].rearrange("c p f -> p c f")),
            "bxt", g % 2, waits=barrier + [last_xtbg.get(g % 2, ("fold", 0))])
        bxa_w = {}
        for w in range(GT):
            t = 4 * g + w
            bxa_w[w] = s.dma("sync", lambda e, t=t, w=w: e.dma_start(
                out=xaB[w][:], in_=x_all[t]),
                "bxa", w, waits=barrier + [last_xaB.get(w, ("fold", 0))])

        col_e, we = nsingle()
        s.on("tensor", lambda tt, tw0=tw0, col_e=col_e: tt.matmul(
            ps[:, col_e:col_e + 512], zk_sb[:], statB[:, tw0:tw0 + 512],
            start=True, stop=True, skip_group_check=True),
            waits=barrier + we, inc="qmm")
        s.on("vector", lambda v, col_e=col_e: v.tensor_copy(
            epsbc[:], ps[:, col_e:col_e + 512]),
            waits=barrier + [("qmm", s.val("qmm"))], force=True, inc="phv")
        epsbc_v = s.val("phv")
        free_single[col_e] = ("phv", epsbc_v)

        for fc in range(NKC):
            col, qw = nsingle([bxt_v])
            for kc in range(NKC):
                s.on("tensor", lambda tt, g=g, fc=fc, kc=kc, col=col: tt.matmul(
                    ps[:, col:col + 512],
                    wq_bf[:, kc * D + fc * 128:kc * D + (fc + 1) * 128],
                    xtbg[g % 2][:, kc * 512:(kc + 1) * 512],
                    start=(kc == 0), stop=False, skip_group_check=True),
                    waits=qw if kc == 0 else (), force=True)
            s.on("tensor", lambda tt, tw0=tw0, fc=fc, col=col: tt.matmul(
                ps[:, col:col + 512], Lq[:, fc * 128:(fc + 1) * 128],
                statB[:, tw0:tw0 + 512],
                start=False, stop=True, skip_group_check=True), inc="qmm")
            qv = s.val("qmm")
            s.on("scalar", lambda sc, fc=fc, col=col: sc.activation(
                tq[fc % 2][:], ps[:, col:col + 512], AF.Relu),
                waits=[("qmm", qv), last_tq.get(fc % 2, ("fold", 0))], force=True,
                inc="phr")
            prv = s.val("phr")
            s.on("vector", lambda v, fc=fc: v.tensor_add(
                phtil[:, fc * 512:(fc + 1) * 512], tq[fc % 2][:], epsbc[:]),
                waits=[("phr", prv), ("phv", epsbc_v), prev_atn], force=True,
                inc="phv")
            pv = s.val("phv")
            free_single[col] = ("phv", pv)
            last_tq[fc % 2] = ("phv", pv)
        PHV = s.val("phv")
        last_xtbg[g % 2] = ("qmm", s.val("qmm"))
        while pend_x1:
            pend_x1.pop(0)()

        for c in range(NKC):
            s.on("tensor", lambda tt, c=c: tt.matmul(
                ps[0:16, SM:SM + 512], KS[:, c * 16:(c + 1) * 16],
                phtil[:, c * 512:(c + 1) * 512],
                start=(c == 0), stop=(c == NKC - 1), skip_group_check=True),
                waits=[("phv", PHV), free_den] if c == 0 else (), force=True,
                inc="dnmm" if c == NKC - 1 else None)
        dnv = s.val("dnmm")
        s.on("vector", lambda v: v.reciprocal(recd[:], ps[0:16, SM:SM + 512]),
             waits=[("dnmm", dnv)], force=True, inc="rcp")
        rcpv = s.val("rcp")
        free_den = ("rcp", rcpv)

        for c in range(NKC):
            col, nw = npair([("phv", PHV)])
            s.on("tensor", lambda tt, c=c, col=col: tt.matmul(
                ps[0:64, col:col + 512], kvsb[0:64, c * 64:(c + 1) * 64],
                phtil[0:64, c * 512:(c + 1) * 512],
                start=True, stop=True, skip_group_check=True),
                waits=nw, force=True)
            s.on("tensor", lambda tt, c=c, col=col: tt.matmul(
                ps[64:128, col:col + 512], kvsb[64:128, c * 64:(c + 1) * 64],
                phtil[64:128, c * 512:(c + 1) * 512],
                start=True, stop=True, skip_group_check=True,
                tile_position=(64, 64)))
            s.on("tensor", lambda tt, c=c, col=col: tt.matmul(
                ps[:, col + 512:col + 1024], sel_sb[:, c * 128:(c + 1) * 128],
                recd[:], start=True, stop=True, skip_group_check=True),
                waits=[("rcp", rcpv)], force=True, inc="bcmm")
            bcv = s.val("bcmm")
            s.on("vector", lambda v, c=c, col=col: v.tensor_copy(
                bcsb[c % 2][:], ps[:, col + 512:col + 1024]),
                waits=[("bcmm", bcv), last_bcsb.get(c % 2, ("fold", 0))],
                force=True, inc="bcev")
            s.on("vector", lambda v, c=c, col=col: v.tensor_mul(
                attnT[:, c * 512:(c + 1) * 512], ps[:, col:col + 512],
                bcsb[c % 2][:]),
                waits=[("bcev", s.val("bcev"))] + ([last_attnT] if c == 0 else []),
                force=True, inc="atn")
            av = s.val("atn")
            free_pair[col] = ("atn", av)
            last_bcsb[c % 2] = ("atn", av)
        ATN = s.val("atn")
        prev_atn = ("atn", ATN)

        ln2_done = {}

        def emit_trb(w):
            nonlocal free_psb
            lnv = ln2_done[w]
            for j in range(NKC):
                s.on("tensor", lambda tt, w=w, j=j: tt.transpose(
                    psb[:, j * 128:(j + 1) * 128],
                    ln2xb[w % 2][:, j * 128:(j + 1) * 128], identB_sb[:]),
                    waits=[("ln2", lnv), free_psb] if j == 0 else (), force=True,
                    inc="trb" if j == NKC - 1 else None)
            trbv = s.val("trb")
            for j in range(NKC):
                s.on("vector", lambda v, w=w, j=j: v.tensor_copy(
                    ln2xT[:, j * 512 + w * 128:j * 512 + (w + 1) * 128],
                    psb[:, j * 128:(j + 1) * 128]),
                    waits=([("trb", trbv), ("h1", prev_h1)]) if j == 0 else (),
                    force=True, inc="trbc" if j == NKC - 1 else None)
            trbcv = s.val("trbc")
            free_psb = ("trbc", trbcv)
            last_ln2xb[w % 2] = ("trbc", trbcv)

        for w in range(GT):
            t = 4 * g + w
            col, ww = npair([("atn", ATN)])
            for c in range(NKC):
                for h in range(2):
                    s.on("tensor", lambda tt, c=c, h=h, w=w, col=col: tt.matmul(
                        ps[:, col + h * 512:col + (h + 1) * 512],
                        attnT[:, c * 512 + w * 128:c * 512 + (w + 1) * 128],
                        wo_bf[:, c * D + h * 512:c * D + (h + 1) * 512],
                        start=(c == 0), stop=False, skip_group_check=True),
                        waits=ww if (c == 0 and h == 0) else (), force=True)
            for h in range(2):
                s.on("tensor", lambda tt, h=h, col=col: tt.matmul(
                    ps[:, col + h * 512:col + (h + 1) * 512], onesrow_sb[:],
                    bo_row[:, h * 512:(h + 1) * 512],
                    start=False, stop=True, skip_group_check=True),
                    inc="womm" if h == 1 else None)
            wov = s.val("womm")
            s.on("vector", lambda v, w=w, col=col: v.scalar_tensor_tensor(
                xresB[w][:], ps[:, col:col + D], 1.0, xaB[w][:],
                ALU.mult, ALU.add, accum_out=statC[:, 0 + w % 2:1 + w % 2]),
                waits=[("womm", wov), bxa_w[w]], force=True, inc="xrs")
            xrv = s.val("xrs")
            free_pair[col] = ("xrs", xrv)
            last_xaB[w] = ("xrs", xrv)
            s.on("scalar", lambda sc, w=w: sc.activation(
                junkB[:], xresB[w][:], AF.Square,
                accum_out=statC[:, 2 + w % 2:3 + w % 2]),
                waits=[("xrs", xrv)], inc="s2ssq")
            sqv = s.val("s2ssq")
            s.on("vector", lambda v, w=w: v.tensor_scalar(
                statC[:, 4 + w % 2:5 + w % 2], statC[:, 0 + w % 2:1 + w % 2],
                1.0 / D, None, ALU.mult),
                waits=[("xrs", xrv)], force=True, inc="s2ch")
            s.on("vector", lambda v, w=w: v.tensor_mul(
                smc[:, 18 + w % 2:19 + w % 2], statC[:, 4 + w % 2:5 + w % 2],
                statC[:, 4 + w % 2:5 + w % 2]),
                waits=[("s2ch", s.val("s2ch"))], force=True, inc="s2ch")
            s.on("vector", lambda v, w=w: v.scalar_tensor_tensor(
                statC[:, 6 + w % 2:7 + w % 2], statC[:, 2 + w % 2:3 + w % 2],
                1.0 / D, smc[:, 18 + w % 2:19 + w % 2], ALU.mult, ALU.subtract),
                waits=[("s2ssq", sqv), ("s2ch", s.val("s2ch"))], force=True,
                inc="s2ch")
            scv = s.val("s2ch")
            s.on("scalar", lambda sc, w=w: sc.activation(
                smc[:, 20 + w % 2:21 + w % 2], statC[:, 6 + w % 2:7 + w % 2],
                AF.Ln, bias=eps6[:]),
                waits=[("s2ch", scv)], inc="s2ch")
            s.on("scalar", lambda sc, w=w: sc.activation(
                statC[:, 6 + w % 2:7 + w % 2], smc[:, 20 + w % 2:21 + w % 2],
                AF.Exp, scale=-0.5),
                waits=[("s2ch", s.val("s2ch"))], force=True, inc="s2ch")
            scv = s.val("s2ch")
            s.on("vector", lambda v, w=w: v.tensor_scalar(
                ln2xb[w % 2][:], xresB[w][:], statC[:, 4 + w % 2:5 + w % 2],
                statC[:, 6 + w % 2:7 + w % 2], ALU.subtract, ALU.mult),
                waits=[("s2ch", scv), last_ln2xb.get(w % 2, ("fold", 0))],
                force=True, inc="ln2")
            ln2_done[w] = s.val("ln2")
            if w >= 1:
                emit_trb(w - 1)
        emit_trb(GT - 1)
        TRBC = s.val("trbc")
        LN2G = s.val("ln2")

        for f1 in range(NF1):
            wldv = s.dma("sync", lambda e, f1=f1: e.dma_start(
                out=w1r[f1 % 3][:], in_=w1scr[f1]),
                "w1ld", f1 % 3, waits=[last_w1r.get(f1 % 3, ("fold", 0))])
            col, w1w = nsingle([wldv, ("trbc", TRBC)])
            for kc in range(NKC):
                s.on("tensor", lambda tt, f1=f1, kc=kc, col=col: tt.matmul(
                    ps[:, col:col + 512],
                    w1r[f1 % 3][:, kc * 128:(kc + 1) * 128],
                    ln2xT[:, kc * 512:(kc + 1) * 512],
                    start=(kc == 0), stop=(kc == NKC - 1), skip_group_check=True),
                    waits=w1w if kc == 0 else (), force=True,
                    inc="ymm" if kc == NKC - 1 else None)
            mv = s.val("ymm")
            s.on("scalar", lambda sc, f1=f1, col=col: sc.activation(
                h1T[:, f1 * 512:(f1 + 1) * 512], ps[:, col:col + 512],
                AF.Gelu_apprx_tanh, bias=b1p[:, f1:f1 + 1]),
                waits=[("ymm", mv)], force=True, inc="h1")
            hv = s.val("h1")
            free_single[col] = ("h1", hv)
            last_w1r[f1 % 3] = ("h1", hv)
        H1V = s.val("h1")
        prev_h1 = H1V

        XRS4 = s.val("xrs")
        fh_evac = {}
        for h in range(2):
            w2u0 = s.val("w2u")
            for w in range(GT):
                col = 512 + 512 * w
                if h == 0:
                    bw = ([("xrs", XRS4), free_pair[512], free_pair[1536]]
                          if w == 0 else ())
                else:
                    bw = [fh_evac[(0, w)]]
                s.on("tensor", lambda tt, h=h, col=col: tt.matmul(
                    ps[:, col:col + 512], onesrow_sb[:],
                    b2_row[:, h * 512:(h + 1) * 512],
                    start=True, stop=False, skip_group_check=True),
                    waits=bw, force=True)
            for f1 in range(NF1):
                slot = f1 % 4
                eng = "sync" if f1 % 2 == 0 else "gpsimd"
                w2v = s.dma(eng, lambda e, f1=f1, slot=slot, h=h: e.dma_start(
                    out=w2r[slot][:], in_=w2scr[f1, :, h * 512:(h + 1) * 512]),
                    "w2f", slot,
                    waits=[last_w2r.get(slot, ("fold", 0))]
                    + (W2CASTS if (g == 0 and h == 0 and f1 < 3) else []))
                for w in range(GT):
                    col = 512 + 512 * w
                    s.on("tensor", lambda tt, f1=f1, w=w, col=col, slot=slot:
                         tt.matmul(
                             ps[:, col:col + 512],
                             h1T[:, f1 * 512 + w * 128:f1 * 512 + (w + 1) * 128],
                             w2r[slot][:, 0:512],
                             start=False, stop=(f1 == NF1 - 1),
                             skip_group_check=True),
                         waits=[w2v, ("h1", H1V)] if w == 0 else (), force=True,
                         inc=("w2u" if f1 == NF1 - 1 else
                              ("w2t" if w == GT - 1 else None)))
                if f1 == NF1 - 1:
                    last_w2r[slot] = ("w2u", s.val("w2u"))
                else:
                    last_w2r[slot] = ("w2t", s.val("w2t"))
            for w in range(GT):
                col = 512 + 512 * w
                s.on("vector", lambda v, w=w, col=col, h=h: v.tensor_add(
                    xresB[w][:, h * 512:(h + 1) * 512], ps[:, col:col + 512],
                    xresB[w][:, h * 512:(h + 1) * 512]),
                    waits=[("w2u", w2u0 + w + 1), ("ln2", LN2G)], force=True,
                    inc="zf")
                fh_evac[(h, w)] = ("zf", s.val("zf"))
        free_pair[512] = fh_evac[(1, 1)]
        free_pair[1536] = fh_evac[(1, 3)]

        if True:
            for w in range(GT):
                t = 4 * g + w
                s.on("vector", lambda v, w=w: v.scalar_tensor_tensor(
                    zbuf[w % 2][:], xresB[w][:], 1.0, xaB[w][:],
                    ALU.mult, ALU.add, accum_out=statC[:, 8 + w % 2:9 + w % 2]),
                    waits=[fh_evac[(1, w)], last_zbuf.get(w % 2, ("fold", 0))],
                    force=True, inc="zf")
                zv = s.val("zf")
                last_xaB[w] = ("zf", zv)
                s.on("scalar", lambda sc, w=w: sc.activation(
                    junkB[:], zbuf[w % 2][:], AF.Square,
                    accum_out=statC[:, 10 + w % 2:11 + w % 2]),
                    waits=[("zf", zv)], inc="s3ssq")
                sqv = s.val("s3ssq")
                s.on("vector", lambda v, w=w: v.tensor_scalar(
                    statC[:, 12 + w % 2:13 + w % 2], statC[:, 8 + w % 2:9 + w % 2],
                    1.0 / D, None, ALU.mult),
                    waits=[("zf", zv)], force=True, inc="s3ch")
                s.on("vector", lambda v, w=w: v.tensor_mul(
                    smc[:, 6 + w % 2:7 + w % 2], statC[:, 12 + w % 2:13 + w % 2],
                    statC[:, 12 + w % 2:13 + w % 2]),
                    waits=[("s3ch", s.val("s3ch"))], force=True, inc="s3ch")
                s.on("vector", lambda v, w=w: v.scalar_tensor_tensor(
                    statC[:, 14 + w % 2:15 + w % 2],
                    statC[:, 10 + w % 2:11 + w % 2],
                    1.0 / D, smc[:, 6 + w % 2:7 + w % 2], ALU.mult, ALU.subtract),
                    waits=[("s3ssq", sqv), ("s3ch", s.val("s3ch"))], force=True,
                    inc="s3ch")
                scv = s.val("s3ch")
                s.on("scalar", lambda sc, w=w: sc.activation(
                    smc[:, 8 + w % 2:9 + w % 2], statC[:, 14 + w % 2:15 + w % 2],
                    AF.Ln, bias=eps5[:]),
                    waits=[("s3ch", scv)], inc="s3ch")
                s.on("scalar", lambda sc, w=w: sc.activation(
                    statC[:, 14 + w % 2:15 + w % 2], smc[:, 8 + w % 2:9 + w % 2],
                    AF.Exp, scale=-0.5),
                    waits=[("s3ch", s.val("s3ch"))], force=True, inc="s3ch")
                scv = s.val("s3ch")
                s.on("vector", lambda v, w=w: v.tensor_scalar(
                    x1f[w % 2][:], zbuf[w % 2][:], statC[:, 12 + w % 2:13 + w % 2],
                    statC[:, 14 + w % 2:15 + w % 2], ALU.subtract, ALU.mult),
                    waits=[("s3ch", scv), last_x1f.get(w % 2, ("fold", 0))],
                    force=True, inc="zf2")
                z2v = s.val("zf2")
                last_zbuf[w % 2] = ("zf2", z2v)
                stv = s.dma("sync", lambda e, t=t, w=w: e.dma_start(
                    out=x1scr[t], in_=x1f[w % 2][:]),
                    "x1st", w % 2, waits=[("zf2", z2v)])
                s.on("vector", lambda v, w=w: v.tensor_copy(
                    x1bfB[w][:], x1f[w % 2][:]),
                    waits=[("zf2", z2v), stv, last_x1bf.get(w, ("fold", 0))],
                    force=True, inc="x1bfc")
                x1bv = s.val("x1bfc")
                last_x1f[w % 2] = ("x1bfc", x1bv)
                pend_x1.append(make_x1trans(w, t, x1bv))
        last_attnT = ("womm", s.val("womm"))
    while pend_x1:
        pend_x1.pop(0)()

    ctx["BARF"] = ([("zf2", s.val("zf2")), ("trbc", s.val("trbc")),
                    ("w2u", s.val("w2u")), ("w2t", s.val("w2t")),
                    ("h1", s.val("h1")), ("zf", s.val("zf")),
                    ("atn", s.val("atn")), ("qmm", s.val("qmm")),
                    ("s3ssq", s.val("s3ssq")), ("ymm", s.val("ymm"))]
                   + s.dma_totals("x1tst") + s.dma_totals("x1st")
                   + s.dma_totals("w1ld") + s.dma_totals("w2f")
                   + s.dma_totals("bxa") + s.dma_totals("bxt"))


def build_ffn(nc, s, ctx):
    ctx["scope_b"].close()
    ctx["scope_ab"].close()
    ps = ctx["ps"]; es = ctx["es"]
    identF_sb = ctx["identF_sb"]; identB_sb = ctx["identB_sb"]
    onesrow_sb = ctx["onesrow_sb"]; onescol_sb = ctx["onescol_sb"]
    eps5 = ctx["eps5"]; smc = ctx["smc"]
    wr_bf = ctx["wr_bf"]
    ut_sb = ctx["ut_sb"]; cio_sb = ctx["cio_sb"]
    x1scr = ctx["x1scr"]; x1Tscr = ctx["x1Tscr"]
    x1g = ctx["x1g"]; yg = ctx["yg"]
    wg32 = ctx["wg32"]; wu32 = ctx["wu32"]; wd32 = ctx["wd32"]
    y = ctx["y"]
    BARF = ctx["BARF"]

    scope_f = ExitStack()
    ctx["scope_f"] = scope_f

    def sbf(name, shape, dt=F32):
        return scope_f.enter_context(nc.sbuf_tensor(name, shape, dt))

    x1Tc = [sbf("x1Tc0", [128, NKC * CH], BF16)] * 2
    wgr = [sbf(f"wgr{i}", [128, NKC * 128], BF16) for i in range(4)]
    wur = [sbf(f"wur{i}", [128, NKC * 128], BF16) for i in range(4)]
    wdb = sbf("wdb", [128, NFC * D], BF16)
    gxt = [sbf(f"gxt{i}", [128, 4 * D], BF16) for i in range(3)]
    xgT = sbf("xgT", [128, NKC * 512], BF16)
    hTf = sbf("hTf", [128, NFC * 512], BF16)
    tref = [sbf(f"tref{i}", [128, 512]) for i in range(2)]
    ygb = [sbf(f"ygb{i}", [128, D], BF16) for i in range(2)]
    gbuf1 = [sbf(f"gbuf1{i}", [128, D], BF16) for i in range(3)]
    gbuf2 = [sbf(f"gbuf2{i}", [128, D], BF16) for i in range(3)]
    x1rd = [sbf(f"x1rd{i}", [128, D]) for i in range(2)]
    xbf_t = [sbf(f"xbf_t{i}", [128, D], BF16) for i in range(2)]
    z2b = [sbf(f"z2b{i}", [128, D]) for i in range(3)]
    outb = [sbf(f"outb{i}", [128, D]) for i in range(2)]
    prt = sbf("prt", [8, CH])
    rtl = [sbf(f"rtl{i}", [128, 8]) for i in range(2)]
    rtm = [sbf(f"rtm{i}", [128, 8]) for i in range(2)]
    rtm2 = [sbf(f"rtm2{i}", [128, 8]) for i in range(2)]
    rtm3 = [sbf(f"rtm3{i}", [128, 8]) for i in range(2)]
    rtmA = [sbf(f"rtmA{i}", [128, 8]) for i in range(2)]
    m12b = [sbf(f"m12b{i}", [128, 8], BF16) for i in range(2)]
    prefsb = [sbf(f"prefsb{i}", [128, 8]) for i in range(2)]
    s0sb = [sbf(f"s0sb{i}", [128, 8]) for i in range(2)]
    tmp8 = [sbf(f"tmp8{i}", [128, 8]) for i in range(2)]
    ciobc = sbf("ciobc", [128, 8])
    baseR = [sbf(f"baseR{i}", [1, 8]) for i in range(2)]
    idx1i = sbf("idx1i", [128, NTB], mybir.dt.int32)
    idx2i = sbf("idx2i", [128, NTB], mybir.dt.int32)
    idxi = [sbf(f"idxi{i}", [128, 1], mybir.dt.int32) for i in range(4)]
    ce1s = sbf("ce1s", [128, NTB])
    ce2s = sbf("ce2s", [128, NTB])
    psf = scope_f.enter_context(nc.psum_tensor("psf", [128, 1024], BF16))

    barrier = list(BARF)
    PREF, TOT, BC = 2560, 2576, 2592

    last_rt = {}; last_xbf = {}; last_x1rd = {}; last_z2b = {}; last_outb = {}
    last_gb = {}; last_wslot = {}; last_gxt = {}; last_ygb = {}; last_sci = {}
    free_rtbank = ("fold", 0)
    free_psf = ("trbc", s.val("trbc"))
    free_b5 = ("fold", 0)
    gf = [0]

    s.on("tensor", lambda t: t.matmul(ps[:, BC:BC + 8], onesrow_sb[:], cio_sb[:],
                                      start=True, stop=True, skip_group_check=True),
         waits=barrier, force=True, inc="dnmm")
    s.on("vector", lambda v: v.tensor_copy(ciobc[:], ps[:, BC:BC + 8]),
         waits=barrier + [("dnmm", s.val("dnmm"))], force=True, inc="ln2")
    s.on("vector", lambda v: v.memset(baseR[0][:], 0.0), inc="s2ch")
    free_b5 = ("ln2", s.val("ln2"))

    last_x1Tc = {}
    for ch in range(NCH):
        fxt_v = s.dma("sync", lambda e, ch=ch: e.dma_start(
            out=x1Tc[ch % 2][:].rearrange("p (c f) -> p c f", c=NKC),
            in_=x1Tscr[:, :, ch * CH:(ch + 1) * CH].rearrange("c p f -> p c f")),
            "fxt", ch % 2, waits=barrier + [last_x1Tc.get(0, ("fold", 0))])
        for kc in range(NKC):
            for h in range(2):
                s.on("tensor", lambda tt, ch=ch, kc=kc, h=h: tt.matmul(
                    ps[0:8, h * 512:(h + 1) * 512],
                    wr_bf[:, kc * NE:(kc + 1) * NE],
                    x1Tc[ch % 2][:, kc * CH + h * 512:kc * CH + (h + 1) * 512],
                    start=(kc == 0), stop=(kc == NKC - 1), skip_group_check=True),
                    waits=(barrier + [fxt_v, ("atn", s.val("atn")),
                                      ("ln2", s.val("ln2")),
                                      ("bcev", s.val("bcev"))]
                           + s.dma_totals("gwr"))
                    if (kc == 0 and h == 0) else (), force=True,
                    inc="dnmm" if (kc == NKC - 1 and h == 1) else None)
        rtv = s.val("dnmm")
        last_x1Tc[0] = ("dnmm", rtv)
        s.on("vector", lambda v: v.tensor_copy(prt[:], ps[0:8, 0:CH]),
             waits=[("dnmm", rtv)], force=True, inc="bcev")
        rte_v = s.val("bcev")
        for t8 in range(NT8):
            tt_g = ch * NT8 + t8
            sl = t8 % 4
            pr = t8 % 2
            s.on("tensor", lambda tx, t8=t8, sl=sl: tx.transpose(
                ps[0:128, 2048 + sl * 128:2048 + sl * 128 + 8],
                prt[:, t8 * 128:(t8 + 1) * 128], identF_sb[0:8, 0:8]),
                waits=[("bcev", rte_v), free_rtbank], force=True, inc="trb")
            trv = s.val("trb")
            s.on("vector", lambda v, sl=sl, pr=pr: v.tensor_copy(
                rtl[pr][:], ps[0:128, 2048 + sl * 128:2048 + sl * 128 + 8]),
                waits=[("trb", trv), last_rt.get(pr, ("fold", 0))], force=True,
                inc="ln2")
            free_rtbank = ("ln2", s.val("ln2"))

            def sm_op(fn, waits=(), inc="ln2", eng="vector"):
                s.on(eng, fn, waits=[("ln2", s.val("ln2"))] + list(waits),
                     force=True, inc=inc)

            c0 = 10 + pr
            sm_op(lambda v, pr=pr, c0=c0: v.reduce_max(
                smc[:, c0:c0 + 1], rtl[pr][:], axis=AX.X))
            sm_op(lambda v, pr=pr, c0=c0: v.tensor_scalar(
                rtm[pr][:], rtl[pr][:], smc[:, c0:c0 + 1], None, ALU.subtract))
            sm_op(lambda v, pr=pr: v.tensor_scalar(
                rtmA[pr][:], rtm[pr][:], 0.0, None, ALU.is_equal))
            sm_op(lambda v, pr=pr: v.scalar_tensor_tensor(
                rtm3[pr][:], rtmA[pr][:], -30000.0, rtl[pr][:],
                ALU.mult, ALU.add))
            sm_op(lambda v, pr=pr, c0=c0: v.reduce_max(
                smc[:, c0 + 4:c0 + 5], rtm3[pr][:], axis=AX.X))
            sm_op(lambda v, pr=pr, c0=c0: v.tensor_scalar(
                rtm2[pr][:], rtl[pr][:], smc[:, c0 + 4:c0 + 5], None, ALU.is_ge))
            sm_op(lambda v, pr=pr: v.tensor_copy(m12b[pr][:], rtm2[pr][:]))
            sm_op(lambda v, pr=pr, c0=c0: v.tensor_sub(
                smc[:, c0:c0 + 1], smc[:, c0:c0 + 1], smc[:, c0 + 4:c0 + 5]))
            sm_op(lambda sc_, pr=pr, c0=c0, tt_g=tt_g: sc_.activation(
                ce1s[:, tt_g:tt_g + 1], smc[:, c0:c0 + 1], AF.Sigmoid),
                eng="scalar")
            sm_op(lambda v, pr=pr, tt_g=tt_g: v.tensor_scalar(
                ce2s[:, tt_g:tt_g + 1], ce1s[:, tt_g:tt_g + 1], -1.0, 1.0,
                ALU.mult, ALU.add))
            M12V = s.val("ln2")
            s.on("tensor", lambda tx, pr=pr: tx.matmul(
                ps[:, PREF:PREF + 8], ut_sb[:], m12b[pr][:],
                start=True, stop=True, skip_group_check=True),
                waits=[("ln2", M12V), free_b5], force=True)
            s.on("tensor", lambda tx, pr=pr: tx.matmul(
                ps[0:1, TOT:TOT + 8], onescol_sb[:], m12b[pr][:],
                start=True, stop=True, skip_group_check=True), force=True)
            cur = tt_g % 2
            s.on("tensor", lambda tx, cur=cur: tx.matmul(
                ps[:, BC:BC + 8], onesrow_sb[:], baseR[cur][:],
                start=True, stop=True, skip_group_check=True),
                waits=[("s2ch", s.val("s2ch"))], force=True, inc="dnmm")
            pfv = s.val("dnmm")
            sm_op(lambda v, pr=pr: v.tensor_copy(prefsb[pr][:], ps[:, PREF:PREF + 8]),
                  waits=[("dnmm", pfv)])
            sm_op(lambda v, cur=cur: v.tensor_add(
                baseR[1 - cur][:], baseR[cur][:], ps[0:1, TOT:TOT + 8]),
                inc="s2ch")
            sm_op(lambda v, pr=pr: v.tensor_add(s0sb[pr][:], prefsb[pr][:],
                                                ps[:, BC:BC + 8]))
            FB5 = s.val("ln2")
            free_b5 = ("ln2", FB5)
            sm_op(lambda v, pr=pr: v.tensor_scalar(
                s0sb[pr][:], s0sb[pr][:], float(CAP - 1), None, ALU.min))
            sm_op(lambda v, pr=pr: v.tensor_add(s0sb[pr][:], s0sb[pr][:], ciobc[:]))
            sm_op(lambda v, pr=pr: v.tensor_mul(tmp8[pr][:], s0sb[pr][:], rtmA[pr][:]))
            sm_op(lambda v, pr=pr, tt_g=tt_g: v.reduce_sum(
                smc[:, 14 + pr:15 + pr], tmp8[pr][:], axis=AX.X))
            sm_op(lambda v, pr=pr: v.tensor_scalar(
                smc[:, 14 + pr:15 + pr], smc[:, 14 + pr:15 + pr],
                float(NE * CAP - 1), None, ALU.min))
            sm_op(lambda v, pr=pr, tt_g=tt_g: v.tensor_copy(
                idx1i[:, tt_g:tt_g + 1], smc[:, 14 + pr:15 + pr]))
            sm_op(lambda v, pr=pr: v.tensor_copy(
                idxi[pr][:], smc[:, 14 + pr:15 + pr]),
                waits=[last_sci.get(pr, ("fold", 0))])
            sm_op(lambda v, pr=pr: v.tensor_sub(rtm2[pr][:], rtm2[pr][:], rtmA[pr][:]))
            sm_op(lambda v, pr=pr: v.tensor_mul(tmp8[pr][:], s0sb[pr][:], rtm2[pr][:]))
            sm_op(lambda v, pr=pr, tt_g=tt_g: v.reduce_sum(
                smc[:, 14 + pr:15 + pr], tmp8[pr][:], axis=AX.X))
            sm_op(lambda v, pr=pr: v.tensor_scalar(
                smc[:, 14 + pr:15 + pr], smc[:, 14 + pr:15 + pr],
                float(NE * CAP - 1), None, ALU.min))
            sm_op(lambda v, pr=pr, tt_g=tt_g: v.tensor_copy(
                idx2i[:, tt_g:tt_g + 1], smc[:, 14 + pr:15 + pr]))
            sm_op(lambda v, pr=pr: v.tensor_copy(
                idxi[2 + pr][:], smc[:, 14 + pr:15 + pr]),
                waits=[last_sci.get(2 + pr, ("fold", 0))], inc="xrs")
            last_rt[pr] = ("xrs", s.val("xrs"))
            RTDONE = s.val("xrs")
            x1v = s.dma("sync", lambda e, tt_g=tt_g, t8=t8: e.dma_start(
                out=x1rd[t8 % 2][:], in_=x1scr[tt_g]),
                "x1ld", t8 % 2, waits=[last_x1rd.get(t8 % 2, ("fold", 0))])
            s.on("vector", lambda v, t8=t8: v.tensor_copy(
                xbf_t[t8 % 2][:], x1rd[t8 % 2][:]),
                waits=[x1v, last_xbf.get(t8 % 2, ("fold", 0))], force=True,
                inc="zf")
            xbv = s.val("zf")
            last_x1rd[t8 % 2] = ("zf", xbv)
            sc1 = s.dma("gpsimd", lambda g_, t8=t8: g_.indirect_dma_start(
                x1g, IndirectOffsetOnAxis(idxi[t8 % 2][:], 0),
                xbf_t[t8 % 2][:], None),
                "axt", t8 % 2, waits=[("zf", xbv), ("xrs", RTDONE)])
            sc2 = s.dma("gpsimd", lambda g_, t8=t8: g_.indirect_dma_start(
                x1g, IndirectOffsetOnAxis(idxi[2 + t8 % 2][:], 0),
                xbf_t[t8 % 2][:], None),
                "bxt", t8 % 2)
            last_xbf[t8 % 2] = sc2
            last_sci[t8 % 2] = sc1
            last_sci[2 + t8 % 2] = sc2

    SCT = s.dma_totals("axt") + s.dma_totals("bxt")

    for e8 in range(NE):
        fwd_v = s.dma("gpsimd", lambda g_, e8=e8: g_.dma_start(
            out=wdb[:].rearrange("p (c d) -> p c d", c=NFC),
            in_=wd32[e8].rearrange("c p d -> p c d")),
            "fwd", e8 % 2, waits=barrier + [("womm", s.val("womm"))])
        for sc_i in range(NSC):
            row0 = e8 * CAP + sc_i * 512
            gslot = (e8 * NSC + sc_i) % 3
            WPREV = s.val("womm")
            gx_v = s.dma("sync", lambda e, row0=row0, gslot=gslot: e.dma_start(
                out=gxt[gslot][:].rearrange("p (t d) -> p t d", t=4),
                in_=x1g[row0:row0 + 512].rearrange("(t p) d -> p t d", p=128)),
                "axa", gslot, waits=SCT + [last_gxt.get(gslot, ("fold", 0))])
            for t4 in range(4):
                for j in range(NKC):
                    s.on("tensor", lambda tx, t4=t4, j=j, gslot=gslot: tx.transpose(
                        psf[:, j * 128:(j + 1) * 128],
                        gxt[gslot][:, t4 * D + j * 128:t4 * D + (j + 1) * 128],
                        identB_sb[:]),
                        waits=[gx_v, free_psf] if j == 0 else (), force=True,
                        inc="trb" if j == NKC - 1 else None)
                trv = s.val("trb")
                for j in range(NKC):
                    s.on("vector", lambda v, t4=t4, j=j: v.tensor_copy(
                        xgT[:, j * 512 + t4 * 128:j * 512 + (t4 + 1) * 128],
                        psf[:, j * 128:(j + 1) * 128]),
                        waits=([("trb", trv), ("qmm", s.val("qmm"))])
                        if j == 0 else (),
                        force=True, inc="trbc" if j == NKC - 1 else None)
                free_psf = ("trbc", s.val("trbc"))
            TRX = s.val("trbc")
            for f in range(NFC):
                slot = gf[0] % 4
                s.dma("gpsimd", lambda g_, e8=e8, f=f, slot=slot: g_.dma_start(
                    out=wgr[slot][:], in_=wg32[e8, f]),
                    "fw", slot, waits=[last_wslot.get(slot, ("fold", 0))])
                fw_v = s.dma("gpsimd", lambda g_, e8=e8, f=f, slot=slot: g_.dma_start(
                    out=wur[slot][:], in_=wu32[e8, f]),
                    "fw", slot)
                gf[0] += 1
                gcol = 0 if f % 2 == 0 else 1024
                ucol = 512 if f % 2 == 0 else 1536
                tw = [fw_v, ("trbc", TRX), ("phv", max(0, s.val("phv") - 1)),
                      ("phr", max(0, s.val("phr") - 1))]
                if f == 0:
                    tw.append(("atn", s.val("atn")))
                for kc in range(NKC):
                    s.on("tensor", lambda tx, kc=kc, slot=slot, gcol=gcol: tx.matmul(
                        ps[:, gcol:gcol + 512],
                        wgr[slot][:, kc * 128:(kc + 1) * 128],
                        xgT[:, kc * 512:(kc + 1) * 512],
                        start=(kc == 0), stop=(kc == NKC - 1),
                        skip_group_check=True),
                        waits=tw if kc == 0 else (), force=True)
                for kc in range(NKC):
                    s.on("tensor", lambda tx, kc=kc, slot=slot, ucol=ucol: tx.matmul(
                        ps[:, ucol:ucol + 512],
                        wur[slot][:, kc * 128:(kc + 1) * 128],
                        xgT[:, kc * 512:(kc + 1) * 512],
                        start=(kc == 0), stop=(kc == NKC - 1),
                        skip_group_check=True),
                        force=True, inc="qmm" if kc == NKC - 1 else None)
                guv = s.val("qmm")
                last_wslot[slot] = ("qmm", guv)
                s.on("scalar", lambda sc_, f=f, gcol=gcol: sc_.activation(
                    tref[f % 2][:], ps[:, gcol:gcol + 512], AF.Relu),
                    waits=[("qmm", guv), ("phv", max(0, s.val("phv") - 1))],
                    force=True, inc="phr")
                prv = s.val("phr")
                mw = [("phr", prv)]
                if f == 0:
                    mw.append(("womm", WPREV))
                s.on("vector", lambda v, f=f, ucol=ucol: v.tensor_mul(
                    hTf[:, f * 512:(f + 1) * 512], tref[f % 2][:],
                    ps[:, ucol:ucol + 512]),
                    waits=mw, force=True, inc="phv")
            HDV = s.val("phv")
            last_gxt[gslot] = ("qmm", s.val("qmm"))
            for t4 in range(4):
                dcol = 2048 if t4 % 2 == 0 else 0
                dw = [fwd_v, ("phv", HDV), ("atn", max(0, s.val("atn") - 1))]
                for kc in range(NFC):
                    for h in range(2):
                        s.on("tensor", lambda tx, kc=kc, h=h, t4=t4, dcol=dcol: tx.matmul(
                            ps[:, dcol + h * 512:dcol + (h + 1) * 512],
                            hTf[:, kc * 512 + t4 * 128:kc * 512 + (t4 + 1) * 128],
                            wdb[:, kc * D + h * 512:kc * D + (h + 1) * 512],
                            start=(kc == 0), stop=(kc == NFC - 1),
                            skip_group_check=True),
                            waits=dw if (kc == 0 and h == 0) else (), force=True,
                            inc="womm" if (kc == NFC - 1 and h == 1) else None)
                ddv = s.val("womm")
                s.on("vector", lambda v, t4=t4, dcol=dcol: v.tensor_copy(
                    ygb[t4 % 2][:], ps[:, dcol:dcol + D]),
                    waits=[("womm", ddv), last_ygb.get(t4 % 2, ("fold", 0))],
                    force=True, inc="atn")
                av = s.val("atn")
                last_ygb[t4 % 2] = s.dma("sync", lambda e, row0=row0, t4=t4: e.dma_start(
                    out=yg[row0 + t4 * 128:row0 + (t4 + 1) * 128],
                    in_=ygb[t4 % 2][:]),
                    "bxa", t4 % 2, waits=[("atn", av)])

    YGST = s.dma_totals("bxa")

    RTALL = ("xrs", s.val("xrs"))
    last_zfl = {}; last_stat = {}
    for tt_g in range(NTB):
        p3 = tt_g % 3
        c0 = 12 + 4 * p3
        g1v = s.dma("gpsimd", lambda g_, p3=p3, tt_g=tt_g: g_.indirect_dma_start(
            gbuf1[p3][:], None, yg, IndirectOffsetOnAxis(idx1i[:, tt_g:tt_g + 1], 0)),
            "w2ld", p3, waits=YGST + [RTALL, last_gb.get((0, p3), ("fold", 0))])
        g2v = s.dma("gpsimd", lambda g_, p3=p3, tt_g=tt_g: g_.indirect_dma_start(
            gbuf2[p3][:], None, yg, IndirectOffsetOnAxis(idx2i[:, tt_g:tt_g + 1], 0)),
            "bgld", p3, waits=[last_gb.get((1, p3), ("fold", 0))])
        x1v = s.dma("sync", lambda e, tt_g=tt_g, pr=tt_g % 2: e.dma_start(
            out=x1rd[pr][:], in_=x1scr[tt_g]),
            "x1ld", tt_g % 2, waits=[last_x1rd.get(tt_g % 2, ("fold", 0))])
        s.on("vector", lambda v, p3=p3, tt_g=tt_g, pr=tt_g % 2: v.scalar_tensor_tensor(
            z2b[p3][:], gbuf1[p3][:], ce1s[:, tt_g:tt_g + 1], x1rd[pr][:],
            ALU.mult, ALU.add),
            waits=[g1v, x1v, last_z2b.get(p3, ("fold", 0))], force=True, inc="zf2")
        s.on("vector", lambda v, p3=p3, tt_g=tt_g, c0=c0, pr=tt_g % 2:
             v.scalar_tensor_tensor(
                 x1rd[pr][:], gbuf2[p3][:], ce2s[:, tt_g:tt_g + 1], z2b[p3][:],
                 ALU.mult, ALU.add, accum_out=smc[:, c0:c0 + 1]),
             waits=[g2v, ("zf2", s.val("zf2")),
                    last_stat.get(p3, ("fold", 0))],
             force=True, inc="zf2")
        zv = s.val("zf2")
        last_gb[(0, p3)] = ("zf2", zv)
        s.on("scalar", lambda sc_, p3=p3, c0=c0, pr=tt_g % 2: sc_.activation(
            z2b[p3][:], x1rd[pr][:], AF.Square,
            accum_out=smc[:, c0 + 1:c0 + 2]),
            waits=[("zf2", zv), last_stat.get(p3, ("fold", 0))], inc="s2ssq")
        sqv = s.val("s2ssq")
        s.on("vector", lambda v, c0=c0: v.tensor_scalar(
            smc[:, c0 + 2:c0 + 3], smc[:, c0:c0 + 1], 1.0 / D, None,
            ALU.mult),
            waits=[("zf2", zv)], force=True, inc="s2ch")
        s.on("vector", lambda v, c0=c0: v.tensor_mul(
            smc[:, c0 + 3:c0 + 4], smc[:, c0 + 2:c0 + 3],
            smc[:, c0 + 2:c0 + 3]),
            waits=[("s2ch", s.val("s2ch"))], force=True, inc="s2ch")
        s.on("vector", lambda v, c0=c0: v.scalar_tensor_tensor(
            smc[:, c0:c0 + 1], smc[:, c0 + 1:c0 + 2], 1.0 / D,
            smc[:, c0 + 3:c0 + 4], ALU.mult, ALU.subtract),
            waits=[("s2ssq", sqv), ("s2ch", s.val("s2ch"))], force=True,
            inc="s2ch")
        scv = s.val("s2ch")
        s.on("scalar", lambda sc_, c0=c0: sc_.activation(
            smc[:, c0 + 3:c0 + 4], smc[:, c0:c0 + 1], AF.Ln,
            bias=eps5[:]),
            waits=[("s2ch", scv)], inc="s2ch")
        s.on("scalar", lambda sc_, c0=c0: sc_.activation(
            smc[:, c0 + 1:c0 + 2], smc[:, c0 + 3:c0 + 4], AF.Exp, scale=-0.5),
            waits=[("s2ch", s.val("s2ch"))], force=True, inc="s2ch")
        scv = s.val("s2ch")
        s.on("vector", lambda v, c0=c0, po=tt_g % 2: v.tensor_scalar(
            outb[po][:], x1rd[po][:], smc[:, c0 + 2:c0 + 3],
            smc[:, c0 + 1:c0 + 2], ALU.subtract, ALU.mult),
            waits=[("s2ch", scv), last_outb.get(tt_g % 2, ("fold", 0))],
            force=True, inc="zf2")
        ov = s.val("zf2")
        last_z2b[p3] = ("zf2", ov)
        last_x1rd[tt_g % 2] = ("zf2", ov)
        last_gb[(1, p3)] = ("zf2", ov)
        last_stat[p3] = ("zf2", ov)
        last_outb[tt_g % 2] = s.dma("sync", lambda e, tt_g=tt_g, po=tt_g % 2:
                                    e.dma_start(out=y[tt_g], in_=outb[po][:]),
                                    "x1st", tt_g % 2, waits=[("zf2", ov)])

    s.on("sync", lambda e: e.nop(), waits=s.all_dma_totals(), force=True)


def _host_weights(inputs):
    f32 = np.float32
    iw = {}
    iw["wq32"] = np.ascontiguousarray(
        inputs["L0_attn_Wq"].reshape(D, D).reshape(NKC, 128, D), dtype=f32)
    iw["wk32"] = np.ascontiguousarray(
        inputs["L0_attn_Wk"].reshape(D, D).reshape(NKC, 128, D), dtype=f32)
    iw["wv32"] = np.ascontiguousarray(
        inputs["L0_attn_Wv"].reshape(D, D).reshape(NKC, 128, D), dtype=f32)
    iw["wo32"] = np.ascontiguousarray(
        inputs["L0_attn_Wo"].reshape(D, D).reshape(NKC, 128, D), dtype=f32)
    iw["w132"] = np.ascontiguousarray(
        inputs["L0_attn_W1"].reshape(D, MLP).reshape(NKC, 128, MLP), dtype=f32)
    iw["w232"] = np.ascontiguousarray(
        inputs["L0_attn_W2"].reshape(MLP, D).reshape(NF1, 128, D), dtype=f32)
    iw["wr32"] = np.ascontiguousarray(
        inputs["L0_ffn_Wr"].reshape(D, NE).reshape(NKC, 128, NE), dtype=f32)
    wg = np.asarray(inputs["L0_ffn_Wg"], dtype=f32)
    wu = np.asarray(inputs["L0_ffn_Wu"], dtype=f32)
    wd = np.asarray(inputs["L0_ffn_Wd"], dtype=f32)
    iw["wg32"] = np.ascontiguousarray(
        wg.reshape(NE, NKC, 128, NFC, 128).transpose(0, 3, 2, 1, 4).reshape(
            NE, NFC, 128, NKC * 128))
    iw["wu32"] = np.ascontiguousarray(
        wu.reshape(NE, NKC, 128, NFC, 128).transpose(0, 3, 2, 1, 4).reshape(
            NE, NFC, 128, NKC * 128))
    iw["wd32"] = np.ascontiguousarray(wd.reshape(NE, NFC, 128, D))
    iw["ln1s"] = np.ascontiguousarray(
        np.asarray(inputs["L0_attn_ln1_scale"], f32).reshape(NKC, 128).T)
    iw["ln1b"] = np.ascontiguousarray(
        np.asarray(inputs["L0_attn_ln1_bias"], f32).reshape(NKC, 128).T)
    iw["ln2s"] = np.ascontiguousarray(
        np.asarray(inputs["L0_attn_ln2_scale"], f32).reshape(NKC, 128).T)
    iw["ln2b"] = np.ascontiguousarray(
        np.asarray(inputs["L0_attn_ln2_bias"], f32).reshape(NKC, 128).T)
    iw["bqr"] = np.asarray(inputs["L0_attn_bq"], f32).reshape(1, D)
    iw["bkr"] = np.asarray(inputs["L0_attn_bk"], f32).reshape(1, D)
    iw["bvr"] = np.asarray(inputs["L0_attn_bv"], f32).reshape(1, D)
    iw["bor32"] = np.asarray(inputs["L0_attn_bo"], f32).reshape(1, D)
    iw["b1r32"] = np.asarray(inputs["L0_attn_b1"], f32).reshape(1, MLP)
    iw["b2r32"] = np.asarray(inputs["L0_attn_b2"], f32).reshape(1, D)
    iw["identF"] = np.eye(128, dtype=f32)
    iw["identB"] = np.eye(128, dtype=f32).astype(bfd)
    iw["onesrow_f"] = np.ones((1, 128), f32)
    iw["onescol_b"] = np.ones((128, 1), f32).astype(bfd)
    zk = np.zeros((2, 128), f32); zk[1, :] = KEPS
    iw["zk_row"] = zk.astype(bfd)
    selm = np.zeros((NE, 16, 128), f32)
    for c in range(NE):
        selm[c, 2 * c, 0:64] = 1.0
        selm[c, 2 * c + 1, 64:128] = 1.0
    iw["sel"] = selm
    iw["ut128"] = np.triu(np.ones((128, 128), f32), 1).astype(bfd)
    iw["cio8"] = (CAP * np.arange(8, dtype=f32)).reshape(1, 8)
    return iw


def _core_x(inputs, c):
    b, j = c // 2, c % 2
    xb = np.asarray(inputs["L0_attn_inputs"][b], dtype=np.float32)
    own = xb[j * SOWN:(j + 1) * SOWN]
    oth = xb[(1 - j) * SOWN:(2 - j) * SOWN]
    xcat = np.concatenate([own, oth], 0)
    x_all = np.ascontiguousarray(xcat.reshape(NTA, 128, D))
    xT_all = np.ascontiguousarray(xcat.T).reshape(NKC, 128, SALL)
    return {"x_all": x_all, "xT_all": xT_all}


def kernel(**inputs):
    iw = _host_weights(inputs)
    nc, _ = build()
    in_maps = []
    for c in range(8):
        m = dict(iw)
        m.update(_core_x(inputs, c))
        in_maps.append(m)
    res = run_bass_kernel_spmd(nc, in_maps, core_ids=list(range(8)))
    out = np.empty((4, SALL, D), dtype=np.float32)
    for c in range(8):
        b, j = c // 2, c % 2
        out[b, j * SOWN:(j + 1) * SOWN] = res.results[c]["y"].reshape(SOWN, D)
    return out


# version 12 (final)
# speedup vs baseline: 1.2143x; 1.0349x over previous
import numpy as np
import concourse.bass as bass
import concourse.mybir as mybir
from concourse.bass import IndirectOffsetOnAxis
from concourse.bass_utils import run_bass_kernel_spmd
import ml_dtypes
from contextlib import ExitStack

F32 = mybir.dt.float32
BF16 = mybir.dt.bfloat16
AF = mybir.ActivationFunctionType
ALU = mybir.AluOpType
AX = mybir.AxisListType
bfd = ml_dtypes.bfloat16

D = 1024
NKC = 8
SOWN = 4096
SALL = 8192
NTA = SALL // 128
NTB = SOWN // 128
NG = 8
GT = 4
MLP = 4096
NF1 = MLP // 128
DFF = 2048
NFC = DFF // 128
NE = 8
NCH = 4
CH = 1024
NT8 = CH // 128
KEPS = 1e-3
LNEPS = 1e-6
PLNEPS = 1e-5
CAP = 1536
NSC = CAP // 512

ENGINES = ("tensor", "vector", "scalar", "gpsimd", "sync")


class Sched:
    def __init__(self):
        self.ops = {e: [] for e in ENGINES}
        self.sems = {}
        self.owner = {}
        self.last_wait = {e: {} for e in ENGINES}
        self.own_inc = {e: {} for e in ENGINES}
        self.streams = {}

    def sem(self, *names):
        for n in names:
            self.sems.setdefault(n, 0)

    def val(self, sname):
        return self.sems[sname]

    def on(self, engine, fn, waits=(), inc=None, n=1, force=False, dma=False):
        pruned = []
        for (sn, v) in waits:
            if v <= 0:
                continue
            if v <= self.last_wait[engine].get(sn, 0):
                continue
            if (not force and self.owner.get(sn, "multi") == engine
                    and v <= self.own_inc[engine].get(sn, 0)):
                continue
            pruned.append((sn, v))
            self.last_wait[engine][sn] = max(self.last_wait[engine].get(sn, 0), v)
        post = None
        if inc is not None:
            self.sems[inc] = self.sems.get(inc, 0) + n
            post = self.sems[inc]
            if dma:
                self.owner[inc] = None
            elif inc not in self.owner:
                self.owner[inc] = engine
            elif self.owner[inc] != engine:
                self.owner[inc] = "multi"
            if not dma:
                self.own_inc[engine][inc] = self.sems[inc]
        self.ops[engine].append((pruned, fn, inc, n))
        return post

    def dma(self, engine, fn, stream, slot=0, waits=()):
        name = f"{stream}@{slot}"
        self.sem(name)
        self.streams.setdefault(stream, set()).add(name)
        v = self.on(engine, fn, waits=waits, inc=name, n=16, dma=True)
        return (name, v)

    def dma_totals(self, stream):
        return [(n, self.sems[n]) for n in sorted(self.streams.get(stream, ()))]

    def all_dma_totals(self):
        out = []
        for st in sorted(self.streams):
            out.extend(self.dma_totals(st))
        return out

    def used_sems(self):
        used = set()
        for e in ENGINES:
            for waits, fn, inc, n in self.ops[e]:
                for (sn, v) in waits:
                    used.add(sn)
                if inc is not None:
                    used.add(inc)
        return used

    def n_ops(self):
        return {e: len(self.ops[e]) for e in ENGINES}

    def emit(self, nc, semhandles):
        with nc.Block() as block:
            for engname in ENGINES:
                if not self.ops[engname]:
                    continue

                def make_body(en):
                    def body(eng):
                        for waits, fn, inc, n in self.ops[en]:
                            for (sn, v) in waits:
                                eng.wait_ge(semhandles[sn], v)
                            ins = fn(eng)
                            if inc is not None:
                                ins.then_inc(semhandles[inc], n)
                    return body

                getattr(block, engname)(make_body(engname))


SEMS = [
    "cst", "ld_kv", "fold", "rowmm", "rowev", "rowasm",
    "axa", "axt", "ssum", "sssq", "sch", "tr", "trc", "kmm", "vmm",
    "epik", "epiv", "kvmm", "bg_ld", "bg_fold", "bg_st", "wold",
    "tev", "tks", "tkc", "lqmm", "lqev", "lqasm", "b1mm", "b1ev", "b1tr", "b1tc",
    "bxt", "bxa", "qmm", "phr", "phv", "dnmm", "rcp", "bcmm", "bcev",
    "atn", "womm", "xrs", "s2ssq", "s2ch", "ln2", "trb", "trbc",
    "w1ld", "h1", "w2ld", "ymm", "zf", "s3ssq", "s3ch", "outv",
    "x1st", "x1tst", "yst", "x1bfc",
    "fxt", "fw", "fwd", "rtmm", "rtev", "rttr", "rtsm", "ce",
    "gu", "hd", "dd", "acc", "acchk", "x1ld", "zf2", "s4ssq", "s4ch", "w2u", "w1rd",
    "w2t",
]

KVC = 0
GK = 512
GV = 1536
SM = 2560
SP1 = 3072


def build(stop_phase=None, dumps=()):
    nc = bass.Bass("TRN2", target_bir_lowering=False)

    def din(name, shape, dt=F32):
        return nc.dram_tensor(name, shape, dt, kind="ExternalInput").ap()

    def dscr(name, shape, dt):
        kind = "ExternalOutput" if name in dumps else "Internal"
        return nc.dram_tensor(name, shape, dt, kind=kind).ap()

    x_all = din("x_all", [NTA, 128, D])
    xT_all = din("xT_all", [NKC, 128, SALL])
    wq32 = din("wq32", [NKC, 128, D]); wk32 = din("wk32", [NKC, 128, D])
    wv32 = din("wv32", [NKC, 128, D]); wo32 = din("wo32", [NKC, 128, D])
    w132 = din("w132", [NKC, 128, MLP])
    w232 = din("w232", [NF1, 128, D])
    wr32 = din("wr32", [NKC, 128, NE])
    wg32 = din("wg32", [NE, NFC, 128, NKC * 128])
    wu32 = din("wu32", [NE, NFC, 128, NKC * 128])
    wd32 = din("wd32", [NE, NFC, 128, D])
    ln1s = din("ln1s", [128, NKC]); ln1b = din("ln1b", [128, NKC])
    ln2s = din("ln2s", [128, NKC]); ln2b = din("ln2b", [128, NKC])
    bqr = din("bqr", [1, D]); bkr = din("bkr", [1, D]); bvr = din("bvr", [1, D])
    bor32 = din("bor32", [1, D]); b1r32 = din("b1r32", [1, MLP]); b2r32 = din("b2r32", [1, D])
    identF = din("identF", [128, 128])
    identB = din("identB", [128, 128], BF16)
    sel = din("sel", [NE, 16, 128])
    onesrow_f = din("onesrow_f", [1, 128])
    onescol_b = din("onescol_b", [128, 1], BF16)
    zk_row = din("zk_row", [2, 128], BF16)
    ut128 = din("ut128", [128, 128], BF16)
    cio8 = din("cio8", [1, 8])

    y = nc.dram_tensor("y", [NTB, 128, D], F32, kind="ExternalOutput").ap()

    w1scr = dscr("w1scr", [NF1, 128, NKC * 128], BF16)
    w2scr = dscr("w2scr", [NF1, 128, D], BF16)
    x1scr = dscr("x1scr", [NTB, 128, D], F32)
    x1Tscr = dscr("x1Tscr", [NKC, 128, SOWN], BF16)
    x1g = dscr("x1g", [NE * CAP, D], BF16)
    yg = dscr("yg", [NE * CAP, D], BF16)

    s = Sched()
    s.sem(*SEMS)

    es = ExitStack()
    ctx = {}
    with es:
        def sb(name, shape, dt=F32):
            h = es.enter_context(nc.sbuf_tensor(name, shape, dt))
            return h

        identF_sb = sb("identF_sb", [128, 128])
        identB_sb = sb("identB_sb", [128, 128], BF16)
        sel_sb = sb("sel_sb", [16, NE * 128])
        zk_sb = sb("zk_sb", [2, 128], BF16)
        onesrow_sb = sb("onesrow_sb", [1, 128])
        onescol_sb = sb("onescol_sb", [128, 1], BF16)
        eps6 = sb("eps6", [128, 1]); eps5 = sb("eps5", [128, 1])
        statB = sb("statB", [2, SOWN], BF16)
        kvsb = sb("kvsb", [128, 512], BF16)
        KS = sb("KS", [128, NKC * 16], BF16)
        Lq = sb("Lq", [2, D], BF16)
        bo_row = sb("bo_row", [1, D]); b2_row = sb("b2_row", [1, D])
        b1p = sb("b1p", [128, NF1])
        ce_sb = sb("ce_sb", [128, NTB * NE])
        wr_bf = sb("wr_bf", [128, NKC * NE], BF16)
        smc = sb("smc", [128, 24])
        ut_sb = sb("ut_sb", [128, 128], BF16)
        cio_sb = sb("cio_sb", [1, 8])

        ps = es.enter_context(nc.psum_tensor("ps", [128, 3584], F32))

        scope_ab = ExitStack()

        def sab(name, shape, dt=F32):
            return scope_ab.enter_context(nc.sbuf_tensor(name, shape, dt))

        wq_bf = sab("wq_bf", [128, NKC * D], BF16)
        wo_bf = sab("wo_bf", [128, NKC * D], BF16)

        scope_a = ExitStack()

        def sba(name, shape, dt=F32):
            return scope_a.enter_context(nc.sbuf_tensor(name, shape, dt))

        wk_bf = sba("wk_bf", [128, NKC * D], BF16)
        wv_bf = sba("wv_bf", [128, NKC * D], BF16)
        stg0 = sba("stg0", [128, MLP])
        stg = [stg0, stg0]
        w1f = [sba(f"w1f{i}", [128, MLP], BF16) for i in range(2)]
        ln1s_sb = sba("ln1s_sb", [128, NKC]); ln1b_sb = sba("ln1b_sb", [128, NKC])
        ln2s_sb = sba("ln2s_sb", [128, NKC]); ln2b_sb = sba("ln2b_sb", [128, NKC])
        rat1 = sba("rat1", [128, NKC], BF16)
        rat2 = sba("rat2", [128, NKC], BF16)
        ratt = sba("ratt", [128, NKC])
        stat_sb = sba("stat_sb", [128, NTA * 4])
        ksr0 = sba("ksr0", [1, 512]); ksr1 = sba("ksr1", [1, 512])
        poolA = sba("poolA", [128, MLP])
        rowf = poolA[0:1, :]
        b1_sb = poolA[32:33, :]
        bk_sb = poolA[64:65, 0:D]
        bv_sb = poolA[64:65, D:2 * D]
        bq_sb = poolA[64:65, 2 * D:3 * D]
        poolB = sba("poolB", [128, MLP], BF16)
        Rk = poolB[0:2, 0:D]
        Rv = poolB[0:2, D:2 * D]
        rowb = poolB[32:33, :]
        rowc = poolB[64:65, :]
        junkA = sba("junkA", [128, D])
        xa = [sba(f"xa{i}", [128, D]) for i in range(3)]
        xtb = [sba(f"xtb{i}", [128, NKC * 128], BF16) for i in range(3)]
        phik = [sba(f"phik{i}", [128, D], BF16) for i in range(2)]
        vsb = [sba(f"vsb{i}", [128, D], BF16) for i in range(2)]
        tk = [sba(f"tk{i}", [128, D]) for i in range(2)]
        trin = [sba(f"trin{i}", [128, 4]) for i in range(2)]
        srowA = [sba(f"srowA{i}", [2, 128], BF16) for i in range(2)]

        for t_, src_ in [(identF_sb, identF), (identB_sb, identB), (zk_sb, zk_row),
                         (onesrow_sb, onesrow_f), (onescol_sb, onescol_b),
                         (ln1s_sb, ln1s), (ln1b_sb, ln1b),
                         (ln2s_sb, ln2s), (ln2b_sb, ln2b),
                         (bo_row, bor32), (b2_row, b2r32),
                         (bk_sb, bkr), (bv_sb, bvr), (bq_sb, bqr), (b1_sb, b1r32),
                         (ut_sb, ut128), (cio_sb, cio8)]:
            s.dma("sync", lambda e, t=t_, sr=src_: e.dma_start(out=t[:], in_=sr),
                  "cst", 0)
        s.dma("sync", lambda e: e.dma_start(
            out=sel_sb[:].rearrange("p (c f) -> p c f", c=NE),
            in_=sel.rearrange("c p f -> p c f")), "cst", 0)
        s.dma("gpsimd", lambda e: e.dma_start(
            out=wr_bf[:].rearrange("p (c f) -> p c f", c=NKC),
            in_=wr32.rearrange("c p f -> p c f")), "gwr", 0)
        CST = ("cst@0", s.val("cst@0"))

        def ratio_chain(dst_bf, lns_, lnb_, c0, c1):
            s.on("vector", lambda v: v.tensor_scalar(smc[:, c0:c0 + NKC], lns_[:],
                                                     0.0, None, ALU.is_equal),
                 waits=[CST], inc="fold")
            fv = s.val("fold")
            s.on("vector", lambda v: v.tensor_add(smc[:, c1:c1 + NKC], lns_[:],
                                                  smc[:, c0:c0 + NKC]),
                 waits=[("fold", fv)], force=True, inc="fold")
            fv = s.val("fold")
            s.on("vector", lambda v: v.reciprocal(ratt[:], smc[:, c1:c1 + NKC]),
                 waits=[("fold", fv)], force=True, inc="fold")
            fv = s.val("fold")
            s.on("vector", lambda v: v.tensor_mul(dst_bf[:], lnb_[:], ratt[:]),
                 waits=[("fold", fv)], force=True, inc="fold")

        ratio_chain(rat1, ln1s_sb, ln1b_sb, 0, 8)
        ratio_chain(rat2, ln2s_sb, ln2b_sb, 0, 8)
        s.on("vector", lambda v: v.memset(eps6[:], LNEPS))
        s.on("vector", lambda v: v.memset(eps5[:], PLNEPS))
        s.on("vector", lambda v: v.memset(KS[:], 0.0), inc="fold")
        RATF = s.val("fold")

        FOLD_K = None
        for wsrc, wdst, eng, bufs, sl0 in [(wk32, wk_bf, "sync", stg, 0),
                                           (wv32, wv_bf, "gpsimd", w1f, 2)]:
            base = s.val("fold")
            for kc in range(NKC):
                pbuf = bufs[kc % 2]
                ldv = s.dma(eng, lambda e, wsrc=wsrc, kc=kc, pbuf=pbuf:
                            e.dma_start(out=pbuf[:, 0:D], in_=wsrc[kc]),
                            "ldkv", sl0 + kc % 2,
                            waits=[("fold", 0 if kc < 2 else base + kc - 1)])
                s.on("vector", lambda v, wdst=wdst, kc=kc, pbuf=pbuf:
                     v.tensor_scalar(wdst[:, kc * D:(kc + 1) * D], pbuf[:, 0:D],
                                     ln1s_sb[:, kc:kc + 1], None, ALU.mult),
                     waits=[ldv, CST], inc="fold")
            if FOLD_K is None:
                FOLD_K = s.val("fold")
        FOLD_KV = s.val("fold")

        def build_R(wdst, brow_sb, Rdst, ratio_sb, neg0=False, extra_w=(),
                    fold_v=None):
            if fold_v is None:
                fold_v = FOLD_KV
            asm0 = s.dma_totals("rasm")
            for kc in range(NKC):
                for h in range(2):
                    s.on("tensor", lambda t, kc=kc, h=h, wdst=wdst: t.matmul(
                        ps[0:1, SM + h * 512:SM + (h + 1) * 512],
                        onescol_sb[:],
                        wdst[:, kc * D + h * 512:kc * D + (h + 1) * 512],
                        start=(kc == 0), stop=(kc == NKC - 1), skip_group_check=True),
                        waits=([("fold", fold_v)] + asm0 + list(extra_w))
                        if (kc == 0 and h == 0) else (), force=True)
            for kc in range(NKC):
                for h in range(2):
                    s.on("tensor", lambda t, kc=kc, h=h, wdst=wdst, rs=ratio_sb: t.matmul(
                        ps[32:33, SM + h * 512:SM + (h + 1) * 512],
                        rs[:, kc:kc + 1],
                        wdst[:, kc * D + h * 512:kc * D + (h + 1) * 512],
                        start=(kc == 0), stop=(kc == NKC - 1), skip_group_check=True,
                        tile_position=(0, 32)),
                        inc="rowmm" if (kc == NKC - 1 and h == 1) else None)
            mmv = s.val("rowmm")
            if neg0:
                s.on("vector", lambda v: v.tensor_scalar(
                    rowb[:, 0:D], ps[0:1, SM:SM + D], -1.0, None, ALU.mult),
                    waits=[("rowmm", mmv)] + asm0, force=True)
            else:
                s.on("vector", lambda v: v.tensor_copy(rowb[:, 0:D], ps[0:1, SM:SM + D]),
                     waits=[("rowmm", mmv)] + asm0, force=True)
            s.on("vector", lambda v, brow_sb=brow_sb: v.tensor_add(
                rowc[:, 0:D], ps[32:33, SM:SM + D], brow_sb[:]), inc="rowev")
            ev = s.val("rowev")
            s.dma("gpsimd", lambda e, Rdst=Rdst: e.dma_start(out=Rdst[0:1, :],
                                                             in_=rowb[:, 0:D]),
                  "rasm", 0, waits=[("rowev", ev)])
            s.dma("gpsimd", lambda e, Rdst=Rdst: e.dma_start(out=Rdst[1:2, :],
                                                             in_=rowc[:, 0:D]),
                  "rasm", 1)
            return s.dma_totals("rasm")

        build_R(wk_bf, bk_sb, Rk, rat1, fold_v=FOLD_K)
        RASM = build_R(wv_bf, bv_sb, Rv, rat1)

        s.dma("gpsimd", lambda e: e.dma_start(
            out=wo_bf[:].rearrange("p (c f) -> p c f", c=NKC),
            in_=wo32.rearrange("c p f -> p c f")), "gwo", 0)

        bg_jobs = []
        last_bgst = {}

        def bg_fold_job(idx, wsrc, kc, width, kind):
            def emitjob():
                pbuf = stg[idx % 2]
                ldv = s.dma("gpsimd",
                            lambda e: e.dma_start(out=pbuf[:, 0:width], in_=wsrc[kc]),
                            "bgld", idx % 2,
                            waits=[("bg_fold", s.val("bg_fold")), ("fold", FOLD_KV)])
                if kind == "wq":
                    s.on("vector", lambda v: v.tensor_scalar(
                        wq_bf[:, kc * D:(kc + 1) * D], pbuf[:, 0:D],
                        ln1s_sb[:, kc:kc + 1], None, ALU.mult),
                        waits=[ldv], inc="bg_fold")
                else:
                    wbuf = w1f[idx % 2]
                    s.on("vector", lambda v: v.tensor_scalar(
                        wbuf[:, 0:MLP], pbuf[:, 0:MLP],
                        ln2s_sb[:, kc:kc + 1], None, ALU.mult),
                        waits=[ldv, last_bgst.get(idx % 2, ("fold", 0))],
                        inc="bg_fold")
                    fv = s.val("bg_fold")
                    last_bgst[idx % 2] = s.dma("sync", lambda e: e.dma_start(
                        out=w1scr[:, :, kc * 128:(kc + 1) * 128].rearrange(
                            "f p c -> p f c"),
                        in_=wbuf[:, 0:MLP].rearrange("p (f c) -> p f c", c=128)),
                        "bgst", idx % 2, waits=[("bg_fold", fv)])
            return emitjob

        jidx = 0
        for kc in range(NKC):
            bg_jobs.append(bg_fold_job(jidx, wq32, kc, D, "wq")); jidx += 1
        for kc in range(NKC):
            bg_jobs.append(bg_fold_job(jidx, w132, kc, MLP, "w1")); jidx += 1

        s.sem("bgst@0", "bgst@1")
        W1ST_FINAL = [("bgst@0", 64), ("bgst@1", 64)]

        def b1_job(f1):
            def emit():
                ldv = s.dma("sync", lambda e, f1=f1: e.dma_start(
                    out=w1f[f1 % 2][:, 0:NKC * 128], in_=w1scr[f1]),
                    "w1rd", f1 % 2,
                    waits=[("b1mm", max(0, f1 - 1))] + W1ST_FINAL)
                for kc in range(NKC):
                    s.on("tensor", lambda t, f1=f1, kc=kc: t.matmul(
                        ps[0:1, SP1 + (f1 % 4) * 128:SP1 + (f1 % 4) * 128 + 128],
                        rat2[:, kc:kc + 1],
                        w1f[f1 % 2][:, kc * 128:(kc + 1) * 128],
                        start=(kc == 0), stop=(kc == NKC - 1),
                        skip_group_check=True),
                        waits=([ldv, ("b1ev", f1)]) if kc == 0 else (),
                        force=True,
                        inc="b1mm" if kc == NKC - 1 else None)
                mmv = s.val("b1mm")
                s.on("vector", lambda v, f1=f1: v.tensor_add(
                    rowf[:, f1 * 128:(f1 + 1) * 128],
                    ps[0:1, SP1 + (f1 % 4) * 128:SP1 + (f1 % 4) * 128 + 128],
                    b1_sb[:, f1 * 128:(f1 + 1) * 128]),
                    waits=[("b1mm", mmv)], force=True, inc="b1ev")
            return emit
        b1_jobs = [b1_job(f1) for f1 in range(NF1)]

        free_gk = ("fold", 0)
        free_gv = ("fold", 0)
        free_sm_dve = ("fold", 0)
        last_phik = {}; last_vsb = {}; last_xa = {}; last_xtb = {}
        last_trin = {}; last_srowA_use = {}; last_tk = {}

        dmas = {}; trin_ready = {}

        def emit_dma(t):
            axa_v = s.dma("sync", lambda e, t=t: e.dma_start(
                out=xa[t % 3][:], in_=x_all[t]), "axa", t % 3,
                waits=list(last_xa.get(t % 3, ())))
            axt_v = s.dma("gpsimd", lambda e, t=t: e.dma_start(
                out=xtb[t % 3][:].rearrange("p (c f) -> p c f", c=NKC),
                in_=xT_all[:, :, t * 128:(t + 1) * 128].rearrange("c p f -> p c f")),
                "axt", t % 3, waits=[last_xtb.get(t % 3, ("fold", 0))])
            dmas[t] = (axa_v, axt_v)

        def emit_stats(t):
            own = t < NTB
            axa_v = dmas[t][0]
            s.on("vector", lambda v, t=t: v.tensor_scalar(
                junkA[:], xa[t % 3][:], 1.0, None, ALU.mult, ALU.add,
                accum_out=stat_sb[:, 4 * t + 3:4 * t + 4]),
                waits=[axa_v], inc="ssum")
            ssum_v = s.val("ssum")
            s.on("scalar", lambda sc, t=t: sc.activation(
                tk[t % 2][:], xa[t % 3][:], AF.Square,
                accum_out=smc[:, 12 + t % 2:13 + t % 2]),
                waits=[axa_v, last_tk.get(t % 2, ("fold", 0))],
                inc="sssq")
            sssq_v = s.val("sssq")
            s.on("vector", lambda v, t=t: v.tensor_scalar(
                stat_sb[:, 4 * t:4 * t + 1], stat_sb[:, 4 * t + 3:4 * t + 4],
                1.0 / D, None, ALU.mult),
                waits=[("ssum", ssum_v)], force=True, inc="sch")
            s.on("vector", lambda v, t=t: v.tensor_mul(
                smc[:, 14 + t % 2:15 + t % 2], stat_sb[:, 4 * t:4 * t + 1],
                stat_sb[:, 4 * t:4 * t + 1]),
                waits=[("sch", s.val("sch"))], force=True, inc="sch")
            s.on("vector", lambda v, t=t: v.scalar_tensor_tensor(
                stat_sb[:, 4 * t + 3:4 * t + 4], smc[:, 12 + t % 2:13 + t % 2],
                1.0 / D, smc[:, 14 + t % 2:15 + t % 2], ALU.mult, ALU.subtract),
                waits=[("sssq", sssq_v), ("sch", s.val("sch"))], force=True,
                inc="sch")
            schv = s.val("sch")
            s.on("scalar", lambda sc, t=t: sc.activation(
                smc[:, 16 + t % 2:17 + t % 2], stat_sb[:, 4 * t + 3:4 * t + 4],
                AF.Ln, bias=eps6[:]),
                waits=[("sch", schv)], inc="sch")
            s.on("scalar", lambda sc, t=t: sc.activation(
                stat_sb[:, 4 * t + 1:4 * t + 2], smc[:, 16 + t % 2:17 + t % 2],
                AF.Exp, scale=-0.5),
                waits=[("sch", s.val("sch"))], force=True, inc="sch")
            s.on("scalar", lambda sc, t=t: sc.activation(
                stat_sb[:, 4 * t + 2:4 * t + 3], smc[:, 16 + t % 2:17 + t % 2],
                AF.Exp, scale=0.5), inc="sch")
            schv2 = s.val("sch")
            s.on("vector", lambda v, t=t: v.tensor_scalar(
                trin[t % 2][:, 0:1], stat_sb[:, 4 * t:4 * t + 1], -1.0, None,
                ALU.mult),
                waits=[("sch", schv2), last_trin.get(t % 2, ("fold", 0))],
                force=True)
            s.on("vector", lambda v, t=t: v.tensor_copy(
                trin[t % 2][:, 1:2], stat_sb[:, 4 * t + 2:4 * t + 3]))
            if own:
                s.on("vector", lambda v, t=t: v.tensor_copy(
                    trin[t % 2][:, 2:3], stat_sb[:, 4 * t:4 * t + 1]))
                s.on("vector", lambda v, t=t: v.tensor_copy(
                    trin[t % 2][:, 3:4], stat_sb[:, 4 * t + 2:4 * t + 3]),
                    inc="sch")
            else:
                s.on("vector", lambda v: v.memset(smc[:, 20:21], 0.0),
                     inc="sch")
            trin_ready[t] = s.val("sch")
            last_xa[t % 3] = [("sssq", sssq_v), ("ssum", ssum_v)]

        def emit_rest(t):
            own = t < NTB
            nonlocal free_gk, free_gv, free_sm_dve
            axt_v = dmas[t][1]
            s.on("tensor", lambda tt, t=t: tt.transpose(
                ps[0:2, SM:SM + 128], trin[t % 2][:, 0:2],
                identF_sb[:]),
                waits=[("sch", trin_ready[t]), free_sm_dve] + RASM,
                force=True, inc="tr")
            if own:
                s.on("tensor", lambda tt, t=t: tt.transpose(
                    ps[0:2, SM + 128:SM + 256], trin[t % 2][:, 2:4],
                    identF_sb[:]), inc="tr")
            tr_v = s.val("tr")

            kw = [axt_v, ("fold", FOLD_K), free_gk]
            for kc in range(NKC):
                for h in range(2):
                    s.on("tensor", lambda tt, t=t, kc=kc, h=h: tt.matmul(
                        ps[:, GK + h * 512:GK + (h + 1) * 512],
                        xtb[t % 3][:, kc * 128:(kc + 1) * 128],
                        wk_bf[:, kc * D + h * 512:kc * D + (h + 1) * 512],
                        start=(kc == 0), stop=False, skip_group_check=True),
                        waits=kw if (kc == 0 and h == 0) else (), force=True)
            vw = [free_gv, ("fold", FOLD_KV)]
            for kc in range(NKC):
                for h in range(2):
                    s.on("tensor", lambda tt, t=t, kc=kc, h=h: tt.matmul(
                        ps[:, GV + h * 512:GV + (h + 1) * 512],
                        xtb[t % 3][:, kc * 128:(kc + 1) * 128],
                        wv_bf[:, kc * D + h * 512:kc * D + (h + 1) * 512],
                        start=(kc == 0), stop=False, skip_group_check=True),
                        waits=vw if (kc == 0 and h == 0) else (), force=True)

            s.on("vector", lambda v, t=t: v.tensor_copy(
                srowA[t % 2][:], ps[0:2, SM:SM + 128]),
                waits=[("tr", tr_v), last_srowA_use.get(t % 2, ("fold", 0))],
                force=True)
            if own:
                s.on("vector", lambda v, t=t: v.tensor_copy(
                    statB[:, t * 128:(t + 1) * 128],
                    ps[0:2, SM + 128:SM + 256]), inc="trc")
            else:
                s.on("vector", lambda v: v.memset(smc[:, 20:21], 0.0),
                     inc="trc")
            trc_v = s.val("trc")
            free_sm_dve = ("trc", trc_v)

            for dst, R, incn in ((GK, Rk, "kmm"), (GV, Rv, "vmm")):
                for h in range(2):
                    s.on("tensor", lambda tt, t=t, dst=dst, R=R, h=h: tt.matmul(
                        ps[:, dst + h * 512:dst + (h + 1) * 512],
                        srowA[t % 2][:], R[:, h * 512:(h + 1) * 512],
                        start=False, stop=True, skip_group_check=True),
                        waits=[("trc", trc_v)] if (h == 0 and dst == GK) else (),
                        force=True, inc=incn if h == 1 else None)
            kmm_v = s.val("kmm"); vmm_v = s.val("vmm")
            last_srowA_use[t % 2] = ("vmm", vmm_v)
            last_trin[t % 2] = ("tr", tr_v)

            s.on("scalar", lambda sc, t=t: sc.activation(
                tk[t % 2][:], ps[:, GK:GK + D], AF.Relu,
                scale=stat_sb[:, 4 * t + 1:4 * t + 2]),
                waits=[("kmm", kmm_v)], inc="epik")
            s.on("vector", lambda v, t=t: v.tensor_scalar(
                phik[t % 2][:], tk[t % 2][:], KEPS, None, ALU.add),
                waits=[("epik", s.val("epik")), last_phik.get(t % 2, ("fold", 0))],
                force=True, inc="epik")
            epik_v = s.val("epik")
            free_gk = ("epik", epik_v)
            last_tk[t % 2] = ("epik", epik_v)
            s.on("vector", lambda v, t=t: v.tensor_scalar(
                vsb[t % 2][:], ps[:, GV:GV + D],
                stat_sb[:, 4 * t + 1:4 * t + 2], None, ALU.mult),
                waits=[("vmm", vmm_v), last_vsb.get(t % 2, ("fold", 0))],
                force=True, inc="epiv")
            epiv_v = s.val("epiv")
            free_gv = ("epiv", epiv_v)
            last_xtb[t % 3] = ("vmm", vmm_v)

            for p in range(8):
                s.on("tensor", lambda tt, t=t, p=p: tt.matmul(
                    ps[0:64, KVC + p * 64:KVC + (p + 1) * 64],
                    phik[t % 2][:, p * 128:p * 128 + 64],
                    vsb[t % 2][:, (2 * p) * 64:(2 * p + 1) * 64],
                    start=(t == 0 and p == 0), stop=(t == NTA - 1),
                    skip_group_check=True),
                    waits=[("epik", epik_v), ("epiv", epiv_v)] if p == 0 else (),
                    force=True)
                s.on("tensor", lambda tt, t=t, p=p: tt.matmul(
                    ps[64:128, KVC + p * 64:KVC + (p + 1) * 64],
                    phik[t % 2][:, p * 128 + 64:(p + 1) * 128],
                    vsb[t % 2][:, (2 * p + 1) * 64:(2 * p + 2) * 64],
                    start=(t == 0 and p == 0), stop=(t == NTA - 1),
                    skip_group_check=True, tile_position=(0, 64)))
            s.on("tensor", lambda tt, t=t: tt.matmul(
                ps[32:33, SM:SM + 512], onescol_sb[:], phik[t % 2][:, 0:512],
                start=(t == 0), stop=(t == NTA - 1), skip_group_check=True,
                tile_position=(0, 32)))
            s.on("tensor", lambda tt, t=t: tt.matmul(
                ps[64:65, SM:SM + 512], onescol_sb[:], phik[t % 2][:, 512:1024],
                start=(t == 0), stop=(t == NTA - 1), skip_group_check=True,
                tile_position=(0, 64)), inc="kvmm")
            kv_done = s.val("kvmm")
            last_phik[t % 2] = ("kvmm", kv_done)
            last_vsb[t % 2] = ("kvmm", kv_done)

        emit_dma(0)
        emit_dma(1)
        emit_stats(0)
        for t in range(NTA):
            if t + 2 < NTA:
                emit_dma(t + 2)
            if t + 1 < NTA:
                emit_stats(t + 1)
            emit_rest(t)
            if t % 3 == 2 and bg_jobs:
                bg_jobs.pop(0)()
            if t % 2 == 0 and t // 2 < NF1:
                s.dma("gpsimd", lambda e, f1c=t // 2: e.dma_start(
                    out=w2scr[f1c], in_=w232[f1c]), "w2c", 0)
            if t >= 48:
                for _ in range(2):
                    if b1_jobs:
                        b1_jobs.pop(0)()

        while bg_jobs:
            bg_jobs.pop(0)()
        while b1_jobs:
            b1_jobs.pop(0)()

        KV_ALL = s.val("kvmm")
        s.on("vector", lambda v: v.tensor_copy(kvsb[:], ps[:, KVC:KVC + 512]),
             waits=[("kvmm", KV_ALL)], force=True)
        s.on("vector", lambda v: v.tensor_copy(ksr0[:], ps[32:33, SM:SM + 512]))
        s.on("vector", lambda v: v.tensor_copy(ksr1[:], ps[64:65, SM:SM + 512]),
             inc="tev")
        TEV = s.val("tev")
        for c in range(NKC):
            src = ksr0 if c < 4 else ksr1
            off = (c % 4) * 128
            pcol = GK + c * 64
            s.on("tensor", lambda tt, c=c, src=src, off=off, pcol=pcol: tt.transpose(
                ps[0:128, pcol:pcol + 1], src[:, off:off + 128],
                identF_sb[0:1, 0:1]),
                waits=[("tev", TEV)] if c == 0 else (), force=True, inc="tks")
        TKS = s.val("tks")
        for c in range(NKC):
            pcol = GK + c * 64
            s.on("vector", lambda v, c=c, pcol=pcol: v.tensor_copy(
                KS[0:64, c * 16 + 2 * c:c * 16 + 2 * c + 1], ps[0:64, pcol:pcol + 1]),
                waits=[("tks", TKS)], force=True)
            s.on("vector", lambda v, c=c, pcol=pcol: v.tensor_copy(
                KS[64:128, c * 16 + 2 * c + 1:c * 16 + 2 * c + 2],
                ps[64:128, pcol:pcol + 1]),
                inc="tkc")
        TKC = s.val("tkc")

        BGF = s.val("bg_fold")
        RASM2 = build_R(wq_bf, bq_sb, Lq, rat1, neg0=True,
                        extra_w=[("bg_fold", BGF), ("tkc", TKC)])

        W1ST = s.dma_totals("bgst")
        B1EV = s.val("b1ev")
        for f1 in range(NF1):
            s.on("tensor", lambda tt, f1=f1: tt.transpose(
                ps[0:128, GK + f1 * 32:GK + f1 * 32 + 1],
                rowf[:, f1 * 128:(f1 + 1) * 128], identF_sb[0:1, 0:1]),
                waits=([("b1ev", B1EV), ("tkc", TKC)] if f1 == 0 else ()),
                force=True, inc="b1tr" if f1 == NF1 - 1 else None)
        B1TR = s.val("b1tr")
        s.on("vector", lambda v: v.tensor_copy(
            b1p[:, 0:NF1], ps[0:128, GK:GK + NF1 * 32:32]),
            waits=[("b1tr", B1TR)], force=True, inc="b1tc")
        B1TC = s.val("b1tc")
        W2CASTS = s.dma_totals("w2c")

        BARB = ([("kvmm", KV_ALL), ("sssq", s.val("sssq")),
                 ("epik", s.val("epik")), ("epiv", s.val("epiv")),
                 ("b1mm", s.val("b1mm")), ("b1ev", B1EV), ("b1tc", B1TC),
                 ("tkc", TKC), ("tr", s.val("tr")), ("vmm", s.val("vmm"))]
                + RASM2 + W1ST + s.dma_totals("gwo") + s.dma_totals("gwr"))

        ctx.update(locals())
        if stop_phase == "A":
            dbgs = [("dbg_kv", kvsb, [128, 512], BF16),
                    ("dbg_KS", KS, [128, NKC * 16], BF16),
                    ("dbg_statB", statB, [2, SOWN], BF16),
                    ("dbg_stat", stat_sb, [128, NTA * 4], F32),
                    ("dbg_Rk", Rk, [2, D], BF16),
                    ("dbg_Rv", Rv, [2, D], BF16),
                    ("dbg_Lq", Lq, [2, D], BF16),
                    ("dbg_b1p", b1p, [128, NF1], F32),
                    ("dbg_wq", wq_bf, [128, NKC * D], BF16),
                    ("dbg_phik", phik[1], [128, D], BF16),
                    ("dbg_vsb", vsb[1], [128, D], BF16)]
            for name, sbt, shape, dt in dbgs:
                dout = nc.dram_tensor(name, shape, dt, kind="ExternalOutput").ap()
                s.dma("sync", lambda e, dout=dout, sbt=sbt: e.dma_start(
                    out=dout, in_=sbt[:]), "dbgst", 0, waits=BARB)
            s.on("sync", lambda e: e.nop(),
                 waits=BARB + s.all_dma_totals(), force=True)
            semh = {name: es.enter_context(nc.semaphore(name.replace("@", "_")))
                    for name in s.used_sems()}
            s.emit(nc, semh)
            scope_a.close(); scope_ab.close()
            return nc, s

        scope_a.close()
        ctx["scope_ab"] = scope_ab
        build_phase_b(nc, s, ctx)
        if stop_phase == "B":
            s.on("sync", lambda e: e.nop(),
                 waits=ctx["BARF"] + s.all_dma_totals(), force=True)
            semh = {name: es.enter_context(nc.semaphore(name.replace("@", "_")))
                    for name in s.used_sems()}
            s.emit(nc, semh)
            ctx["scope_b"].close()
            scope_ab.close()
            return nc, s
        build_ffn(nc, s, ctx)
        semh = {name: es.enter_context(nc.semaphore(name.replace("@", "_")))
                    for name in s.used_sems()}
        s.emit(nc, semh)
        ctx["scope_f"].close()
        scope_ab2 = None
    return nc, s


def build_phase_b(nc, s, ctx):
    ps = ctx["ps"]; es = ctx["es"]
    identF_sb = ctx["identF_sb"]; identB_sb = ctx["identB_sb"]
    sel_sb = ctx["sel_sb"]; zk_sb = ctx["zk_sb"]
    onesrow_sb = ctx["onesrow_sb"]; statB = ctx["statB"]; kvsb = ctx["kvsb"]
    KS = ctx["KS"]; Lq = ctx["Lq"]; bo_row = ctx["bo_row"]; b2_row = ctx["b2_row"]
    b1p = ctx["b1p"]; eps6 = ctx["eps6"]; eps5 = ctx["eps5"]
    wq_bf = ctx["wq_bf"]; wo_bf = ctx["wo_bf"]
    xT_all = ctx["xT_all"]; x_all = ctx["x_all"]
    w1scr = ctx["w1scr"]; w2scr = ctx["w2scr"]; W2CASTS = ctx["W2CASTS"]
    x1scr = ctx["x1scr"]; x1Tscr = ctx["x1Tscr"]
    BARB = ctx["BARB"]; smc = ctx["smc"]

    scope_b = ExitStack()
    ctx["scope_b"] = scope_b

    def sbb(name, shape, dt=F32):
        return scope_b.enter_context(nc.sbuf_tensor(name, shape, dt))

    xtbg = [sbb("xtbg0", [128, NKC * 512], BF16)] * 2
    xaB = [sbb(f"xaB{i}", [128, D]) for i in range(4)]
    tq = [sbb(f"tq{i}", [128, 512], BF16) for i in range(2)]
    phtil = sbb("phtil", [128, NKC * 512], BF16)
    attnT = sbb("attnT", [128, NKC * 512], BF16)
    epsbc = sbb("epsbc", [128, 512], BF16)
    recd = sbb("recd", [16, 512])
    bcsb = [sbb(f"bcsb{i}", [128, 512], BF16) for i in range(2)]
    xresB = [sbb(f"xresB{i}", [128, D]) for i in range(4)]
    ln2xb = [sbb(f"ln2xb{i}", [128, D], BF16) for i in range(2)]
    ln2xT = sbb("ln2xT", [128, NKC * 512], BF16)
    h1T = sbb("h1T", [128, NF1 * 512], BF16)
    w1r = [sbb(f"w1r{i}", [128, NKC * 128], BF16) for i in range(3)]
    w2r = [sbb(f"w2r{i}", [128, 512], BF16) for i in range(4)]
    x1f = [sbb(f"x1f{i}", [128, D]) for i in range(2)]
    x1bfB = [sbb(f"x1bfB{i}", [128, D], BF16) for i in range(4)]
    x1tb = [sbb(f"x1tb{i}", [128, NKC * 128], BF16) for i in range(2)]
    zbuf = [sbb(f"zbuf{i}", [128, D]) for i in range(2)]
    junkB = sbb("junkB", [128, D])
    statC = sbb("statC", [128, 16])
    psb = scope_b.enter_context(nc.psum_tensor("psb", [128, 1024], BF16))
    ctx["psb"] = psb

    singles = [0, SP1]
    pairs = [512, 1536]
    free_single = {0: BARB[0], SP1: ("b1mm", s.val("b1mm"))}
    free_pair = {512: ("b1tc", s.val("b1tc")), 1536: ("b1tc", s.val("b1tc"))}
    free_den = ("tev", s.val("tev"))
    free_psb = ("fold", 0)

    last_xtbg = {}; last_xaB = {}; last_tq = {}; last_w1r = {}; last_w2r = {}
    last_ln2xb = {}; last_x1f = {}; last_x1bf = {}; last_zbuf = {}
    last_x1tb = {}; last_bcsb = {}
    last_attnT = ("fold", 0)
    prev_atn = ("fold", 0)
    pend_x1 = []

    def make_x1trans(w, t, x1bv):
        def emit():
            nonlocal free_psb
            for j in range(NKC):
                s.on("tensor", lambda tt, w=w, j=j: tt.transpose(
                    psb[:, j * 128:(j + 1) * 128],
                    x1bfB[w][:, j * 128:(j + 1) * 128], identB_sb[:]),
                    waits=[("x1bfc", x1bv), free_psb] if j == 0 else (),
                    force=True, inc="trb" if j == NKC - 1 else None)
            trbv = s.val("trb")
            last_x1bf[w] = ("trb", trbv)
            for j in range(NKC):
                s.on("vector", lambda v, w=w, j=j: v.tensor_copy(
                    x1tb[w % 2][:, j * 128:(j + 1) * 128],
                    psb[:, j * 128:(j + 1) * 128]),
                    waits=([("trb", trbv), last_x1tb.get(w % 2, ("fold", 0))])
                    if j == 0 else (),
                    force=True, inc="trbc" if j == NKC - 1 else None)
            trbcv = s.val("trbc")
            free_psb = ("trbc", trbcv)
            last_x1tb[w % 2] = s.dma("sync", lambda e, t=t, w=w: e.dma_start(
                out=x1Tscr[:, :, t * 128:(t + 1) * 128].rearrange("c p f -> p c f"),
                in_=x1tb[w % 2][:].rearrange("p (c f) -> p c f", c=NKC)),
                "x1tst", w % 2, waits=[("trbc", trbcv)])
        return emit
    prev_h1 = 0
    barrier = list(BARB)

    for g in range(NG):
        tw0 = 512 * g
        sing_i = [0]; pair_i = [0]

        def nsingle(extra=()):
            col = singles[sing_i[0] % 2]
            sing_i[0] += 1
            return col, [free_single[col]] + list(extra)

        def npair(extra=()):
            col = pairs[pair_i[0] % 2]
            pair_i[0] += 1
            return col, [free_pair[col]] + list(extra)

        bxt_v = s.dma("gpsimd", lambda e, g=g, tw0=tw0: e.dma_start(
            out=xtbg[g % 2][:].rearrange("p (c f) -> p c f", c=NKC),
            in_=xT_all[:, :, tw0:tw0 + 512].rearrange("c p f -> p c f")),
            "bxt", g % 2, waits=barrier + [last_xtbg.get(g % 2, ("fold", 0))])
        bxa_w = {}
        for w in range(GT):
            t = 4 * g + w
            bxa_w[w] = s.dma("sync", lambda e, t=t, w=w: e.dma_start(
                out=xaB[w][:], in_=x_all[t]),
                "bxa", w, waits=barrier + [last_xaB.get(w, ("fold", 0))])

        col_e, we = nsingle()
        s.on("tensor", lambda tt, tw0=tw0, col_e=col_e: tt.matmul(
            ps[:, col_e:col_e + 512], zk_sb[:], statB[:, tw0:tw0 + 512],
            start=True, stop=True, skip_group_check=True),
            waits=barrier + we, inc="qmm")
        s.on("vector", lambda v, col_e=col_e: v.tensor_copy(
            epsbc[:], ps[:, col_e:col_e + 512]),
            waits=barrier + [("qmm", s.val("qmm"))], force=True, inc="phv")
        epsbc_v = s.val("phv")
        free_single[col_e] = ("phv", epsbc_v)

        for fc in range(NKC):
            col, qw = nsingle([bxt_v])
            for kc in range(NKC):
                s.on("tensor", lambda tt, g=g, fc=fc, kc=kc, col=col: tt.matmul(
                    ps[:, col:col + 512],
                    wq_bf[:, kc * D + fc * 128:kc * D + (fc + 1) * 128],
                    xtbg[g % 2][:, kc * 512:(kc + 1) * 512],
                    start=(kc == 0), stop=False, skip_group_check=True),
                    waits=qw if kc == 0 else (), force=True)
            s.on("tensor", lambda tt, tw0=tw0, fc=fc, col=col: tt.matmul(
                ps[:, col:col + 512], Lq[:, fc * 128:(fc + 1) * 128],
                statB[:, tw0:tw0 + 512],
                start=False, stop=True, skip_group_check=True), inc="qmm")
            qv = s.val("qmm")
            s.on("scalar", lambda sc, fc=fc, col=col: sc.activation(
                tq[fc % 2][:], ps[:, col:col + 512], AF.Relu),
                waits=[("qmm", qv), last_tq.get(fc % 2, ("fold", 0))], force=True,
                inc="phr")
            prv = s.val("phr")
            s.on("vector", lambda v, fc=fc: v.tensor_add(
                phtil[:, fc * 512:(fc + 1) * 512], tq[fc % 2][:], epsbc[:]),
                waits=[("phr", prv), ("phv", epsbc_v), prev_atn], force=True,
                inc="phv")
            pv = s.val("phv")
            free_single[col] = ("phv", pv)
            last_tq[fc % 2] = ("phv", pv)
        PHV = s.val("phv")
        last_xtbg[g % 2] = ("qmm", s.val("qmm"))
        while pend_x1:
            pend_x1.pop(0)()

        for c in range(NKC):
            s.on("tensor", lambda tt, c=c: tt.matmul(
                ps[0:16, SM:SM + 512], KS[:, c * 16:(c + 1) * 16],
                phtil[:, c * 512:(c + 1) * 512],
                start=(c == 0), stop=(c == NKC - 1), skip_group_check=True),
                waits=[("phv", PHV), free_den] if c == 0 else (), force=True,
                inc="dnmm" if c == NKC - 1 else None)
        dnv = s.val("dnmm")
        s.on("vector", lambda v: v.reciprocal(recd[:], ps[0:16, SM:SM + 512]),
             waits=[("dnmm", dnv)], force=True, inc="rcp")
        rcpv = s.val("rcp")
        free_den = ("rcp", rcpv)

        for c in range(NKC):
            col, nw = npair([("phv", PHV)])
            s.on("tensor", lambda tt, c=c, col=col: tt.matmul(
                ps[0:64, col:col + 512], kvsb[0:64, c * 64:(c + 1) * 64],
                phtil[0:64, c * 512:(c + 1) * 512],
                start=True, stop=True, skip_group_check=True),
                waits=nw, force=True)
            s.on("tensor", lambda tt, c=c, col=col: tt.matmul(
                ps[64:128, col:col + 512], kvsb[64:128, c * 64:(c + 1) * 64],
                phtil[64:128, c * 512:(c + 1) * 512],
                start=True, stop=True, skip_group_check=True,
                tile_position=(64, 64)))
            s.on("tensor", lambda tt, c=c, col=col: tt.matmul(
                ps[:, col + 512:col + 1024], sel_sb[:, c * 128:(c + 1) * 128],
                recd[:], start=True, stop=True, skip_group_check=True),
                waits=[("rcp", rcpv)], force=True, inc="bcmm")
            bcv = s.val("bcmm")
            s.on("vector", lambda v, c=c, col=col: v.tensor_copy(
                bcsb[c % 2][:], ps[:, col + 512:col + 1024]),
                waits=[("bcmm", bcv), last_bcsb.get(c % 2, ("fold", 0))],
                force=True, inc="bcev")
            s.on("vector", lambda v, c=c, col=col: v.tensor_mul(
                attnT[:, c * 512:(c + 1) * 512], ps[:, col:col + 512],
                bcsb[c % 2][:]),
                waits=[("bcev", s.val("bcev"))] + ([last_attnT] if c == 0 else []),
                force=True, inc="atn")
            av = s.val("atn")
            free_pair[col] = ("atn", av)
            last_bcsb[c % 2] = ("atn", av)
        ATN = s.val("atn")
        prev_atn = ("atn", ATN)

        ln2_done = {}

        def emit_trb(w):
            nonlocal free_psb
            lnv = ln2_done[w]
            for j in range(NKC):
                s.on("tensor", lambda tt, w=w, j=j: tt.transpose(
                    psb[:, j * 128:(j + 1) * 128],
                    ln2xb[w % 2][:, j * 128:(j + 1) * 128], identB_sb[:]),
                    waits=[("ln2", lnv), free_psb] if j == 0 else (), force=True,
                    inc="trb" if j == NKC - 1 else None)
            trbv = s.val("trb")
            for j in range(NKC):
                s.on("vector", lambda v, w=w, j=j: v.tensor_copy(
                    ln2xT[:, j * 512 + w * 128:j * 512 + (w + 1) * 128],
                    psb[:, j * 128:(j + 1) * 128]),
                    waits=([("trb", trbv), ("h1", prev_h1)]) if j == 0 else (),
                    force=True, inc="trbc" if j == NKC - 1 else None)
            trbcv = s.val("trbc")
            free_psb = ("trbc", trbcv)
            last_ln2xb[w % 2] = ("trbc", trbcv)

        for w in range(GT):
            t = 4 * g + w
            col, ww = npair([("atn", ATN)])
            for c in range(NKC):
                for h in range(2):
                    s.on("tensor", lambda tt, c=c, h=h, w=w, col=col: tt.matmul(
                        ps[:, col + h * 512:col + (h + 1) * 512],
                        attnT[:, c * 512 + w * 128:c * 512 + (w + 1) * 128],
                        wo_bf[:, c * D + h * 512:c * D + (h + 1) * 512],
                        start=(c == 0), stop=False, skip_group_check=True),
                        waits=ww if (c == 0 and h == 0) else (), force=True)
            for h in range(2):
                s.on("tensor", lambda tt, h=h, col=col: tt.matmul(
                    ps[:, col + h * 512:col + (h + 1) * 512], onesrow_sb[:],
                    bo_row[:, h * 512:(h + 1) * 512],
                    start=False, stop=True, skip_group_check=True),
                    inc="womm" if h == 1 else None)
            wov = s.val("womm")
            s.on("vector", lambda v, w=w, col=col: v.scalar_tensor_tensor(
                xresB[w][:], ps[:, col:col + D], 1.0, xaB[w][:],
                ALU.mult, ALU.add, accum_out=statC[:, 0 + w % 2:1 + w % 2]),
                waits=[("womm", wov), bxa_w[w]], force=True, inc="xrs")
            xrv = s.val("xrs")
            free_pair[col] = ("xrs", xrv)
            last_xaB[w] = ("xrs", xrv)
            s.on("scalar", lambda sc, w=w: sc.activation(
                junkB[:], xresB[w][:], AF.Square,
                accum_out=statC[:, 2 + w % 2:3 + w % 2]),
                waits=[("xrs", xrv)], inc="s2ssq")
            sqv = s.val("s2ssq")
            s.on("vector", lambda v, w=w: v.tensor_scalar(
                statC[:, 4 + w % 2:5 + w % 2], statC[:, 0 + w % 2:1 + w % 2],
                1.0 / D, None, ALU.mult),
                waits=[("xrs", xrv)], force=True, inc="s2ch")
            s.on("vector", lambda v, w=w: v.tensor_mul(
                smc[:, 18 + w % 2:19 + w % 2], statC[:, 4 + w % 2:5 + w % 2],
                statC[:, 4 + w % 2:5 + w % 2]),
                waits=[("s2ch", s.val("s2ch"))], force=True, inc="s2ch")
            s.on("vector", lambda v, w=w: v.scalar_tensor_tensor(
                statC[:, 6 + w % 2:7 + w % 2], statC[:, 2 + w % 2:3 + w % 2],
                1.0 / D, smc[:, 18 + w % 2:19 + w % 2], ALU.mult, ALU.subtract),
                waits=[("s2ssq", sqv), ("s2ch", s.val("s2ch"))], force=True,
                inc="s2ch")
            scv = s.val("s2ch")
            s.on("scalar", lambda sc, w=w: sc.activation(
                smc[:, 20 + w % 2:21 + w % 2], statC[:, 6 + w % 2:7 + w % 2],
                AF.Ln, bias=eps6[:]),
                waits=[("s2ch", scv)], inc="s2ch")
            s.on("scalar", lambda sc, w=w: sc.activation(
                statC[:, 6 + w % 2:7 + w % 2], smc[:, 20 + w % 2:21 + w % 2],
                AF.Exp, scale=-0.5),
                waits=[("s2ch", s.val("s2ch"))], force=True, inc="s2ch")
            scv = s.val("s2ch")
            s.on("vector", lambda v, w=w: v.tensor_scalar(
                ln2xb[w % 2][:], xresB[w][:], statC[:, 4 + w % 2:5 + w % 2],
                statC[:, 6 + w % 2:7 + w % 2], ALU.subtract, ALU.mult),
                waits=[("s2ch", scv), last_ln2xb.get(w % 2, ("fold", 0))],
                force=True, inc="ln2")
            ln2_done[w] = s.val("ln2")
            if w >= 1:
                emit_trb(w - 1)
        emit_trb(GT - 1)
        TRBC = s.val("trbc")
        LN2G = s.val("ln2")

        for f1 in range(NF1):
            wldv = s.dma("sync", lambda e, f1=f1: e.dma_start(
                out=w1r[f1 % 3][:], in_=w1scr[f1]),
                "w1ld", f1 % 3, waits=[last_w1r.get(f1 % 3, ("fold", 0))])
            col, w1w = nsingle([wldv, ("trbc", TRBC)])
            for kc in range(NKC):
                s.on("tensor", lambda tt, f1=f1, kc=kc, col=col: tt.matmul(
                    ps[:, col:col + 512],
                    w1r[f1 % 3][:, kc * 128:(kc + 1) * 128],
                    ln2xT[:, kc * 512:(kc + 1) * 512],
                    start=(kc == 0), stop=(kc == NKC - 1), skip_group_check=True),
                    waits=w1w if kc == 0 else (), force=True,
                    inc="ymm" if kc == NKC - 1 else None)
            mv = s.val("ymm")
            s.on("scalar", lambda sc, f1=f1, col=col: sc.activation(
                h1T[:, f1 * 512:(f1 + 1) * 512], ps[:, col:col + 512],
                AF.Gelu_apprx_tanh, bias=b1p[:, f1:f1 + 1]),
                waits=[("ymm", mv)], force=True, inc="h1")
            hv = s.val("h1")
            free_single[col] = ("h1", hv)
            last_w1r[f1 % 3] = ("h1", hv)
        H1V = s.val("h1")
        prev_h1 = H1V

        XRS4 = s.val("xrs")
        fh_evac = {}
        for h in range(2):
            w2u0 = s.val("w2u")
            for w in range(GT):
                col = 512 + 512 * w
                if h == 0:
                    bw = ([("xrs", XRS4), free_pair[512], free_pair[1536]]
                          if w == 0 else ())
                else:
                    bw = [fh_evac[(0, w)]]
                s.on("tensor", lambda tt, h=h, col=col: tt.matmul(
                    ps[:, col:col + 512], onesrow_sb[:],
                    b2_row[:, h * 512:(h + 1) * 512],
                    start=True, stop=False, skip_group_check=True),
                    waits=bw, force=True)
            for f1 in range(NF1):
                slot = f1 % 4
                eng = "sync" if f1 % 2 == 0 else "gpsimd"
                w2v = s.dma(eng, lambda e, f1=f1, slot=slot, h=h: e.dma_start(
                    out=w2r[slot][:], in_=w2scr[f1, :, h * 512:(h + 1) * 512]),
                    "w2f", slot,
                    waits=[last_w2r.get(slot, ("fold", 0))]
                    + (W2CASTS if (g == 0 and h == 0 and f1 < 3) else []))
                for w in range(GT):
                    col = 512 + 512 * w
                    s.on("tensor", lambda tt, f1=f1, w=w, col=col, slot=slot:
                         tt.matmul(
                             ps[:, col:col + 512],
                             h1T[:, f1 * 512 + w * 128:f1 * 512 + (w + 1) * 128],
                             w2r[slot][:, 0:512],
                             start=False, stop=(f1 == NF1 - 1),
                             skip_group_check=True),
                         waits=[w2v, ("h1", H1V)] if w == 0 else (), force=True,
                         inc=("w2u" if f1 == NF1 - 1 else
                              ("w2t" if w == GT - 1 else None)))
                if f1 == NF1 - 1:
                    last_w2r[slot] = ("w2u", s.val("w2u"))
                else:
                    last_w2r[slot] = ("w2t", s.val("w2t"))
            for w in range(GT):
                col = 512 + 512 * w
                s.on("vector", lambda v, w=w, col=col, h=h: v.tensor_add(
                    xresB[w][:, h * 512:(h + 1) * 512], ps[:, col:col + 512],
                    xresB[w][:, h * 512:(h + 1) * 512]),
                    waits=[("w2u", w2u0 + w + 1), ("ln2", LN2G)], force=True,
                    inc="zf")
                fh_evac[(h, w)] = ("zf", s.val("zf"))
        free_pair[512] = fh_evac[(1, 1)]
        free_pair[1536] = fh_evac[(1, 3)]

        if True:
            for w in range(GT):
                t = 4 * g + w
                s.on("vector", lambda v, w=w: v.scalar_tensor_tensor(
                    zbuf[w % 2][:], xresB[w][:], 1.0, xaB[w][:],
                    ALU.mult, ALU.add, accum_out=statC[:, 8 + w % 2:9 + w % 2]),
                    waits=[fh_evac[(1, w)], last_zbuf.get(w % 2, ("fold", 0))],
                    force=True, inc="zf")
                zv = s.val("zf")
                last_xaB[w] = ("zf", zv)
                s.on("scalar", lambda sc, w=w: sc.activation(
                    junkB[:], zbuf[w % 2][:], AF.Square,
                    accum_out=statC[:, 10 + w % 2:11 + w % 2]),
                    waits=[("zf", zv)], inc="s3ssq")
                sqv = s.val("s3ssq")
                s.on("vector", lambda v, w=w: v.tensor_scalar(
                    statC[:, 12 + w % 2:13 + w % 2], statC[:, 8 + w % 2:9 + w % 2],
                    1.0 / D, None, ALU.mult),
                    waits=[("zf", zv)], force=True, inc="s3ch")
                s.on("vector", lambda v, w=w: v.tensor_mul(
                    smc[:, 6 + w % 2:7 + w % 2], statC[:, 12 + w % 2:13 + w % 2],
                    statC[:, 12 + w % 2:13 + w % 2]),
                    waits=[("s3ch", s.val("s3ch"))], force=True, inc="s3ch")
                s.on("vector", lambda v, w=w: v.scalar_tensor_tensor(
                    statC[:, 14 + w % 2:15 + w % 2],
                    statC[:, 10 + w % 2:11 + w % 2],
                    1.0 / D, smc[:, 6 + w % 2:7 + w % 2], ALU.mult, ALU.subtract),
                    waits=[("s3ssq", sqv), ("s3ch", s.val("s3ch"))], force=True,
                    inc="s3ch")
                scv = s.val("s3ch")
                s.on("scalar", lambda sc, w=w: sc.activation(
                    smc[:, 8 + w % 2:9 + w % 2], statC[:, 14 + w % 2:15 + w % 2],
                    AF.Ln, bias=eps5[:]),
                    waits=[("s3ch", scv)], inc="s3ch")
                s.on("scalar", lambda sc, w=w: sc.activation(
                    statC[:, 14 + w % 2:15 + w % 2], smc[:, 8 + w % 2:9 + w % 2],
                    AF.Exp, scale=-0.5),
                    waits=[("s3ch", s.val("s3ch"))], force=True, inc="s3ch")
                scv = s.val("s3ch")
                s.on("vector", lambda v, w=w: v.tensor_scalar(
                    x1f[w % 2][:], zbuf[w % 2][:], statC[:, 12 + w % 2:13 + w % 2],
                    statC[:, 14 + w % 2:15 + w % 2], ALU.subtract, ALU.mult),
                    waits=[("s3ch", scv), last_x1f.get(w % 2, ("fold", 0))],
                    force=True, inc="zf2")
                z2v = s.val("zf2")
                last_zbuf[w % 2] = ("zf2", z2v)
                stv = s.dma("sync", lambda e, t=t, w=w: e.dma_start(
                    out=x1scr[t], in_=x1f[w % 2][:]),
                    "x1st", w % 2, waits=[("zf2", z2v)])
                s.on("vector", lambda v, w=w: v.tensor_copy(
                    x1bfB[w][:], x1f[w % 2][:]),
                    waits=[("zf2", z2v), stv, last_x1bf.get(w, ("fold", 0))],
                    force=True, inc="x1bfc")
                x1bv = s.val("x1bfc")
                last_x1f[w % 2] = ("x1bfc", x1bv)
                pend_x1.append(make_x1trans(w, t, x1bv))
        last_attnT = ("womm", s.val("womm"))
    while pend_x1:
        pend_x1.pop(0)()

    ctx["BARF"] = ([("zf2", s.val("zf2")), ("trbc", s.val("trbc")),
                    ("w2u", s.val("w2u")), ("w2t", s.val("w2t")),
                    ("h1", s.val("h1")), ("zf", s.val("zf")),
                    ("atn", s.val("atn")), ("qmm", s.val("qmm")),
                    ("s3ssq", s.val("s3ssq")), ("ymm", s.val("ymm"))]
                   + s.dma_totals("x1tst") + s.dma_totals("x1st")
                   + s.dma_totals("w1ld") + s.dma_totals("w2f")
                   + s.dma_totals("bxa") + s.dma_totals("bxt"))


def build_ffn(nc, s, ctx):
    ctx["scope_b"].close()
    ctx["scope_ab"].close()
    ps = ctx["ps"]; es = ctx["es"]
    identF_sb = ctx["identF_sb"]; identB_sb = ctx["identB_sb"]
    onesrow_sb = ctx["onesrow_sb"]; onescol_sb = ctx["onescol_sb"]
    eps5 = ctx["eps5"]; smc = ctx["smc"]
    wr_bf = ctx["wr_bf"]
    ut_sb = ctx["ut_sb"]; cio_sb = ctx["cio_sb"]
    x1scr = ctx["x1scr"]; x1Tscr = ctx["x1Tscr"]
    x1g = ctx["x1g"]; yg = ctx["yg"]
    wg32 = ctx["wg32"]; wu32 = ctx["wu32"]; wd32 = ctx["wd32"]
    y = ctx["y"]
    BARF = ctx["BARF"]

    scope_f = ExitStack()
    ctx["scope_f"] = scope_f

    def sbf(name, shape, dt=F32):
        return scope_f.enter_context(nc.sbuf_tensor(name, shape, dt))

    x1Tc = [sbf("x1Tc0", [128, NKC * CH], BF16)] * 2
    wgr = [sbf(f"wgr{i}", [128, NKC * 128], BF16) for i in range(4)]
    wur = [sbf(f"wur{i}", [128, NKC * 128], BF16) for i in range(4)]
    wdb = sbf("wdb", [128, NFC * D], BF16)
    gxt = [sbf(f"gxt{i}", [128, 4 * D], BF16) for i in range(3)]
    xgT = sbf("xgT", [128, NKC * 512], BF16)
    hTf = sbf("hTf", [128, NFC * 512], BF16)
    tref = [sbf(f"tref{i}", [128, 512]) for i in range(2)]
    ygb = [sbf(f"ygb{i}", [128, D], BF16) for i in range(2)]
    gbuf1 = [sbf(f"gbuf1{i}", [128, D], BF16) for i in range(3)]
    gbuf2 = [sbf(f"gbuf2{i}", [128, D], BF16) for i in range(3)]
    x1rd = [sbf(f"x1rd{i}", [128, D]) for i in range(2)]
    xbf_t = [sbf(f"xbf_t{i}", [128, D], BF16) for i in range(2)]
    z2b = [sbf(f"z2b{i}", [128, D]) for i in range(3)]
    outb = [sbf(f"outb{i}", [128, D]) for i in range(2)]
    prt = sbf("prt", [8, CH])
    rtl = [sbf(f"rtl{i}", [128, 8]) for i in range(2)]
    rtm = [sbf(f"rtm{i}", [128, 8]) for i in range(2)]
    rtm2 = [sbf(f"rtm2{i}", [128, 8]) for i in range(2)]
    rtm3 = [sbf(f"rtm3{i}", [128, 8]) for i in range(2)]
    rtmA = [sbf(f"rtmA{i}", [128, 8]) for i in range(2)]
    m12b = [sbf(f"m12b{i}", [128, 8], BF16) for i in range(2)]
    prefsb = [sbf(f"prefsb{i}", [128, 8]) for i in range(2)]
    s0sb = [sbf(f"s0sb{i}", [128, 8]) for i in range(2)]
    tmp8 = [sbf(f"tmp8{i}", [128, 8]) for i in range(2)]
    ciobc = sbf("ciobc", [128, 8])
    baseR = [sbf(f"baseR{i}", [1, 8]) for i in range(2)]
    idx1i = sbf("idx1i", [128, NTB], mybir.dt.int32)
    idx2i = sbf("idx2i", [128, NTB], mybir.dt.int32)
    idxi = [sbf(f"idxi{i}", [128, 1], mybir.dt.int32) for i in range(4)]
    ce1s = sbf("ce1s", [128, NTB])
    ce2s = sbf("ce2s", [128, NTB])
    psf = scope_f.enter_context(nc.psum_tensor("psf", [128, 1024], BF16))

    barrier = list(BARF)
    PREF, TOT, BC = 2560, 2576, 2592

    last_rt = {}; last_xbf = {}; last_x1rd = {}; last_z2b = {}; last_outb = {}
    last_gb = {}; last_wslot = {}; last_gxt = {}; last_ygb = {}; last_sci = {}
    free_rtbank = ("fold", 0)
    free_psf = ("trbc", s.val("trbc"))
    free_b5 = ("fold", 0)
    gf = [0]

    s.on("tensor", lambda t: t.matmul(ps[:, BC:BC + 8], onesrow_sb[:], cio_sb[:],
                                      start=True, stop=True, skip_group_check=True),
         waits=barrier, force=True, inc="dnmm")
    s.on("vector", lambda v: v.tensor_copy(ciobc[:], ps[:, BC:BC + 8]),
         waits=barrier + [("dnmm", s.val("dnmm"))], force=True, inc="ln2")
    s.on("vector", lambda v: v.memset(baseR[0][:], 0.0), inc="s2ch")
    free_b5 = ("ln2", s.val("ln2"))

    last_x1Tc = {}
    for ch in range(NCH):
        fxt_v = s.dma("sync", lambda e, ch=ch: e.dma_start(
            out=x1Tc[ch % 2][:].rearrange("p (c f) -> p c f", c=NKC),
            in_=x1Tscr[:, :, ch * CH:(ch + 1) * CH].rearrange("c p f -> p c f")),
            "fxt", ch % 2, waits=barrier + [last_x1Tc.get(0, ("fold", 0))])
        for kc in range(NKC):
            for h in range(2):
                s.on("tensor", lambda tt, ch=ch, kc=kc, h=h: tt.matmul(
                    ps[0:8, h * 512:(h + 1) * 512],
                    wr_bf[:, kc * NE:(kc + 1) * NE],
                    x1Tc[ch % 2][:, kc * CH + h * 512:kc * CH + (h + 1) * 512],
                    start=(kc == 0), stop=(kc == NKC - 1), skip_group_check=True),
                    waits=(barrier + [fxt_v, ("atn", s.val("atn")),
                                      ("ln2", s.val("ln2")),
                                      ("bcev", s.val("bcev"))]
                           + s.dma_totals("gwr"))
                    if (kc == 0 and h == 0) else (), force=True,
                    inc="dnmm" if (kc == NKC - 1 and h == 1) else None)
        rtv = s.val("dnmm")
        last_x1Tc[0] = ("dnmm", rtv)
        s.on("vector", lambda v: v.tensor_copy(prt[:], ps[0:8, 0:CH]),
             waits=[("dnmm", rtv)], force=True, inc="bcev")
        rte_v = s.val("bcev")
        for t8 in range(NT8):
            tt_g = ch * NT8 + t8
            sl = t8 % 4
            pr = t8 % 2
            s.on("tensor", lambda tx, t8=t8, sl=sl: tx.transpose(
                ps[0:128, 2048 + sl * 128:2048 + sl * 128 + 8],
                prt[:, t8 * 128:(t8 + 1) * 128], identF_sb[0:8, 0:8]),
                waits=[("bcev", rte_v), free_rtbank], force=True, inc="trb")
            trv = s.val("trb")
            s.on("vector", lambda v, sl=sl, pr=pr: v.tensor_copy(
                rtl[pr][:], ps[0:128, 2048 + sl * 128:2048 + sl * 128 + 8]),
                waits=[("trb", trv), last_rt.get(pr, ("fold", 0))], force=True,
                inc="ln2")
            free_rtbank = ("ln2", s.val("ln2"))

            def sm_op(fn, waits=(), inc="ln2", eng="vector"):
                s.on(eng, fn, waits=[("ln2", s.val("ln2"))] + list(waits),
                     force=True, inc=inc)

            c0 = 10 + pr
            sm_op(lambda v, pr=pr, c0=c0: v.reduce_max(
                smc[:, c0:c0 + 1], rtl[pr][:], axis=AX.X))
            sm_op(lambda v, pr=pr, c0=c0: v.tensor_scalar(
                rtm[pr][:], rtl[pr][:], smc[:, c0:c0 + 1], None, ALU.subtract))
            sm_op(lambda v, pr=pr: v.tensor_scalar(
                rtmA[pr][:], rtm[pr][:], 0.0, None, ALU.is_equal))
            sm_op(lambda v, pr=pr: v.scalar_tensor_tensor(
                rtm3[pr][:], rtmA[pr][:], -30000.0, rtl[pr][:],
                ALU.mult, ALU.add))
            sm_op(lambda v, pr=pr, c0=c0: v.reduce_max(
                smc[:, c0 + 4:c0 + 5], rtm3[pr][:], axis=AX.X))
            sm_op(lambda v, pr=pr, c0=c0: v.tensor_scalar(
                rtm2[pr][:], rtl[pr][:], smc[:, c0 + 4:c0 + 5], None, ALU.is_ge))
            sm_op(lambda v, pr=pr: v.tensor_copy(m12b[pr][:], rtm2[pr][:]))
            sm_op(lambda v, pr=pr, c0=c0: v.tensor_sub(
                smc[:, c0:c0 + 1], smc[:, c0:c0 + 1], smc[:, c0 + 4:c0 + 5]))
            sm_op(lambda sc_, pr=pr, c0=c0, tt_g=tt_g: sc_.activation(
                ce1s[:, tt_g:tt_g + 1], smc[:, c0:c0 + 1], AF.Sigmoid),
                eng="scalar")
            sm_op(lambda v, pr=pr, tt_g=tt_g: v.tensor_scalar(
                ce2s[:, tt_g:tt_g + 1], ce1s[:, tt_g:tt_g + 1], -1.0, 1.0,
                ALU.mult, ALU.add))
            M12V = s.val("ln2")
            s.on("tensor", lambda tx, pr=pr: tx.matmul(
                ps[:, PREF:PREF + 8], ut_sb[:], m12b[pr][:],
                start=True, stop=True, skip_group_check=True),
                waits=[("ln2", M12V), free_b5], force=True)
            s.on("tensor", lambda tx, pr=pr: tx.matmul(
                ps[0:1, TOT:TOT + 8], onescol_sb[:], m12b[pr][:],
                start=True, stop=True, skip_group_check=True), force=True)
            cur = tt_g % 2
            s.on("tensor", lambda tx, cur=cur: tx.matmul(
                ps[:, BC:BC + 8], onesrow_sb[:], baseR[cur][:],
                start=True, stop=True, skip_group_check=True),
                waits=[("s2ch", s.val("s2ch"))], force=True, inc="dnmm")
            pfv = s.val("dnmm")
            sm_op(lambda v, pr=pr: v.tensor_copy(prefsb[pr][:], ps[:, PREF:PREF + 8]),
                  waits=[("dnmm", pfv)])
            sm_op(lambda v, cur=cur: v.tensor_add(
                baseR[1 - cur][:], baseR[cur][:], ps[0:1, TOT:TOT + 8]),
                inc="s2ch")
            sm_op(lambda v, pr=pr: v.tensor_add(s0sb[pr][:], prefsb[pr][:],
                                                ps[:, BC:BC + 8]))
            FB5 = s.val("ln2")
            free_b5 = ("ln2", FB5)
            sm_op(lambda v, pr=pr: v.tensor_scalar(
                s0sb[pr][:], s0sb[pr][:], float(CAP - 1), None, ALU.min))
            sm_op(lambda v, pr=pr: v.tensor_add(s0sb[pr][:], s0sb[pr][:], ciobc[:]))
            sm_op(lambda v, pr=pr: v.tensor_mul(tmp8[pr][:], s0sb[pr][:], rtmA[pr][:]))
            sm_op(lambda v, pr=pr, tt_g=tt_g: v.reduce_sum(
                smc[:, 14 + pr:15 + pr], tmp8[pr][:], axis=AX.X))
            sm_op(lambda v, pr=pr: v.tensor_scalar(
                smc[:, 14 + pr:15 + pr], smc[:, 14 + pr:15 + pr],
                float(NE * CAP - 1), None, ALU.min))
            sm_op(lambda v, pr=pr, tt_g=tt_g: v.tensor_copy(
                idx1i[:, tt_g:tt_g + 1], smc[:, 14 + pr:15 + pr]))
            sm_op(lambda v, pr=pr: v.tensor_copy(
                idxi[pr][:], smc[:, 14 + pr:15 + pr]),
                waits=[last_sci.get(pr, ("fold", 0))])
            sm_op(lambda v, pr=pr: v.tensor_sub(rtm2[pr][:], rtm2[pr][:], rtmA[pr][:]))
            sm_op(lambda v, pr=pr: v.tensor_mul(tmp8[pr][:], s0sb[pr][:], rtm2[pr][:]))
            sm_op(lambda v, pr=pr, tt_g=tt_g: v.reduce_sum(
                smc[:, 14 + pr:15 + pr], tmp8[pr][:], axis=AX.X))
            sm_op(lambda v, pr=pr: v.tensor_scalar(
                smc[:, 14 + pr:15 + pr], smc[:, 14 + pr:15 + pr],
                float(NE * CAP - 1), None, ALU.min))
            sm_op(lambda v, pr=pr, tt_g=tt_g: v.tensor_copy(
                idx2i[:, tt_g:tt_g + 1], smc[:, 14 + pr:15 + pr]))
            sm_op(lambda v, pr=pr: v.tensor_copy(
                idxi[2 + pr][:], smc[:, 14 + pr:15 + pr]),
                waits=[last_sci.get(2 + pr, ("fold", 0))], inc="xrs")
            last_rt[pr] = ("xrs", s.val("xrs"))
            RTDONE = s.val("xrs")
            x1v = s.dma("sync", lambda e, tt_g=tt_g, t8=t8: e.dma_start(
                out=x1rd[t8 % 2][:], in_=x1scr[tt_g]),
                "x1ld", t8 % 2, waits=[last_x1rd.get(t8 % 2, ("fold", 0))])
            s.on("vector", lambda v, t8=t8: v.tensor_copy(
                xbf_t[t8 % 2][:], x1rd[t8 % 2][:]),
                waits=[x1v, last_xbf.get(t8 % 2, ("fold", 0))], force=True,
                inc="zf")
            xbv = s.val("zf")
            last_x1rd[t8 % 2] = ("zf", xbv)
            sc1 = s.dma("gpsimd", lambda g_, t8=t8: g_.indirect_dma_start(
                x1g, IndirectOffsetOnAxis(idxi[t8 % 2][:], 0),
                xbf_t[t8 % 2][:], None),
                "axt", t8 % 2, waits=[("zf", xbv), ("xrs", RTDONE)])
            sc2 = s.dma("gpsimd", lambda g_, t8=t8: g_.indirect_dma_start(
                x1g, IndirectOffsetOnAxis(idxi[2 + t8 % 2][:], 0),
                xbf_t[t8 % 2][:], None),
                "bxt", t8 % 2)
            last_xbf[t8 % 2] = sc2
            last_sci[t8 % 2] = sc1
            last_sci[2 + t8 % 2] = sc2

    SCT = s.dma_totals("axt") + s.dma_totals("bxt")

    for e8 in range(NE):
        fwd_v = s.dma("gpsimd", lambda g_, e8=e8: g_.dma_start(
            out=wdb[:].rearrange("p (c d) -> p c d", c=NFC),
            in_=wd32[e8].rearrange("c p d -> p c d")),
            "fwd", e8 % 2, waits=barrier + [("womm", s.val("womm"))])
        for sc_i in range(NSC):
            row0 = e8 * CAP + sc_i * 512
            gslot = (e8 * NSC + sc_i) % 3
            WPREV = s.val("womm")
            gx_v = s.dma("sync", lambda e, row0=row0, gslot=gslot: e.dma_start(
                out=gxt[gslot][:].rearrange("p (t d) -> p t d", t=4),
                in_=x1g[row0:row0 + 512].rearrange("(t p) d -> p t d", p=128)),
                "axa", gslot, waits=SCT + [last_gxt.get(gslot, ("fold", 0))])
            for t4 in range(4):
                for j in range(NKC):
                    s.on("tensor", lambda tx, t4=t4, j=j, gslot=gslot: tx.transpose(
                        psf[:, j * 128:(j + 1) * 128],
                        gxt[gslot][:, t4 * D + j * 128:t4 * D + (j + 1) * 128],
                        identB_sb[:]),
                        waits=[gx_v, free_psf] if j == 0 else (), force=True,
                        inc="trb" if j == NKC - 1 else None)
                trv = s.val("trb")
                for j in range(NKC):
                    s.on("vector", lambda v, t4=t4, j=j: v.tensor_copy(
                        xgT[:, j * 512 + t4 * 128:j * 512 + (t4 + 1) * 128],
                        psf[:, j * 128:(j + 1) * 128]),
                        waits=([("trb", trv), ("qmm", s.val("qmm"))])
                        if j == 0 else (),
                        force=True, inc="trbc" if j == NKC - 1 else None)
                free_psf = ("trbc", s.val("trbc"))
            TRX = s.val("trbc")
            for f in range(NFC):
                slot = gf[0] % 4
                s.dma("gpsimd", lambda g_, e8=e8, f=f, slot=slot: g_.dma_start(
                    out=wgr[slot][:], in_=wg32[e8, f]),
                    "fw", slot, waits=[last_wslot.get(slot, ("fold", 0))])
                fw_v = s.dma("gpsimd", lambda g_, e8=e8, f=f, slot=slot: g_.dma_start(
                    out=wur[slot][:], in_=wu32[e8, f]),
                    "fw", slot)
                gf[0] += 1
                gcol = 0 if f % 2 == 0 else 1024
                ucol = 512 if f % 2 == 0 else 1536
                tw = [fw_v, ("trbc", TRX), ("phv", max(0, s.val("phv") - 1)),
                      ("phr", max(0, s.val("phr") - 1))]
                if f == 0:
                    tw.append(("atn", s.val("atn")))
                for kc in range(NKC):
                    s.on("tensor", lambda tx, kc=kc, slot=slot, gcol=gcol: tx.matmul(
                        ps[:, gcol:gcol + 512],
                        wgr[slot][:, kc * 128:(kc + 1) * 128],
                        xgT[:, kc * 512:(kc + 1) * 512],
                        start=(kc == 0), stop=(kc == NKC - 1),
                        skip_group_check=True),
                        waits=tw if kc == 0 else (), force=True)
                for kc in range(NKC):
                    s.on("tensor", lambda tx, kc=kc, slot=slot, ucol=ucol: tx.matmul(
                        ps[:, ucol:ucol + 512],
                        wur[slot][:, kc * 128:(kc + 1) * 128],
                        xgT[:, kc * 512:(kc + 1) * 512],
                        start=(kc == 0), stop=(kc == NKC - 1),
                        skip_group_check=True),
                        force=True, inc="qmm" if kc == NKC - 1 else None)
                guv = s.val("qmm")
                last_wslot[slot] = ("qmm", guv)
                s.on("scalar", lambda sc_, f=f, gcol=gcol: sc_.activation(
                    tref[f % 2][:], ps[:, gcol:gcol + 512], AF.Relu),
                    waits=[("qmm", guv), ("phv", max(0, s.val("phv") - 1))],
                    force=True, inc="phr")
                prv = s.val("phr")
                mw = [("phr", prv)]
                if f == 0:
                    mw.append(("womm", WPREV))
                s.on("vector", lambda v, f=f, ucol=ucol: v.tensor_mul(
                    hTf[:, f * 512:(f + 1) * 512], tref[f % 2][:],
                    ps[:, ucol:ucol + 512]),
                    waits=mw, force=True, inc="phv")
            HDV = s.val("phv")
            last_gxt[gslot] = ("qmm", s.val("qmm"))
            for t4 in range(4):
                dcol = 2048 if t4 % 2 == 0 else 0
                dw = [fwd_v, ("phv", HDV), ("atn", max(0, s.val("atn") - 1))]
                for kc in range(NFC):
                    for h in range(2):
                        s.on("tensor", lambda tx, kc=kc, h=h, t4=t4, dcol=dcol: tx.matmul(
                            ps[:, dcol + h * 512:dcol + (h + 1) * 512],
                            hTf[:, kc * 512 + t4 * 128:kc * 512 + (t4 + 1) * 128],
                            wdb[:, kc * D + h * 512:kc * D + (h + 1) * 512],
                            start=(kc == 0), stop=(kc == NFC - 1),
                            skip_group_check=True),
                            waits=dw if (kc == 0 and h == 0) else (), force=True,
                            inc="womm" if (kc == NFC - 1 and h == 1) else None)
                ddv = s.val("womm")
                s.on("vector", lambda v, t4=t4, dcol=dcol: v.tensor_copy(
                    ygb[t4 % 2][:], ps[:, dcol:dcol + D]),
                    waits=[("womm", ddv), last_ygb.get(t4 % 2, ("fold", 0))],
                    force=True, inc="atn")
                av = s.val("atn")
                last_ygb[t4 % 2] = s.dma("sync", lambda e, row0=row0, t4=t4: e.dma_start(
                    out=yg[row0 + t4 * 128:row0 + (t4 + 1) * 128],
                    in_=ygb[t4 % 2][:]),
                    "bxa", t4 % 2, waits=[("atn", av)])

    YGST = s.dma_totals("bxa")

    RTALL = ("xrs", s.val("xrs"))
    last_zfl = {}; last_stat = {}
    for tt_g in range(NTB):
        p3 = tt_g % 3
        c0 = 12 + 4 * p3
        g1v = s.dma("gpsimd", lambda g_, p3=p3, tt_g=tt_g: g_.indirect_dma_start(
            gbuf1[p3][:], None, yg, IndirectOffsetOnAxis(idx1i[:, tt_g:tt_g + 1], 0)),
            "w2ld", p3, waits=YGST + [RTALL, last_gb.get((0, p3), ("fold", 0))])
        g2v = s.dma("gpsimd", lambda g_, p3=p3, tt_g=tt_g: g_.indirect_dma_start(
            gbuf2[p3][:], None, yg, IndirectOffsetOnAxis(idx2i[:, tt_g:tt_g + 1], 0)),
            "bgld", p3, waits=[last_gb.get((1, p3), ("fold", 0))])
        x1v = s.dma("sync", lambda e, tt_g=tt_g, pr=tt_g % 2: e.dma_start(
            out=x1rd[pr][:], in_=x1scr[tt_g]),
            "x1ld", tt_g % 2, waits=[last_x1rd.get(tt_g % 2, ("fold", 0))])
        s.on("vector", lambda v, p3=p3, tt_g=tt_g, pr=tt_g % 2: v.scalar_tensor_tensor(
            z2b[p3][:], gbuf1[p3][:], ce1s[:, tt_g:tt_g + 1], x1rd[pr][:],
            ALU.mult, ALU.add),
            waits=[g1v, x1v, last_z2b.get(p3, ("fold", 0))], force=True, inc="zf2")
        s.on("vector", lambda v, p3=p3, tt_g=tt_g, c0=c0, pr=tt_g % 2:
             v.scalar_tensor_tensor(
                 x1rd[pr][:], gbuf2[p3][:], ce2s[:, tt_g:tt_g + 1], z2b[p3][:],
                 ALU.mult, ALU.add, accum_out=smc[:, c0:c0 + 1]),
             waits=[g2v, ("zf2", s.val("zf2")),
                    last_stat.get(p3, ("fold", 0))],
             force=True, inc="zf2")
        zv = s.val("zf2")
        last_gb[(0, p3)] = ("zf2", zv)
        s.on("scalar", lambda sc_, p3=p3, c0=c0, pr=tt_g % 2: sc_.activation(
            z2b[p3][:], x1rd[pr][:], AF.Square,
            accum_out=smc[:, c0 + 1:c0 + 2]),
            waits=[("zf2", zv), last_stat.get(p3, ("fold", 0))], inc="s2ssq")
        sqv = s.val("s2ssq")
        s.on("vector", lambda v, c0=c0: v.tensor_scalar(
            smc[:, c0 + 2:c0 + 3], smc[:, c0:c0 + 1], 1.0 / D, None,
            ALU.mult),
            waits=[("zf2", zv)], force=True, inc="s2ch")
        s.on("vector", lambda v, c0=c0: v.tensor_mul(
            smc[:, c0 + 3:c0 + 4], smc[:, c0 + 2:c0 + 3],
            smc[:, c0 + 2:c0 + 3]),
            waits=[("s2ch", s.val("s2ch"))], force=True, inc="s2ch")
        s.on("vector", lambda v, c0=c0: v.scalar_tensor_tensor(
            smc[:, c0:c0 + 1], smc[:, c0 + 1:c0 + 2], 1.0 / D,
            smc[:, c0 + 3:c0 + 4], ALU.mult, ALU.subtract),
            waits=[("s2ssq", sqv), ("s2ch", s.val("s2ch"))], force=True,
            inc="s2ch")
        scv = s.val("s2ch")
        s.on("scalar", lambda sc_, c0=c0: sc_.activation(
            smc[:, c0 + 3:c0 + 4], smc[:, c0:c0 + 1], AF.Ln,
            bias=eps5[:]),
            waits=[("s2ch", scv)], inc="s2ch")
        s.on("scalar", lambda sc_, c0=c0: sc_.activation(
            smc[:, c0 + 1:c0 + 2], smc[:, c0 + 3:c0 + 4], AF.Exp, scale=-0.5),
            waits=[("s2ch", s.val("s2ch"))], force=True, inc="s2ch")
        scv = s.val("s2ch")
        s.on("vector", lambda v, c0=c0, po=tt_g % 2: v.tensor_scalar(
            outb[po][:], x1rd[po][:], smc[:, c0 + 2:c0 + 3],
            smc[:, c0 + 1:c0 + 2], ALU.subtract, ALU.mult),
            waits=[("s2ch", scv), last_outb.get(tt_g % 2, ("fold", 0))],
            force=True, inc="zf2")
        ov = s.val("zf2")
        last_z2b[p3] = ("zf2", ov)
        last_x1rd[tt_g % 2] = ("zf2", ov)
        last_gb[(1, p3)] = ("zf2", ov)
        last_stat[p3] = ("zf2", ov)
        last_outb[tt_g % 2] = s.dma("sync", lambda e, tt_g=tt_g, po=tt_g % 2:
                                    e.dma_start(out=y[tt_g], in_=outb[po][:]),
                                    "x1st", tt_g % 2, waits=[("zf2", ov)])

    s.on("sync", lambda e: e.nop(), waits=s.all_dma_totals(), force=True)


def _host_weights(inputs):
    f32 = np.float32
    iw = {}
    iw["wq32"] = np.ascontiguousarray(
        inputs["L0_attn_Wq"].reshape(D, D).reshape(NKC, 128, D), dtype=f32)
    iw["wk32"] = np.ascontiguousarray(
        inputs["L0_attn_Wk"].reshape(D, D).reshape(NKC, 128, D), dtype=f32)
    iw["wv32"] = np.ascontiguousarray(
        inputs["L0_attn_Wv"].reshape(D, D).reshape(NKC, 128, D), dtype=f32)
    iw["wo32"] = np.ascontiguousarray(
        inputs["L0_attn_Wo"].reshape(D, D).reshape(NKC, 128, D), dtype=f32)
    iw["w132"] = np.ascontiguousarray(
        inputs["L0_attn_W1"].reshape(D, MLP).reshape(NKC, 128, MLP), dtype=f32)
    iw["w232"] = np.ascontiguousarray(
        inputs["L0_attn_W2"].reshape(MLP, D).reshape(NF1, 128, D), dtype=f32)
    iw["wr32"] = np.ascontiguousarray(
        inputs["L0_ffn_Wr"].reshape(D, NE).reshape(NKC, 128, NE), dtype=f32)
    wg = np.asarray(inputs["L0_ffn_Wg"], dtype=f32)
    wu = np.asarray(inputs["L0_ffn_Wu"], dtype=f32)
    wd = np.asarray(inputs["L0_ffn_Wd"], dtype=f32)
    iw["wg32"] = np.ascontiguousarray(
        wg.reshape(NE, NKC, 128, NFC, 128).transpose(0, 3, 2, 1, 4).reshape(
            NE, NFC, 128, NKC * 128))
    iw["wu32"] = np.ascontiguousarray(
        wu.reshape(NE, NKC, 128, NFC, 128).transpose(0, 3, 2, 1, 4).reshape(
            NE, NFC, 128, NKC * 128))
    iw["wd32"] = np.ascontiguousarray(wd.reshape(NE, NFC, 128, D))
    iw["ln1s"] = np.ascontiguousarray(
        np.asarray(inputs["L0_attn_ln1_scale"], f32).reshape(NKC, 128).T)
    iw["ln1b"] = np.ascontiguousarray(
        np.asarray(inputs["L0_attn_ln1_bias"], f32).reshape(NKC, 128).T)
    iw["ln2s"] = np.ascontiguousarray(
        np.asarray(inputs["L0_attn_ln2_scale"], f32).reshape(NKC, 128).T)
    iw["ln2b"] = np.ascontiguousarray(
        np.asarray(inputs["L0_attn_ln2_bias"], f32).reshape(NKC, 128).T)
    iw["bqr"] = np.asarray(inputs["L0_attn_bq"], f32).reshape(1, D)
    iw["bkr"] = np.asarray(inputs["L0_attn_bk"], f32).reshape(1, D)
    iw["bvr"] = np.asarray(inputs["L0_attn_bv"], f32).reshape(1, D)
    iw["bor32"] = np.asarray(inputs["L0_attn_bo"], f32).reshape(1, D)
    iw["b1r32"] = np.asarray(inputs["L0_attn_b1"], f32).reshape(1, MLP)
    iw["b2r32"] = np.asarray(inputs["L0_attn_b2"], f32).reshape(1, D)
    iw["identF"] = np.eye(128, dtype=f32)
    iw["identB"] = np.eye(128, dtype=f32).astype(bfd)
    iw["onesrow_f"] = np.ones((1, 128), f32)
    iw["onescol_b"] = np.ones((128, 1), f32).astype(bfd)
    zk = np.zeros((2, 128), f32); zk[1, :] = KEPS
    iw["zk_row"] = zk.astype(bfd)
    selm = np.zeros((NE, 16, 128), f32)
    for c in range(NE):
        selm[c, 2 * c, 0:64] = 1.0
        selm[c, 2 * c + 1, 64:128] = 1.0
    iw["sel"] = selm
    iw["ut128"] = np.triu(np.ones((128, 128), f32), 1).astype(bfd)
    iw["cio8"] = (CAP * np.arange(8, dtype=f32)).reshape(1, 8)
    return iw


def _core_x(inputs, c):
    b, j = c // 2, c % 2
    xb = np.asarray(inputs["L0_attn_inputs"][b], dtype=np.float32)
    own = xb[j * SOWN:(j + 1) * SOWN]
    oth = xb[(1 - j) * SOWN:(2 - j) * SOWN]
    xcat = np.concatenate([own, oth], 0)
    x_all = np.ascontiguousarray(xcat.reshape(NTA, 128, D))
    xT_all = np.ascontiguousarray(xcat.T).reshape(NKC, 128, SALL)
    return {"x_all": x_all, "xT_all": xT_all}


def kernel(**inputs):
    iw = _host_weights(inputs)
    nc, _ = build()
    in_maps = []
    for c in range(8):
        m = dict(iw)
        m.update(_core_x(inputs, c))
        in_maps.append(m)
    res = run_bass_kernel_spmd(nc, in_maps, core_ids=list(range(8)))
    out = np.empty((4, SALL, D), dtype=np.float32)
    for c in range(8):
        b, j = c // 2, c % 2
        out[b, j * SOWN:(j + 1) * SOWN] = res.results[c]["y"].reshape(SOWN, D)
    return out
